# version 06
# baseline (speedup 1.0000x reference)
import numpy as np
import concourse.bass as bass
import concourse.mybir as mybir
from concourse.bass_utils import run_bass_kernel_spmd

F32 = mybir.dt.float32
AX = mybir.AxisListType
ALU = mybir.AluOpType
ACT = mybir.ActivationFunctionType

B, L, DIM = 4, 4096, 1024
CH = 64
NCH = L // CH
G = 256
NG = L // G
CPG = G // CH
NEXP = 8
TOWN = L // 2
NT = TOWN // 128
EPS_LN = 1e-5
BASE_LR, BASE_WD = 1e-3, 0.9
GEPS, WTHR = 1e-4, 0.1
SCAN_STAGE = 6


class _Op:
    __slots__ = ("eng", "emit", "deps", "inc", "post", "sem", "hz")

    def __init__(self, eng, emit, deps, inc, sem, hz):
        self.eng, self.emit, self.deps, self.inc, self.sem = eng, emit, deps, inc, sem
        self.hz = hz
        self.post = None


class Buf:
    __slots__ = ("writer", "readers")

    def __init__(self):
        self.writer = None
        self.readers = []


class Prog:
    ENGS = ("sync", "tensor", "vector", "scalar", "gpsimd")

    def __init__(self):
        self.ops = []
        self.barrier = None
        self.barrier_applied = set()

    def op(self, eng, emit, reads=(), writes=(), deps=(), dma=False, sem=None, hz=False):
        dd = list(deps)
        if self.barrier is not None and eng not in self.barrier_applied:
            dd.extend(self.barrier)
            self.barrier_applied.add(eng)
        for b in reads:
            if b.writer is not None:
                dd.append(b.writer)
        for b in writes:
            dd.extend(b.readers)
            if b.writer is not None:
                dd.append(b.writer)
        assert (sem is not None) == dma, "DMA ops need an explicit slot sem"
        o = _Op(eng, emit, dd, 16 if dma else 1, sem if dma else eng, hz)
        self.ops.append(o)
        for b in reads:
            b.readers.append(o)
        for b in writes:
            b.writer = o
            b.readers = []
        return o

    def plan(self):
        cnt = {}
        for o in self.ops:
            cnt[o.sem] = cnt.get(o.sem, 0) + o.inc
            o.post = cnt[o.sem]
        return cnt

    def emit(self, eng_name, eng, sems):
        watermark = {}
        for o in self.ops:
            if o.eng != eng_name:
                continue
            best = {}
            for d in o.deps:
                if (d.eng == eng_name == "tensor" and o.eng == "tensor"
                        and d.sem == "tensor" and not d.hz):
                    continue
                if best.get(d.sem, -1) < d.post:
                    best[d.sem] = d.post
            for s2 in sorted(best):
                v = best[s2]
                if watermark.get(s2, 0) >= v:
                    continue
                eng.wait_ge(sems[s2], v)
                watermark[s2] = v
            o.emit(eng).then_inc(sems[o.sem], o.inc)


def _head_perm():
    idx = np.empty(DIM, np.int64)
    for oc in range(8):
        for c2 in range(2):
            h = oc + 8 * c2
            idx[oc * 128 + c2 * 64: oc * 128 + c2 * 64 + 64] = np.arange(h * 64, h * 64 + 64)
    return idx


def host_prep(inp):
    idx = _head_perm()
    f32c = lambda a: np.ascontiguousarray(np.asarray(a), dtype=np.float32)
    d = {}
    d["wqT"] = f32c(np.asarray(inp["L0_attn_Wq"]).T[:, idx])
    d["wkT"] = f32c(np.asarray(inp["L0_attn_Wk"]).T[:, idx])
    d["wvT"] = f32c(np.asarray(inp["L0_attn_Wv"]).T[:, idx])
    d["wlbT"] = f32c(np.concatenate([np.asarray(inp["L0_attn_Wlr"]).T,
                                     np.asarray(inp["L0_attn_Wbeta"]).T], axis=1))
    d["woT"] = f32c(np.asarray(inp["L0_attn_Wo"]).T[idx, :])
    wi = np.asarray(inp["L0_attn_W_in_init"])[0]
    wo = np.asarray(inp["L0_attn_W_out_init"])[0]

    def compact(arr_hab):
        a = np.asarray(arr_hab).reshape(2, 8, 64, 64)
        return f32c(a.transpose(0, 2, 1, 3).reshape(128, 8, 64))

    def blockdiag(cm):
        bd = np.zeros((128, 8, 128), np.float32)
        bd[0:64, :, 0:64] = cm[0:64]
        bd[64:128, :, 64:128] = cm[64:128]
        return bd

    d["winT0"] = blockdiag(compact(wi.transpose(1, 2, 0)))
    d["wout0"] = blockdiag(compact(wo.transpose(1, 0, 2)))
    d["woutT0"] = blockdiag(compact(wo.transpose(1, 2, 0)))
    d["gw"] = f32c(inp["L0_ffn_gate_W"])
    d["gb"] = f32c(np.asarray(inp["L0_ffn_gate_b"]).reshape(1, 8))
    d["ew"] = f32c(inp["L0_ffn_expert_W"])
    d["eb"] = f32c(inp["L0_ffn_expert_b"])
    d["identw"] = np.eye(128, dtype=np.float32)
    oc2 = np.zeros((128, 2), np.float32)
    oc2[:64, 0] = 1.0
    oc2[64:, 1] = 1.0
    d["ones2colw"] = oc2
    hs = np.zeros((2, 128), np.float32)
    hs[0, :64] = 1.0
    hs[1, 64:] = 1.0
    d["halfselw"] = hs
    d["onesroww"] = np.ones((1, 128), np.float32)
    d["pow2w"] = np.ascontiguousarray(
        np.broadcast_to(2.0 ** -np.arange(8, dtype=np.float32), (128, 8)))
    c64 = np.empty((64, 1), np.float32)
    c64[:32] = BASE_LR / 64.0
    c64[32:] = BASE_WD / 64.0
    d["c64w"] = c64
    d["epsw"] = np.full((128, 1), EPS_LN, np.float32)
    return d


def build_core(n_groups=NG, n_chunks=NCH, do_post=True, taps=()):
    nc = bass.Bass("TRN2", target_bir_lowering=False, detect_race_conditions=False)
    p = Prog()
    bufs = {}

    def bf(name):
        if name not in bufs:
            bufs[name] = Buf()
        return bufs[name]

    def dram_in(name, shape):
        return nc.dram_tensor(name, list(shape), F32, kind="ExternalInput").ap()

    xb = dram_in("x", (L, DIM))
    xown_d = dram_in("x_own", (TOWN, DIM))
    msel_d = dram_in("mselw", (128, 2))
    wqT = dram_in("wqT", (DIM, DIM))
    wkT = dram_in("wkT", (DIM, DIM))
    wvT = dram_in("wvT", (DIM, DIM))
    wlbT_d = dram_in("wlbT", (DIM, 64))
    woT_d = dram_in("woT", (DIM, DIM))
    winT0 = dram_in("winT0", (128, 8, 128))
    wout0 = dram_in("wout0", (128, 8, 128))
    woutT0 = dram_in("woutT0", (128, 8, 128))
    gw_d = dram_in("gw", (DIM, 8))
    gb_d = dram_in("gb", (1, 8))
    ew_d = dram_in("ew", (NEXP, DIM, DIM))
    eb_d = dram_in("eb", (NEXP, DIM))
    ident_d = dram_in("identw", (128, 128))
    ones2col_d = dram_in("ones2colw", (128, 2))
    halfsel_d = dram_in("halfselw", (2, 128))
    onesrow_d = dram_in("onesroww", (1, 128))
    pow2_d = dram_in("pow2w", (128, 8))
    c64_d = dram_in("c64w", (64, 1))
    eps_d = dram_in("epsw", (128, 1))

    out_d = nc.dram_tensor("out", [TOWN, DIM], F32, kind="ExternalOutput").ap()
    oscr = nc.dram_tensor("oscr", [128, 8, L], F32, kind="Internal").ap()

    import contextlib
    es = contextlib.ExitStack()
    esA = contextlib.ExitStack()
    sbP = lambda name, shape: es.enter_context(nc.sbuf_tensor(name, list(shape), F32))
    sb = lambda name, shape: esA.enter_context(nc.sbuf_tensor(name, list(shape), F32))
    psum = lambda name, shape: es.enter_context(nc.psum_tensor(name, list(shape), F32))

    ident = sbP("ident", (128, 128))
    onesrow = sbP("onesrow", (1, 128))
    pow2 = sbP("pow2", (128, 8))
    gb_s = sbP("gb_s", (1, 8))
    eb_s = sbP("eb_s", (8, DIM))
    msel = sbP("msel", (128, 2))
    epsv = sbP("epsv", (128, 1))
    nc.const_aps.register(F32, float(EPS_LN), epsv[:, 0:1])

    wq_s = sb("wq_s", (128, 8, DIM))
    wk_s = sb("wk_s", (128, 8, DIM))
    wlb_s = sb("wlb_s", (128, 8, 64))
    wv_s = [sb(f"wv_s{i}", (128, 8, 128)) for i in range(2)]
    ones2col = sb("ones2col", (128, 2))
    halfsel = sb("halfsel", (2, 128))
    c64 = sb("c64", (64, 1))

    winT = sb("winT", (128, 8, 128))
    wout = sb("wout", (128, 8, 128))
    woutT = sb("woutT", (128, 8, 128))

    xrow = [sb(f"xrow{i}", (128, 2, DIM)) for i in range(2)]
    xT = sb("xT", (128, 8, G))
    qT = [sb(f"qT{i}", (128, CPG, 4, 128)) for i in range(2)]
    kT = [sb(f"kT{i}", (128, CPG, 4, 128)) for i in range(2)]
    vT = [sb(f"vT{i}", (128, CPG, 4, 128)) for i in range(2)]
    lrwdT = sb("lrwdT", (64, G))
    lrwd_c = sb("lrwd_c", (64, NCH))
    lrtmp = sb("lrtmp", (4, 64))
    flat2 = sb("flat2", (2, NCH * 32))
    bc_c = [sb(f"bc_c{i}", (128, 32)) for i in range(2)]

    pqT = sb("pqT", (128, 512))
    pT = sb("pT", (128, 512))
    zsub = sb("zsub", (128, 512))
    expt = sb("expt", (128, 512))
    m_s = sb("m_s", (128, 512))
    t1_s = sb("t1_s", (128, 512))
    hh_s = sb("hh_s", (128, 512))
    ph_s = sb("ph_s", (128, 512))
    s2s = sb("s2s", (2, 512))
    mx_s = sb("mx_s", (128, 8))
    sm_s = sb("sm_s", (128, 8))
    rc_s = sb("rc_s", (128, 8))
    ktok = sb("ktok", (128, 4, 128))
    vtok = sb("vtok", (128, 4, 128))
    ptok = sb("ptok", (128, 4, 128))
    htok = sb("htok", (128, 4, 128))
    ostg = sb("ostg", (128, 8, G))

    bank = [psum(f"bank{i}", (128, 512)) for i in range(8)]

    def dma_in(bufname, out_ap, in_ap, eng="sync", extra_reads=(), sem="s_init"):
        return p.op(eng, lambda e, o=out_ap, i=in_ap: e.dma_start(out=o, in_=i),
                    reads=list(extra_reads), writes=[bf(bufname)], dma=True, sem=sem)

    dma_in("wq_s", wq_s[:], wqT.rearrange("(k p) f -> p k f", p=128))
    dma_in("wk_s", wk_s[:], wkT.rearrange("(k p) f -> p k f", p=128))
    dma_in("wlb_s", wlb_s[:], wlbT_d.rearrange("(k p) f -> p k f", p=128))
    for name, s_t, d_t in [("ident", ident, ident_d), ("ones2col", ones2col, ones2col_d),
                           ("halfsel", halfsel, halfsel_d), ("onesrow", onesrow, onesrow_d),
                           ("pow2", pow2, pow2_d), ("c64", c64, c64_d),
                           ("gb_s", gb_s, gb_d), ("eb_s", eb_s, eb_d), ("msel", msel, msel_d),
                           ("epsv", epsv, eps_d)]:
        dma_in(name, s_t[:], d_t[:])
    dma_in("winT", winT[:], winT0[:])
    dma_in("wout", wout[:], wout0[:])
    dma_in("woutT", woutT[:], woutT0[:])

    def mm(out_ap, lhsT_ap, rhs_ap, wbankname, rbufnames, start=True, stop=True, tp=None):
        return p.op("tensor",
                    lambda e, o=out_ap, a=lhsT_ap, b=rhs_ap, s=start, t=stop, q=tp:
                        e.matmul(o, a, b, start=s, stop=t, tile_position=q),
                    reads=[bf(n) for n in rbufnames], writes=[bf(wbankname)])

    def transpose(out_ap, in_ap, wbankname, rbufnames):
        np_ = in_ap.shape[0]
        return p.op("tensor",
                    lambda e, o=out_ap, i=in_ap, n_=np_: e.transpose(o, i, ident[0:n_, 0:n_]),
                    reads=[bf(n) for n in rbufnames] + [bf("ident")],
                    writes=[bf(wbankname)])

    def vop(emit, r=(), w=(), hz=False):
        return p.op("vector", emit, reads=[bf(n) for n in r],
                    writes=[bf(n) for n in w], hz=hz)

    def sop(emit, r=(), w=(), hz=False):
        return p.op("scalar", emit, reads=[bf(n) for n in r],
                    writes=[bf(n) for n in w], hz=hz)

    def _copy(e, o, i_):
        return e.tensor_copy(o, i_) if hasattr(e, "tensor_copy") else e.copy(o, i_)

    def gop(emit, r=(), w=()):
        return p.op("gpsimd", emit, reads=[bf(n) for n in r], writes=[bf(n) for n in w])

    def proj_group_gen(g):
        slot = g % 2
        xr = xrow[slot]
        dma_in(f"xrow{slot}", xr[:], xb[G * g:G * (g + 1), :].rearrange("(t p) f -> p t f", p=128), sem=f"s_xrow{slot}")
        for tt in range(2):
            for dk in range(8):
                i = tt * 8 + dk
                bkn = f"bank{3 + (i // 4) % 2}"
                bk = bank[3 + (i // 4) % 2]
                transpose(bk[0:128, 128 * (i % 4):128 * (i % 4) + 128],
                          xr[:, tt, 128 * dk:128 * dk + 128], bkn, [f"xrow{slot}"])
                cop = sop if (i % 2) else vop
                cop(lambda e, o=xT[:, dk, 128 * tt:128 * tt + 128],
                    i_=bk[0:128, 128 * (i % 4):128 * (i % 4) + 128]: _copy(e, o, i_),
                    r=[bkn], w=["xT"])
                if i % 4 == 3:
                    yield

        def projmm(lhs_ap_fn, wname, dst_ap, dstname, act, bkidx):
            bkn, bk = f"bank{bkidx}", bank[bkidx]
            for dk in range(8):
                mm(bk[0:128, 0:G], lhs_ap_fn(dk), xT[:, dk, :], bkn, [wname, "xT"],
                   start=(dk == 0), stop=(dk == 7))
            src3 = bk[0:128, 0:G].rearrange("p (s n) -> p s n", n=CH)
            if act == "silu":
                sop(lambda e, o=dst_ap, i_=src3: e.activation(o, i_, ACT.Silu),
                    r=[bkn], w=[dstname])
            else:
                vop(lambda e, o=dst_ap, i_=src3: e.tensor_copy(o, i_),
                    r=[bkn], w=[dstname])

        def qkv_dst(tens, oc):
            return tens[:, 0:CPG, oc // 2, 64 * (oc % 2):64 * (oc % 2) + 64]

        for oc in range(8):
            projmm(lambda dk, oc=oc: wk_s[:, dk, 128 * oc:128 * oc + 128], "wk_s",
                   qkv_dst(kT[slot], oc), f"kT{slot}", "silu", 3 + oc % 2)
            yield
        for oc in range(8):
            dma_in(f"wv_s{oc % 2}", wv_s[oc % 2][:],
                   wvT.rearrange("(k p) f -> p k f", p=128)[:, :, 128 * oc:128 * oc + 128],
                   sem=f"s_wv{oc % 2}")
            projmm(lambda dk, oc=oc: wv_s[oc % 2][:, dk, :], f"wv_s{oc % 2}",
                   qkv_dst(vT[slot], oc), f"vT{slot}", "copy", 3 + oc % 2)
            yield
        for oc in range(8):
            projmm(lambda dk, oc=oc: wq_s[:, dk, 128 * oc:128 * oc + 128], "wq_s",
                   qkv_dst(qT[slot], oc), f"qT{slot}", "silu", 3 + oc % 2)
            yield
        bkn, bk = "bank3", bank[3]
        for dk in range(8):
            mm(bk[0:64, 0:G], wlb_s[:, dk, :], xT[:, dk, :], bkn, ["wlb_s", "xT"],
               start=(dk == 0), stop=(dk == 7))
        sop(lambda e, o=lrwdT[:, :], i_=bk[0:64, 0:G]: e.activation(o, i_, ACT.Sigmoid),
            r=[bkn], w=["lrwdT"])
        vop(lambda e, o=lrwd_c[:, CPG * g:CPG * (g + 1)].unsqueeze(2),
            i_=lrwdT[:, :].rearrange("p (c n) -> p c n", n=CH): e.reduce_sum(o, i_, axis=AX.X),
            r=["lrwdT"], w=["lrwd_c"], hz=True)
        vop(lambda e, o=lrwd_c[:, CPG * g:CPG * (g + 1)], i_=lrwd_c[:, CPG * g:CPG * (g + 1)]:
                e.tensor_scalar(o, i_, c64[:, :], None, op0=ALU.mult),
            r=["lrwd_c", "c64"], w=["lrwd_c"])
        transpose(bank[4][0:CPG, 0:64], lrwd_c[:, CPG * g:CPG * (g + 1)], "bank4", ["lrwd_c"])
        vop(lambda e, o=lrtmp[:, :], i_=bank[4][0:CPG, 0:64]: e.tensor_copy(o, i_),
            r=["bank4"], w=["lrtmp"])
        for hh in range(2):
            in_ap = bass.AP(lrtmp, 8 * hh, [[64, CPG], [16, 4], [1, 8]])
            out_ap = bass.AP(flat2, 32 * NCH * hh + 32 * CPG * g, [[32 * NCH, 1], [1, 32 * CPG]])
            p.op("gpsimd", lambda e, o=out_ap, i=in_ap: e.dma_start(out=o, in_=i),
                 reads=[bf("lrtmp")], writes=[bf("flat2")], dma=True, sem="g_flat")

    def scan_chunk(c, pump):
        g, sub = c // CPG, c % CPG
        slot = g % 2
        kTs, vTs, qTs = kT[slot], vT[slot], qT[slot]
        kTn, vTn, qTn = f"kT{slot}", f"vT{slot}", f"qT{slot}"
        bsl = c % 2
        bcn = f"bc_c{bsl}"

        def qkv_sl(tens, hh, k):
            return tens[64 * hh:64 * hh + 64, sub, k // 2, 64 * (k % 2):64 * (k % 2) + 64]

        mm(bank[7][0:128, 0:32], halfsel[:, :], flat2[0:2, 32 * c:32 * c + 32],
           "bank7", ["halfsel", "flat2"])
        vop(lambda e, o=bc_c[bsl][:, :], i_=bank[7][0:128, 0:32]: e.tensor_copy(o, i_),
            r=["bank7"], w=[bcn])

        for j in range(4):
            transpose(bank[5][0:128, 128 * j:128 * j + 128], kTs[:, sub, j, :], "bank5", [kTn])
            cop = sop if (j % 2) else vop
            cop(lambda e, o=ktok[:, j, :], i_=bank[5][0:128, 128 * j:128 * j + 128]:
                    _copy(e, o, i_), r=["bank5"], w=["ktok"])
        for j in range(4):
            transpose(bank[6][0:128, 128 * j:128 * j + 128], vTs[:, sub, j, :], "bank6", [vTn])
            cop = sop if (j % 2) else vop
            cop(lambda e, o=vtok[:, j, :], i_=bank[6][0:128, 128 * j:128 * j + 128]:
                    _copy(e, o, i_), r=["bank6"], w=["vtok"])

        def pair_sl(tens, k):
            return tens[0:128, sub, k // 2, 64 * (k % 2):64 * (k % 2) + 64]

        for k in range(8):
            mm(bank[0][0:128, 64 * k:64 * k + 64], winT[:, k, :], pair_sl(kTs, k),
               "bank0", ["winT", kTn])
        for k in range(8):
            mm(bank[1][0:128, 64 * k:64 * k + 64], winT[:, k, :], pair_sl(qTs, k),
               "bank1", ["winT", qTn])
        for k in range(8):
            mm(bank[2][0:128, 64 * k:64 * k + 64], woutT[:, k, :], pair_sl(vTs, k),
               "bank2", ["woutT", vTn])

        def softmax(bkidx, dst_ap, dstname):
            bkn, bk = f"bank{bkidx}", bank[bkidx]
            vop(lambda e, o=mx_s[:, :].unsqueeze(2),
                i_=bk[0:128, 0:512].rearrange("p (c n) -> p c n", n=CH):
                    e.reduce_max(o, i_, axis=AX.X), r=[bkn], w=["mx_s"], hz=True)
            mxb = bass.AP(mx_s, 0, [[8, 128], [1, 8], [0, CH]])
            vop(lambda e, o=zsub[:, :].rearrange("p (c n) -> p c n", n=CH),
                i0=bk[0:128, 0:512].rearrange("p (c n) -> p c n", n=CH), i1=mxb:
                    e.tensor_tensor(o, i0, i1, op=ALU.subtract),
                r=[bkn, "mx_s"], w=["zsub"])
            sop(lambda e: e.activation(expt[:, :], zsub[:, :], ACT.Exp),
                r=["zsub"], w=["expt"])
            vop(lambda e, o=sm_s[:, :].unsqueeze(2),
                i_=expt[:, :].rearrange("p (c n) -> p c n", n=CH):
                    e.reduce_sum(o, i_, axis=AX.X), r=["expt"], w=["sm_s"], hz=True)
            vop(lambda e: e.reciprocal(rc_s[:, :], sm_s[:, :]), r=["sm_s"], w=["rc_s"], hz=True)
            rcb = bass.AP(rc_s, 0, [[8, 128], [1, 8], [0, CH]])
            vop(lambda e, o=dst_ap.rearrange("p (c n) -> p c n", n=CH),
                i0=expt[:, :].rearrange("p (c n) -> p c n", n=CH), i1=rcb:
                    e.tensor_tensor(o, i0, i1, op=ALU.mult),
                r=["expt", "rc_s"], w=[dstname])

        softmax(0, pT[:, :], "pT")
        vop(lambda e: e.tensor_tensor(m_s[:, :], pT[:, :], bank[2][0:128, 0:512], op=ALU.mult),
            r=["pT", "bank2"], w=["m_s"])
        mm(bank[3][0:2, 0:512], ones2col[:, :], m_s[:, :], "bank3", ["ones2col", "m_s"])
        vop(lambda e: e.tensor_copy(s2s[:, :], bank[3][0:2, 0:512]), r=["bank3"], w=["s2s"])
        mm(bank[4][0:128, 0:512], halfsel[:, :], s2s[:, :], "bank4", ["halfsel", "s2s"])
        vop(lambda e: e.tensor_tensor(t1_s[:, :], pT[:, :], bank[4][0:128, 0:512], op=ALU.mult),
            r=["pT", "bank4"], w=["t1_s"])
        vop(lambda e: e.tensor_tensor(hh_s[:, :], m_s[:, :], t1_s[:, :], op=ALU.subtract),
            r=["m_s", "t1_s"], w=["hh_s"])
        lrin_b = bass.AP(bc_c[bsl], 0, [[32, 128], [1, 8], [0, CH]])
        lrout_b = bass.AP(bc_c[bsl], 8, [[32, 128], [1, 8], [0, CH]])
        vop(lambda e, i1=lrin_b: e.tensor_tensor(
                hh_s[:, :].rearrange("p (c n) -> p c n", n=CH),
                hh_s[:, :].rearrange("p (c n) -> p c n", n=CH), i1, op=ALU.mult),
            r=["hh_s", bcn], w=["hh_s"])
        vop(lambda e, i1=lrout_b: e.tensor_tensor(
                ph_s[:, :].rearrange("p (c n) -> p c n", n=CH),
                pT[:, :].rearrange("p (c n) -> p c n", n=CH), i1, op=ALU.mult),
            r=["pT", bcn], w=["ph_s"])
        for j in range(4):
            transpose(bank[3][0:128, 128 * j:128 * j + 128],
                      ph_s[:, 128 * j:128 * j + 128], "bank3", ["ph_s"])
            cop = sop if (j % 2) else vop
            cop(lambda e, o=ptok[:, j, :], i_=bank[3][0:128, 128 * j:128 * j + 128]:
                    _copy(e, o, i_), r=["bank3"], w=["ptok"])
        for j in range(4):
            transpose(bank[4][0:128, 128 * j:128 * j + 128],
                      hh_s[:, 128 * j:128 * j + 128], "bank4", ["hh_s"])
            cop = sop if (j % 2) else vop
            cop(lambda e, o=htok[:, j, :], i_=bank[4][0:128, 128 * j:128 * j + 128]:
                    _copy(e, o, i_), r=["bank4"], w=["htok"])
        softmax(1, pqT[:, :], "pqT")
        for k in range(8):
            par, j = k % 2, k // 2
            for hh in range(2):
                ksl = ktok[64 * par:64 * par + 64, j, 64 * hh:64 * hh + 64]
                vsl = vtok[64 * par:64 * par + 64, j, 64 * hh:64 * hh + 64]
                psl = ptok[64 * par:64 * par + 64, j, 64 * hh:64 * hh + 64]
                hsl = htok[64 * par:64 * par + 64, j, 64 * hh:64 * hh + 64]
                mm(bank[5][64 * hh:64 * hh + 64, 64 * k:64 * k + 64], ksl, hsl,
                   "bank5", ["ktok", "htok"], tp=(64 * par, 64 * hh))
                mm(bank[6][64 * hh:64 * hh + 64, 64 * k:64 * k + 64], psl, vsl,
                   "bank6", ["ptok", "vtok"], tp=(64 * par, 64 * hh))
                mm(bank[7][64 * hh:64 * hh + 64, 64 * k:64 * k + 64], vsl, psl,
                   "bank7", ["vtok", "ptok"], tp=(64 * par, 64 * hh))
        for k in range(8):
            mm(bank[1][0:128, 64 * k:64 * k + 64], wout[:, k, :],
               pqT[0:128, 64 * k:64 * k + 64], "bank1", ["wout", "pqT"])
        vop(lambda e, o=ostg[:, :, slice(CH * sub, CH * (sub + 1))],
            i_=bank[1][0:128, 0:512].rearrange("p (a b) -> p a b", b=CH):
                e.tensor_copy(o, i_), r=["bank1"], w=["ostg"])
        for st_t, st_n, bkidx, qoff in [(winT, "winT", 5, 16),
                                        (wout, "wout", 6, 24),
                                        (woutT, "woutT", 7, 24)]:
            for hh in range(2):
                wd_ap = bass.AP(bc_c[bsl], qoff + 64 * hh * 32,
                                [[32, 64], [1, 8], [0, 64]])
                dia = st_t[64 * hh:64 * hh + 64, :, 64 * hh:64 * hh + 64]
                gop(lambda e, o=dia, i1=wd_ap:
                        e.tensor_tensor(o, o, i1, op=ALU.mult),
                    r=[st_n, bcn], w=[st_n])
                vop(lambda e, o=dia, bk=bank[bkidx], h2=hh:
                        e.tensor_tensor(o, bk[64 * h2:64 * h2 + 64, 0:512]
                                        .rearrange("p (a b) -> p a b", b=64), o,
                                        op=ALU.add),
                    r=[f"bank{bkidx}", st_n], w=[st_n])
        pump(8)
        if sub == CPG - 1:
            p.op("gpsimd", lambda e, o=oscr[:, :, G * g:G * (g + 1)], i=ostg[:, :, :]:
                    e.dma_start(out=o, in_=i),
                 reads=[bf("ostg")], writes=[bf("oscr")], dma=True, sem="g_oscr")

    pending = []

    def pump(n):
        done = 0
        while pending and done < n:
            try:
                next(pending[0])
                done += 1
            except StopIteration:
                pending.pop(0)

    for g in range(min(2, n_groups)):
        for _ in proj_group_gen(g):
            pass
    for c in range(n_chunks):
        g = c // CPG
        if c % CPG == 0:
            while pending:
                pump(100)
        scan_chunk(c, pump)
        if c % CPG == CPG - 1 and g + 2 < n_groups:
            pending.append(proj_group_gen(g + 2))
    while pending:
        pump(100)

    tapspec = {
        "kT0": (kT[0][:, :, :], "kT0"), "vT0": (vT[0][:, :, :], "vT0"),
        "qT0": (qT[0][:, :, :], "qT0"), "lrwd_c": (lrwd_c[:, :], "lrwd_c"),
        "flat2": (flat2[:, :], "flat2"), "winT": (winT[:, :, :], "winT"),
        "wout": (wout[:, :, :], "wout"), "woutT": (woutT[:, :, :], "woutT"),
        "ostg": (ostg[:, :, :], "ostg"), "pqT": (pqT[:, :], "pqT"),
        "pT": (pT[:, :], "pT"), "bc_c0": (bc_c[0][:, :], "bc_c0"),
        "xT": (xT[:, :, :], "xT"),
    }
    tap_outs = {}
    if "oscr" in taps:
        td = nc.dram_tensor("tap_oscr", [128, 8, L], F32, kind="ExternalOutput").ap()
        tap_outs["oscr"] = (128, 8, L)
        p.op("gpsimd", lambda e, o=td, i=oscr[:, :, :]: e.dma_start(out=o, in_=i),
             reads=[bf("oscr")], dma=True, sem="g_out")
    for name in taps:
        if name not in tapspec:
            continue
        ap, bufname = tapspec[name]
        shp = list(ap.shape)
        td = nc.dram_tensor("tap_" + name, shp, F32, kind="ExternalOutput").ap()
        tap_outs[name] = tuple(shp)
        p.op("gpsimd", lambda e, o=td, i=ap: e.dma_start(out=o, in_=i),
             reads=[bf(bufname)], dma=True, sem="g_out")

    if do_post:
        last_by_sem = {}
        for o in p.ops:
            last_by_sem[o.sem] = o
        p.barrier = list(last_by_sem.values())
        esA.close()

    if do_post:
        import contextlib as _ctx
        esB1 = _ctx.ExitStack()
        sb = lambda name, shape: es.enter_context(nc.sbuf_tensor(name, list(shape), F32))
        sb1 = lambda name, shape: esB1.enter_context(nc.sbuf_tensor(name, list(shape), F32))
        x1T = sb("x1T", (128, 8, TOWN))
        acc = sb("acc", (128, NT, DIM))
        comb = sb("comb", (128, NT, 8))
        combT = sb("combT", (8, NT, 128))
        r_s = sb("r_s", (128, DIM))
        s1a = sb("s1a", (128, 1))
        s1b = sb("s1b", (128, 1))
        s2a = sb("s2a", (128, 1))
        s2b = sb("s2b", (128, 1))
        mva = sb("mva", (128, 1))
        rst = sb("rst", (128, 1))
        g8 = sb("g8", (128, 8))
        gsc = sb("gsc", (128, 8))
        gm1 = sb("gm1", (128, 1))
        ge1 = sb("ge1", (128, 8))
        gs1 = sb("gs1", (128, 8))
        gf1 = sb("gf1", (128, 8))
        gp2 = sb("gp2", (128, 8))
        gmask = sb("gmask", (128, 8))
        gw_s = sb("gw_s", (128, 8, 8))
        outt = sb("outt", (128, DIM))
        dbgB = sb("dbgB", (128, 2 * DIM))
        woT_s = sb1("woT_s", (128, 8, DIM))
        otA = [sb1("otA0", (128, 8, 128))] * 2
        otB = [sb1("otB0", (128, 8, 128))] * 2
        otS = sb1("otS", (128, 8, 128))
        xres = [sb1("xres0", (128, DIM))] * 2

        dma_in("woT_s", woT_s[:], woT_d.rearrange("(k p) f -> p k f", p=128))
        dma_in("gw_s", gw_s[:], gw_d.rearrange("(k p) e -> p k e", p=128))


        def post_tile(t):
            slot = t % 2
            dma_in("otA0", otA[slot][:], oscr[:, :, 128 * t:128 * t + 128],
                   sem="s_otA0")
            dma_in("otB0", otB[slot][:], oscr[:, :, TOWN + 128 * t:TOWN + 128 * t + 128],
                   sem="s_otB0")
            dma_in("xres0", xres[slot][:], xown_d[128 * t:128 * t + 128, :],
                   sem="s_xres0")
            vop(lambda e, i0=otA[slot][:, :, :].rearrange("p a b -> p (a b)"),
                sc=msel[:, 1:2]:
                    e.tensor_scalar(otS[:, :, :].rearrange("p a b -> p (a b)"), i0, sc, None,
                                    op0=ALU.mult),
                r=["otA0", "msel"], w=["otS"])
            vop(lambda e, i0=otB[slot][:, :, :].rearrange("p a b -> p (a b)"),
                sc=msel[:, 0:1]:
                    e.scalar_tensor_tensor(otS[:, :, :].rearrange("p a b -> p (a b)"),
                                           i0, sc, otS[:, :, :].rearrange("p a b -> p (a b)"),
                                           op0=ALU.mult, op1=ALU.add),
                r=["otB0", "msel", "otS"], w=["otS"])
            for fh in range(2):
                for oc in range(8):
                    mm(bank[fh][0:128, 0:512], otS[:, oc, :],
                       woT_s[:, oc, 512 * fh:512 * fh + 512], f"bank{fh}",
                       ["otS", "woT_s"], start=(oc == 0), stop=(oc == 7))

            for fh in range(2):
                vop(lambda e, o=r_s[:, 512 * fh:512 * fh + 512], i0=bank[fh][0:128, 0:512],
                    i1=xres[slot][:, 512 * fh:512 * fh + 512]:
                        e.tensor_tensor(o, i0, i1, op=ALU.add),
                    r=[f"bank{fh}", "xres0"], w=["r_s"])
            vop(lambda e: e.tensor_reduce(s1a[:, :], r_s[:, :], axis=AX.X, op=ALU.add),
                r=["r_s"], w=["s1a"], hz=True)
            vop(lambda e: e.tensor_scalar(mva[:, :], s1a[:, :], -1.0 / DIM, None, op0=ALU.mult),
                r=["s1a"], w=["mva"])
            for fh in range(2):
                nm = "s2a" if fh == 0 else "s2b"
                a_out = s2a if fh == 0 else s2b
                sop(lambda e, i=r_s[:, 512 * fh:512 * fh + 512], a=a_out:
                        e.activation(bank[6][0:128, 0:512], i, ACT.Square,
                                     bias=mva[:, :], accum_out=a[:, :]),
                    r=["r_s", "mva"], w=["bank6", nm], hz=True)
            vop(lambda e: e.tensor_tensor(s2a[:, :], s2a[:, :], s2b[:, :], op=ALU.add),
                r=["s2a", "s2b"], w=["s2a"])
            sop(lambda e: e.activation(rst[:, :], s2a[:, :], ACT.Sqrt,
                                       bias=float(EPS_LN), scale=1.0 / DIM),
                r=["s2a", "epsv"], w=["rst"])
            vop(lambda e: e.reciprocal(rst[:, :], rst[:, :]), r=["rst"], w=["rst"], hz=True)
            vop(lambda e, o=acc[:, t, :]:
                    e.tensor_scalar(o, r_s[:, :], mva[:, :], rst[:, :],
                                    op0=ALU.add, op1=ALU.mult),
                r=["r_s", "mva", "rst"], w=["acc"])
            for dk in range(8):
                bkn = f"bank{2 + dk % 2}"
                bk = bank[2 + dk % 2]
                col = 128 * ((dk // 2) % 2)
                transpose(bk[0:128, col:col + 128], acc[:, t, 128 * dk:128 * dk + 128],
                          bkn, ["acc"])
                vop(lambda e, o=x1T[:, dk, 128 * t:128 * t + 128], i_=bk[0:128, col:col + 128]:
                        e.tensor_copy(o, i_), r=[bkn], w=["x1T"])
            for dk in range(8):
                mm(bank[4][0:128, 0:8], x1T[:, dk, 128 * t:128 * t + 128], gw_s[:, dk, :],
                   "bank4", ["x1T", "gw_s"], start=(dk == 0), stop=False)
            mm(bank[4][0:128, 0:8], onesrow[:, :], gb_s[:, :], "bank4",
               ["onesrow", "gb_s"], start=False, stop=True)
            vop(lambda e: e.tensor_scalar(gsc[:, :], bank[4][0:128, 0:8], float(GEPS), None,
                                          op0=ALU.max), r=["bank4"], w=["gsc"])
            vop(lambda e: e.reduce_max(gm1[:, :], gsc[:, :], axis=AX.X), r=["gsc"], w=["gm1"], hz=True)
            vop(lambda e: e.tensor_scalar(gm1[:, :], gm1[:, :], -1.0, None, op0=ALU.mult),
                r=["gm1"], w=["gm1"])
            sop(lambda e: e.activation(g8[:, :], gsc[:, :], ACT.Exp, bias=gm1[:, :]),
                r=["gsc", "gm1"], w=["g8"])
            vop(lambda e: e.reduce_sum(gm1[:, :], g8[:, :], axis=AX.X), r=["g8"], w=["gm1"], hz=True)
            vop(lambda e: e.reciprocal(gm1[:, :], gm1[:, :]), r=["gm1"], w=["gm1"], hz=True)
            vop(lambda e: e.tensor_scalar(g8[:, :], g8[:, :], gm1[:, :], None, op0=ALU.mult),
                r=["g8", "gm1"], w=["g8"])
            vop(lambda e: e.reduce_max(gm1[:, :], g8[:, :], axis=AX.X), r=["g8"], w=["gm1"], hz=True)
            gm1b = bass.AP(gm1, 0, [[1, 128], [1, 1], [0, 8]])
            vop(lambda e, i1=gm1b: e.tensor_tensor(ge1[:, :].unsqueeze(1),
                g8[:, :].unsqueeze(1), i1, op=ALU.is_ge), r=["g8", "gm1"], w=["ge1"])
            vop(lambda e: e.tensor_tensor(gs1[:, :], ge1[:, :], pow2[:, :], op=ALU.mult),
                r=["ge1", "pow2"], w=["gs1"])
            vop(lambda e: e.reduce_max(gm1[:, :], gs1[:, :], axis=AX.X), r=["gs1"], w=["gm1"], hz=True)
            gm1b2 = bass.AP(gm1, 0, [[1, 128], [1, 1], [0, 8]])
            vop(lambda e, i1=gm1b2: e.tensor_tensor(gf1[:, :].unsqueeze(1),
                gs1[:, :].unsqueeze(1), i1, op=ALU.is_ge), r=["gs1", "gm1"], w=["gf1"])
            vop(lambda e: e.scalar_tensor_tensor(gp2[:, :], gf1[:, :], -2.0, g8[:, :],
                                                 op0=ALU.mult, op1=ALU.add),
                r=["gf1", "g8"], w=["gp2"])
            vop(lambda e: e.reduce_max(gm1[:, :], gp2[:, :], axis=AX.X), r=["gp2"], w=["gm1"], hz=True)
            gm1b3 = bass.AP(gm1, 0, [[1, 128], [1, 1], [0, 8]])
            vop(lambda e, i1=gm1b3: e.tensor_tensor(ge1[:, :].unsqueeze(1),
                gp2[:, :].unsqueeze(1), i1, op=ALU.is_ge), r=["gp2", "gm1"], w=["ge1"])
            vop(lambda e: e.tensor_tensor(gs1[:, :], ge1[:, :], pow2[:, :], op=ALU.mult),
                r=["ge1", "pow2"], w=["gs1"])
            vop(lambda e: e.reduce_max(gm1[:, :], gs1[:, :], axis=AX.X), r=["gs1"], w=["gm1"], hz=True)
            gm1b4 = bass.AP(gm1, 0, [[1, 128], [1, 1], [0, 8]])
            vop(lambda e, i1=gm1b4: e.tensor_tensor(ge1[:, :].unsqueeze(1),
                gs1[:, :].unsqueeze(1), i1, op=ALU.is_ge), r=["gs1", "gm1"], w=["ge1"])
            vop(lambda e: e.tensor_tensor(gmask[:, :], gf1[:, :], ge1[:, :], op=ALU.add),
                r=["gf1", "ge1"], w=["gmask"])
            vop(lambda e: e.tensor_scalar(gs1[:, :], g8[:, :], float(WTHR), None, op0=ALU.max),
                r=["g8"], w=["gs1"])
            vop(lambda e: e.tensor_tensor(gs1[:, :], gs1[:, :], gmask[:, :], op=ALU.mult),
                r=["gs1", "gmask"], w=["gs1"])
            vop(lambda e: e.reduce_sum(gm1[:, :], gs1[:, :], axis=AX.X), r=["gs1"], w=["gm1"], hz=True)
            vop(lambda e: e.reciprocal(gm1[:, :], gm1[:, :]), r=["gm1"], w=["gm1"], hz=True)
            vop(lambda e, o=comb[:, t, :]: e.tensor_scalar(o, gs1[:, :], gm1[:, :], 0.5,
                                                           op0=ALU.mult, op1=ALU.mult),
                r=["gs1", "gm1"], w=["comb"])
            if t == 0:
                vop(lambda e: e.tensor_copy(dbgB[:, 64:72], bank[4][0:128, 0:8]),
                    r=["bank4"], w=["dbgB"])
                vop(lambda e: e.tensor_copy(dbgB[:, 0:8], gsc[:, :]), r=["gsc"], w=["dbgB"])
                vop(lambda e: e.tensor_copy(dbgB[:, 8:16], g8[:, :]), r=["g8"], w=["dbgB"])
                vop(lambda e: e.tensor_copy(dbgB[:, 16:24], gf1[:, :]), r=["gf1"], w=["dbgB"])
                vop(lambda e: e.tensor_copy(dbgB[:, 24:32], ge1[:, :]), r=["ge1"], w=["dbgB"])
                vop(lambda e: e.tensor_copy(dbgB[:, 32:40], gmask[:, :]), r=["gmask"], w=["dbgB"])
                vop(lambda e: e.tensor_copy(dbgB[:, 40:48], gs1[:, :]), r=["gs1"], w=["dbgB"])
                vop(lambda e: e.tensor_copy(dbgB[:, 48:49], gm1[:, :]), r=["gm1"], w=["dbgB"])
            transpose(bank[5][0:8, 0:128], comb[:, t, :], "bank5", ["comb"])
            vop(lambda e, o=combT[:, t, :]: e.tensor_copy(o, bank[5][0:8, 0:128]),
                r=["bank5"], w=["combT"])
            for fh in range(2):
                mm(bank[6][0:128, 0:512], combT[:, t, :], eb_s[:, 512 * fh:512 * fh + 512],
                   "bank6", ["combT", "eb_s"])
                vop(lambda e, o=acc[:, t, 512 * fh:512 * fh + 512], i0=bank[6][0:128, 0:512]:
                        e.tensor_tensor(o, i0, o, op=ALU.add),
                    r=["bank6", "acc"], w=["acc"])

        for t in range(NT):
            post_tile(t)

        last_by_sem = {}
        for o in p.ops:
            last_by_sem[o.sem] = o
        p.barrier = list(last_by_sem.values())
        p.barrier_applied = set()
        esB1.close()
        ew_s = [sb(f"ew_s{i}", (128, 8, 512)) for i in range(2)]

        for e_i in range(NEXP):
            for fh in range(2):
                slot = (e_i * 2 + fh) % 2
                dma_in(f"ew_s{slot}", ew_s[slot][:],
                       ew_d[e_i, :, 512 * fh:512 * fh + 512].rearrange("(k p) f -> p k f", p=128),
                       sem=f"s_ew{slot}")
                for t in range(NT):
                    bkn = f"bank{t % 4}"
                    bk = bank[t % 4]
                    for dk in range(8):
                        mm(bk[0:128, 0:512], x1T[:, dk, 128 * t:128 * t + 128],
                           ew_s[slot][:, dk, :], bkn, ["x1T", f"ew_s{slot}"],
                           start=(dk == 0), stop=(dk == 7))
                    vop(lambda e, o=acc[:, t, 512 * fh:512 * fh + 512], i0=bk[0:128, 0:512],
                        sc=comb[:, t, e_i:e_i + 1]:
                            e.scalar_tensor_tensor(o, i0, sc, o, op0=ALU.mult, op1=ALU.add),
                        r=[bkn, "comb", "acc"], w=["acc"])

        for t in range(NT):
            for fh in range(2):
                nm = "s1a" if fh == 0 else "s1b"
                a_out = s1a if fh == 0 else s1b
                vop(lambda e, i=acc[:, t, 512 * fh:512 * fh + 512], a=a_out:
                        e.tensor_reduce(a[:, :], i, axis=AX.X, op=ALU.add),
                    r=["acc"], w=[nm], hz=True)
            vop(lambda e: e.tensor_tensor(s1a[:, :], s1a[:, :], s1b[:, :], op=ALU.add),
                r=["s1a", "s1b"], w=["s1a"])
            vop(lambda e: e.tensor_scalar(mva[:, :], s1a[:, :], -1.0 / DIM, None, op0=ALU.mult),
                r=["s1a"], w=["mva"])
            for fh in range(2):
                nm = "s2a" if fh == 0 else "s2b"
                a_out = s2a if fh == 0 else s2b
                sop(lambda e, i=acc[:, t, 512 * fh:512 * fh + 512], a=a_out:
                        e.activation(bank[6][0:128, 0:512], i, ACT.Square,
                                     bias=mva[:, :], accum_out=a[:, :]),
                    r=["acc", "mva"], w=["bank6", nm], hz=True)
            vop(lambda e: e.tensor_tensor(s2a[:, :], s2a[:, :], s2b[:, :], op=ALU.add),
                r=["s2a", "s2b"], w=["s2a"])
            sop(lambda e: e.activation(rst[:, :], s2a[:, :], ACT.Sqrt,
                                       bias=float(EPS_LN), scale=1.0 / DIM),
                r=["s2a", "epsv"], w=["rst"])
            vop(lambda e: e.reciprocal(rst[:, :], rst[:, :]), r=["rst"], w=["rst"], hz=True)
            vop(lambda e, o=outt[:, :], i=acc[:, t, :]:
                    e.tensor_scalar(o, i, mva[:, :], rst[:, :], op0=ALU.add, op1=ALU.mult),
                r=["acc", "mva", "rst"], w=["outt"])
            p.op("gpsimd", lambda e, o=out_d[128 * t:128 * t + 128, :], i=outt[:, :]:
                    e.dma_start(out=o, in_=i),
                 reads=[bf("outt")], writes=[bf("out_d")], dma=True, sem="g_out")

    if do_post:
        tapspecB = {"acc": (acc[:, :, :], "acc"), "comb": (comb[:, :, :], "comb"),
                    "x1T": (x1T[:, :, :], "x1T"), "dbgB": (dbgB[:, :], "dbgB")}
        for name in taps:
            if name not in tapspecB:
                continue
            ap, bufname = tapspecB[name]
            shp = list(ap.shape)
            td = nc.dram_tensor("tap_" + name, shp, F32, kind="ExternalOutput").ap()
            tap_outs[name] = tuple(shp)
            p.op("gpsimd", lambda e, o=td, i=ap: e.dma_start(out=o, in_=i),
                 reads=[bf(bufname)], dma=True, sem="g_out")

    final_counts = p.plan()
    sems = {s: es.enter_context(nc.semaphore("sem_" + s)) for s in final_counts}

    with nc.Block() as block:
        @block.sync
        def _(eng):
            p.emit("sync", eng, sems)
            for s2 in sorted(final_counts):
                if s2 != "sync" and final_counts[s2]:
                    eng.wait_ge(sems[s2], final_counts[s2])

        @block.tensor
        def _(eng):
            p.emit("tensor", eng, sems)

        @block.vector
        def _(eng):
            p.emit("vector", eng, sems)

        @block.scalar
        def _(eng):
            p.emit("scalar", eng, sems)

        @block.gpsimd
        def _(eng):
            p.emit("gpsimd", eng, sems)

    if not do_post:
        esA.close()
    es.close()
    return nc, tap_outs


def make_in_maps(inputs):
    prep = host_prep(inputs)
    x = np.ascontiguousarray(np.asarray(inputs["L0_attn_x"], dtype=np.float32))
    in_maps = []
    for c in range(8):
        b, half = c // 2, c % 2
        ms = np.zeros((128, 2), np.float32)
        ms[:, 0] = float(half)
        ms[:, 1] = 1.0 - float(half)
        m = {"x": np.ascontiguousarray(x[b]),
             "x_own": np.ascontiguousarray(x[b, TOWN * half:TOWN * (half + 1), :]),
             "mselw": ms}
        m.update(prep)
        in_maps.append(m)
    return in_maps


def kernel(**inputs):
    in_maps = make_in_maps(inputs)
    nc, _ = build_core()
    mybir.codegen_inst_isa_subclasses(nc)
    res = run_bass_kernel_spmd(nc, in_maps, core_ids=list(range(8)))
    out = np.empty((B, L, DIM), np.float32)
    for c in range(8):
        b, half = c // 2, c % 2
        out[b, TOWN * half:TOWN * (half + 1), :] = np.asarray(res.results[c]["out"])
    return out


# version 07
# speedup vs baseline: 1.1566x; 1.1566x over previous
import numpy as np
import concourse.bass as bass
import concourse.mybir as mybir
from concourse.bass_utils import run_bass_kernel_spmd

F32 = mybir.dt.float32
AX = mybir.AxisListType
ALU = mybir.AluOpType
ACT = mybir.ActivationFunctionType

B, L, DIM = 4, 4096, 1024
CH = 64
NCH = L // CH
G = 256
NG = L // G
CPG = G // CH
NEXP = 8
TOWN = L // 2
NT = TOWN // 128
EPS_LN = 1e-5
BASE_LR, BASE_WD = 1e-3, 0.9
GEPS, WTHR = 1e-4, 0.1
SCAN_STAGE = 6


class _Op:
    __slots__ = ("eng", "emit", "deps", "inc", "post", "sem", "hz")

    def __init__(self, eng, emit, deps, inc, sem, hz):
        self.eng, self.emit, self.deps, self.inc, self.sem = eng, emit, deps, inc, sem
        self.hz = hz
        self.post = None


class Buf:
    __slots__ = ("writer", "readers")

    def __init__(self):
        self.writer = None
        self.readers = []


class Prog:
    ENGS = ("sync", "tensor", "vector", "scalar", "gpsimd")

    def __init__(self):
        self.ops = []
        self.barrier = None
        self.barrier_applied = set()

    def op(self, eng, emit, reads=(), writes=(), deps=(), dma=False, sem=None, hz=False):
        dd = list(deps)
        if self.barrier is not None and eng not in self.barrier_applied:
            dd.extend(self.barrier)
            self.barrier_applied.add(eng)
        for b in reads:
            if b.writer is not None:
                dd.append(b.writer)
        for b in writes:
            dd.extend(b.readers)
            if b.writer is not None:
                dd.append(b.writer)
        assert (sem is not None) == dma, "DMA ops need an explicit slot sem"
        o = _Op(eng, emit, dd, 16 if dma else 1, sem if dma else eng, hz)
        self.ops.append(o)
        for b in reads:
            b.readers.append(o)
        for b in writes:
            b.writer = o
            b.readers = []
        return o

    def plan(self):
        cnt = {}
        for o in self.ops:
            cnt[o.sem] = cnt.get(o.sem, 0) + o.inc
            o.post = cnt[o.sem]
        return cnt

    def emit(self, eng_name, eng, sems):
        watermark = {}
        for o in self.ops:
            if o.eng != eng_name:
                continue
            best = {}
            for d in o.deps:
                if (d.eng == eng_name == "tensor" and o.eng == "tensor"
                        and d.sem == "tensor" and not d.hz):
                    continue
                if best.get(d.sem, -1) < d.post:
                    best[d.sem] = d.post
            for s2 in sorted(best):
                v = best[s2]
                if watermark.get(s2, 0) >= v:
                    continue
                eng.wait_ge(sems[s2], v)
                watermark[s2] = v
            o.emit(eng).then_inc(sems[o.sem], o.inc)


def _head_perm():
    idx = np.empty(DIM, np.int64)
    for oc in range(8):
        for c2 in range(2):
            h = oc + 8 * c2
            idx[oc * 128 + c2 * 64: oc * 128 + c2 * 64 + 64] = np.arange(h * 64, h * 64 + 64)
    return idx


def host_prep(inp):
    idx = _head_perm()
    f32c = lambda a: np.ascontiguousarray(np.asarray(a), dtype=np.float32)
    d = {}
    d["wqT"] = f32c(np.asarray(inp["L0_attn_Wq"]).T[:, idx])
    d["wkT"] = f32c(np.asarray(inp["L0_attn_Wk"]).T[:, idx])
    d["wvT"] = f32c(np.asarray(inp["L0_attn_Wv"]).T[:, idx])
    d["wlbT"] = f32c(np.concatenate([np.asarray(inp["L0_attn_Wlr"]).T,
                                     np.asarray(inp["L0_attn_Wbeta"]).T], axis=1))
    d["woT"] = f32c(np.asarray(inp["L0_attn_Wo"]).T[idx, :])
    wi = np.asarray(inp["L0_attn_W_in_init"])[0]
    wo = np.asarray(inp["L0_attn_W_out_init"])[0]

    def compact(arr_hab):
        a = np.asarray(arr_hab).reshape(2, 8, 64, 64)
        return f32c(a.transpose(0, 2, 1, 3).reshape(128, 8, 64))

    d["winT0"] = compact(wi.transpose(1, 2, 0))
    d["wout0"] = compact(wo.transpose(1, 0, 2))
    d["woutT0"] = compact(wo.transpose(1, 2, 0))
    d["gw"] = f32c(inp["L0_ffn_gate_W"])
    d["gb"] = f32c(np.asarray(inp["L0_ffn_gate_b"]).reshape(1, 8))
    d["ew"] = f32c(inp["L0_ffn_expert_W"])
    d["eb"] = f32c(inp["L0_ffn_expert_b"])
    d["identw"] = np.eye(128, dtype=np.float32)
    oc2 = np.zeros((128, 2), np.float32)
    oc2[:64, 0] = 1.0
    oc2[64:, 1] = 1.0
    d["ones2colw"] = oc2
    hs = np.zeros((2, 128), np.float32)
    hs[0, :64] = 1.0
    hs[1, 64:] = 1.0
    d["halfselw"] = hs
    d["onesroww"] = np.ones((1, 128), np.float32)
    d["pow2w"] = np.ascontiguousarray(
        np.broadcast_to(2.0 ** -np.arange(8, dtype=np.float32), (128, 8)))
    c64 = np.empty((64, 1), np.float32)
    c64[:32] = BASE_LR / 64.0
    c64[32:] = BASE_WD / 64.0
    d["c64w"] = c64
    d["epsw"] = np.full((128, 1), EPS_LN, np.float32)
    return d


def build_core(n_groups=NG, n_chunks=NCH, do_post=True, taps=()):
    nc = bass.Bass("TRN2", target_bir_lowering=False, detect_race_conditions=False)
    p = Prog()
    bufs = {}

    def bf(name):
        if name not in bufs:
            bufs[name] = Buf()
        return bufs[name]

    def dram_in(name, shape):
        return nc.dram_tensor(name, list(shape), F32, kind="ExternalInput").ap()

    xb = dram_in("x", (L, DIM))
    xown_d = dram_in("x_own", (TOWN, DIM))
    msel_d = dram_in("mselw", (128, 2))
    wqT = dram_in("wqT", (DIM, DIM))
    wkT = dram_in("wkT", (DIM, DIM))
    wvT = dram_in("wvT", (DIM, DIM))
    wlbT_d = dram_in("wlbT", (DIM, 64))
    woT_d = dram_in("woT", (DIM, DIM))
    winT0 = dram_in("winT0", (128, 8, 64))
    wout0 = dram_in("wout0", (128, 8, 64))
    woutT0 = dram_in("woutT0", (128, 8, 64))
    gw_d = dram_in("gw", (DIM, 8))
    gb_d = dram_in("gb", (1, 8))
    ew_d = dram_in("ew", (NEXP, DIM, DIM))
    eb_d = dram_in("eb", (NEXP, DIM))
    ident_d = dram_in("identw", (128, 128))
    ones2col_d = dram_in("ones2colw", (128, 2))
    halfsel_d = dram_in("halfselw", (2, 128))
    onesrow_d = dram_in("onesroww", (1, 128))
    pow2_d = dram_in("pow2w", (128, 8))
    c64_d = dram_in("c64w", (64, 1))
    eps_d = dram_in("epsw", (128, 1))

    out_d = nc.dram_tensor("out", [TOWN, DIM], F32, kind="ExternalOutput").ap()
    oscr = nc.dram_tensor("oscr", [128, 8, L], F32, kind="Internal").ap()

    import contextlib
    es = contextlib.ExitStack()
    esA = contextlib.ExitStack()
    sbP = lambda name, shape: es.enter_context(nc.sbuf_tensor(name, list(shape), F32))
    sb = lambda name, shape: esA.enter_context(nc.sbuf_tensor(name, list(shape), F32))
    psum = lambda name, shape: es.enter_context(nc.psum_tensor(name, list(shape), F32))

    ident = sbP("ident", (128, 128))
    onesrow = sbP("onesrow", (1, 128))
    pow2 = sbP("pow2", (128, 8))
    gb_s = sbP("gb_s", (1, 8))
    eb_s = sbP("eb_s", (8, DIM))
    msel = sbP("msel", (128, 2))
    epsv = sbP("epsv", (128, 1))
    nc.const_aps.register(F32, float(EPS_LN), epsv[:, 0:1])

    wq_s = sb("wq_s", (128, 8, DIM))
    wk_s = sb("wk_s", (128, 8, DIM))
    wlb_s = sb("wlb_s", (128, 8, 64))
    wv_s = [sb(f"wv_s{i}", (128, 8, 128)) for i in range(2)]
    ones2col = sb("ones2col", (128, 2))
    halfsel = sb("halfsel", (2, 128))
    c64 = sb("c64", (64, 1))

    winT = sb("winT", (128, 8, 64))
    wout = sb("wout", (128, 8, 64))
    woutT = sb("woutT", (128, 8, 64))

    xrow = [sb(f"xrow{i}", (128, 2, DIM)) for i in range(2)]
    xT = sb("xT", (128, 8, G))
    qT = [sb(f"qT{i}", (128, CPG, 4, 128)) for i in range(2)]
    kT = [sb(f"kT{i}", (128, CPG, 4, 128)) for i in range(2)]
    vT = [sb(f"vT{i}", (128, CPG, 4, 128)) for i in range(2)]
    lrwdT = sb("lrwdT", (64, G))
    lrwd_c = sb("lrwd_c", (64, NCH))
    lrtmp = sb("lrtmp", (4, 64))
    flat2 = sb("flat2", (2, NCH * 32))
    bc_c = [sb(f"bc_c{i}", (128, 32)) for i in range(2)]

    pqT = sb("pqT", (128, 512))
    pT = sb("pT", (128, 512))
    zsub = sb("zsub", (128, 512))
    expt = sb("expt", (128, 512))
    m_s = sb("m_s", (128, 512))
    t1_s = sb("t1_s", (128, 512))
    hh_s = sb("hh_s", (128, 512))
    ph_s = sb("ph_s", (128, 512))
    s2s = sb("s2s", (2, 512))
    mx_s = sb("mx_s", (128, 8))
    sm_s = sb("sm_s", (128, 8))
    rc_s = sb("rc_s", (128, 8))
    ktok = sb("ktok", (128, 4, 128))
    vtok = sb("vtok", (128, 4, 128))
    ptok = sb("ptok", (128, 4, 128))
    htok = sb("htok", (128, 4, 128))
    ostg = sb("ostg", (128, 8, G))
    dbg1 = sb("dbg1", (128, 512))
    dbg2 = sb("dbg2", (128, 512))
    dbg3 = sb("dbg3", (128, 512))

    bank = [psum(f"bank{i}", (128, 512)) for i in range(8)]

    def dma_in(bufname, out_ap, in_ap, eng="sync", extra_reads=(), sem="s_init"):
        return p.op(eng, lambda e, o=out_ap, i=in_ap: e.dma_start(out=o, in_=i),
                    reads=list(extra_reads), writes=[bf(bufname)], dma=True, sem=sem)

    dma_in("wq_s", wq_s[:], wqT.rearrange("(k p) f -> p k f", p=128))
    dma_in("wk_s", wk_s[:], wkT.rearrange("(k p) f -> p k f", p=128))
    dma_in("wlb_s", wlb_s[:], wlbT_d.rearrange("(k p) f -> p k f", p=128))
    for name, s_t, d_t in [("ident", ident, ident_d), ("ones2col", ones2col, ones2col_d),
                           ("halfsel", halfsel, halfsel_d), ("onesrow", onesrow, onesrow_d),
                           ("pow2", pow2, pow2_d), ("c64", c64, c64_d),
                           ("gb_s", gb_s, gb_d), ("eb_s", eb_s, eb_d), ("msel", msel, msel_d),
                           ("epsv", epsv, eps_d)]:
        dma_in(name, s_t[:], d_t[:])
    dma_in("winT", winT[:], winT0[:])
    dma_in("wout", wout[:], wout0[:])
    dma_in("woutT", woutT[:], woutT0[:])

    def mm(out_ap, lhsT_ap, rhs_ap, wbankname, rbufnames, start=True, stop=True, tp=None):
        wr = [bf(wbankname)]
        if wbankname in ("bank3", "bank4", "bank5", "bank6"):
            wr += [bf(wbankname + f"q{j}") for j in range(4)]
        return p.op("tensor",
                    lambda e, o=out_ap, a=lhsT_ap, b=rhs_ap, s=start, t=stop, q=tp:
                        e.matmul(o, a, b, start=s, stop=t, tile_position=q),
                    reads=[bf(n) for n in rbufnames], writes=wr)

    def transpose(out_ap, in_ap, wbankname, rbufnames):
        np_ = in_ap.shape[0]
        return p.op("tensor",
                    lambda e, o=out_ap, i=in_ap, n_=np_: e.transpose(o, i, ident[0:n_, 0:n_]),
                    reads=[bf(n) for n in rbufnames] + [bf("ident")],
                    writes=[bf(wbankname)])

    def transpose_q(bkidx, j, in_ap, rbufnames):
        np_ = in_ap.shape[0]
        out_ap = bank[bkidx][0:128, 128 * j:128 * j + 128]
        return p.op("tensor",
                    lambda e, o=out_ap, i=in_ap, n_=np_: e.transpose(o, i, ident[0:n_, 0:n_]),
                    reads=[bf(n) for n in rbufnames] + [bf("ident"), bf(f"bank{bkidx}")],
                    writes=[bf(f"bank{bkidx}q{j}")])

    def bank_w(bkidx):
        return [bf(f"bank{bkidx}")] + [bf(f"bank{bkidx}q{j}") for j in range(4)]

    def vop(emit, r=(), w=(), hz=False):
        return p.op("vector", emit, reads=[bf(n) for n in r],
                    writes=[bf(n) for n in w], hz=hz)

    def sop(emit, r=(), w=(), hz=False):
        return p.op("scalar", emit, reads=[bf(n) for n in r],
                    writes=[bf(n) for n in w], hz=hz)

    def _copy(e, o, i_):
        return e.tensor_copy(o, i_) if hasattr(e, "tensor_copy") else e.copy(o, i_)

    def gop(emit, r=(), w=()):
        return p.op("gpsimd", emit, reads=[bf(n) for n in r], writes=[bf(n) for n in w])

    def proj_group_gen(g):
        slot = g % 2
        xr = xrow[slot]
        dma_in(f"xrow{slot}", xr[:], xb[G * g:G * (g + 1), :].rearrange("(t p) f -> p t f", p=128), sem=f"s_xrow{slot}")
        for quad in range(4):
            tt, dk0 = quad // 2, 4 * (quad % 2)
            bkidx = 3 + quad % 2
            for jj in range(4):
                transpose_q(bkidx, jj, xr[:, tt, 128 * (dk0 + jj):128 * (dk0 + jj) + 128],
                            [f"xrow{slot}"])
            cop = sop if (quad % 2) else vop
            xt_dst = bass.AP(xT, (4 * (quad % 2)) * G + 128 * tt,
                             [[8 * G, 128], [G, 4], [1, 128]])
            cop(lambda e, o=xt_dst, i_=bank[bkidx][0:128, 0:512]
                .rearrange("p (a b) -> p a b", b=128): _copy(e, o, i_),
                r=[f"bank{bkidx}q{j}" for j in range(4)], w=["xT"])
            yield

        def projmm(lhs_ap_fn, wname, dst_ap, dstname, act, bkidx):
            bkn, bk = f"bank{bkidx}", bank[bkidx]
            for dk in range(8):
                mm(bk[0:128, 0:G], lhs_ap_fn(dk), xT[:, dk, :], bkn, [wname, "xT"],
                   start=(dk == 0), stop=(dk == 7))
            src3 = bk[0:128, 0:G].rearrange("p (s n) -> p s n", n=CH)
            if act == "silu":
                sop(lambda e, o=dst_ap, i_=src3: e.activation(o, i_, ACT.Silu),
                    r=[bkn], w=[dstname])
            else:
                vop(lambda e, o=dst_ap, i_=src3: e.tensor_copy(o, i_),
                    r=[bkn], w=[dstname])

        def qkv_dst(tens, oc):
            return tens[:, 0:CPG, oc // 2, 64 * (oc % 2):64 * (oc % 2) + 64]

        for oc in range(8):
            projmm(lambda dk, oc=oc: wk_s[:, dk, 128 * oc:128 * oc + 128], "wk_s",
                   qkv_dst(kT[slot], oc), f"kT{slot}", "silu", 3 + oc % 2)
            yield
        for oc in range(8):
            dma_in(f"wv_s{oc % 2}", wv_s[oc % 2][:],
                   wvT.rearrange("(k p) f -> p k f", p=128)[:, :, 128 * oc:128 * oc + 128],
                   sem=f"s_wv{oc % 2}")
            projmm(lambda dk, oc=oc: wv_s[oc % 2][:, dk, :], f"wv_s{oc % 2}",
                   qkv_dst(vT[slot], oc), f"vT{slot}", "copy", 3 + oc % 2)
            yield
        for oc in range(8):
            projmm(lambda dk, oc=oc: wq_s[:, dk, 128 * oc:128 * oc + 128], "wq_s",
                   qkv_dst(qT[slot], oc), f"qT{slot}", "silu", 3 + oc % 2)
            yield
        bkn, bk = "bank3", bank[3]
        for dk in range(8):
            mm(bk[0:64, 0:G], wlb_s[:, dk, :], xT[:, dk, :], bkn, ["wlb_s", "xT"],
               start=(dk == 0), stop=(dk == 7))
        sop(lambda e, o=lrwdT[:, :], i_=bk[0:64, 0:G]: e.activation(o, i_, ACT.Sigmoid),
            r=[bkn], w=["lrwdT"])
        vop(lambda e, o=lrwd_c[:, CPG * g:CPG * (g + 1)].unsqueeze(2),
            i_=lrwdT[:, :].rearrange("p (c n) -> p c n", n=CH): e.reduce_sum(o, i_, axis=AX.X),
            r=["lrwdT"], w=["lrwd_c"], hz=True)
        vop(lambda e, o=lrwd_c[:, CPG * g:CPG * (g + 1)], i_=lrwd_c[:, CPG * g:CPG * (g + 1)]:
                e.tensor_scalar(o, i_, c64[:, :], None, op0=ALU.mult),
            r=["lrwd_c", "c64"], w=["lrwd_c"])
        transpose(bank[4][0:CPG, 0:64], lrwd_c[:, CPG * g:CPG * (g + 1)], "bank4", ["lrwd_c"])
        vop(lambda e, o=lrtmp[:, :], i_=bank[4][0:CPG, 0:64]: e.tensor_copy(o, i_),
            r=["bank4"], w=["lrtmp"])
        for hh in range(2):
            in_ap = bass.AP(lrtmp, 8 * hh, [[64, CPG], [16, 4], [1, 8]])
            out_ap = bass.AP(flat2, 32 * NCH * hh + 32 * CPG * g, [[32 * NCH, 1], [1, 32 * CPG]])
            p.op("gpsimd", lambda e, o=out_ap, i=in_ap: e.dma_start(out=o, in_=i),
                 reads=[bf("lrtmp")], writes=[bf("flat2")], dma=True, sem="g_flat")

    def scan_chunk(c, pump):
        g, sub = c // CPG, c % CPG
        slot = g % 2
        kTs, vTs, qTs = kT[slot], vT[slot], qT[slot]
        kTn, vTn, qTn = f"kT{slot}", f"vT{slot}", f"qT{slot}"
        bsl = c % 2
        bcn = f"bc_c{bsl}"

        def qkv_sl(tens, hh, k):
            return tens[64 * hh:64 * hh + 64, sub, k // 2, 64 * (k % 2):64 * (k % 2) + 64]

        mm(bank[7][0:128, 0:32], halfsel[:, :], flat2[0:2, 32 * c:32 * c + 32],
           "bank7", ["halfsel", "flat2"])
        vop(lambda e, o=bc_c[bsl][:, :], i_=bank[7][0:128, 0:32]: e.tensor_copy(o, i_),
            r=["bank7"], w=[bcn])

        for j in range(4):
            transpose_q(5, j, kTs[:, sub, j, :], [kTn])
        vop(lambda e: e.tensor_copy(ktok[:, :, :],
                                    bank[5][0:128, 0:512].rearrange("p (a b) -> p a b", b=128)),
            r=[f"bank5q{j}" for j in range(4)], w=["ktok"])
        for j in range(4):
            transpose_q(6, j, vTs[:, sub, j, :], [vTn])
        sop(lambda e: _copy(e, vtok[:, :, :],
                            bank[6][0:128, 0:512].rearrange("p (a b) -> p a b", b=128)),
            r=[f"bank6q{j}" for j in range(4)], w=["vtok"])

        for k in range(8):
            for hh in range(2):
                mm(bank[0][64 * hh:64 * hh + 64, 64 * k:64 * k + 64],
                   winT[64 * hh:64 * hh + 64, k, :], qkv_sl(kTs, hh, k),
                   "bank0", ["winT", kTn], tp=(64 * hh, 64 * hh))
        for k in range(8):
            for hh in range(2):
                mm(bank[1][64 * hh:64 * hh + 64, 64 * k:64 * k + 64],
                   winT[64 * hh:64 * hh + 64, k, :], qkv_sl(qTs, hh, k),
                   "bank1", ["winT", qTn], tp=(64 * hh, 64 * hh))
        for k in range(8):
            for hh in range(2):
                mm(bank[2][64 * hh:64 * hh + 64, 64 * k:64 * k + 64],
                   woutT[64 * hh:64 * hh + 64, k, :], qkv_sl(vTs, hh, k),
                   "bank2", ["woutT", vTn], tp=(64 * hh, 64 * hh))

        def softmax(bkidx, dst_ap, dstname):
            bkn, bk = f"bank{bkidx}", bank[bkidx]
            vop(lambda e, o=mx_s[:, :].unsqueeze(2),
                i_=bk[0:128, 0:512].rearrange("p (c n) -> p c n", n=CH):
                    e.reduce_max(o, i_, axis=AX.X), r=[bkn], w=["mx_s"], hz=True)
            mxb = bass.AP(mx_s, 0, [[8, 128], [1, 8], [0, CH]])
            vop(lambda e, o=zsub[:, :].rearrange("p (c n) -> p c n", n=CH),
                i0=bk[0:128, 0:512].rearrange("p (c n) -> p c n", n=CH), i1=mxb:
                    e.tensor_tensor(o, i0, i1, op=ALU.subtract),
                r=[bkn, "mx_s"], w=["zsub"])
            sop(lambda e: e.activation(expt[:, :], zsub[:, :], ACT.Exp),
                r=["zsub"], w=["expt"])
            vop(lambda e, o=sm_s[:, :].unsqueeze(2),
                i_=expt[:, :].rearrange("p (c n) -> p c n", n=CH):
                    e.reduce_sum(o, i_, axis=AX.X), r=["expt"], w=["sm_s"], hz=True)
            vop(lambda e: e.reciprocal(rc_s[:, :], sm_s[:, :]), r=["sm_s"], w=["rc_s"], hz=True)
            rcb = bass.AP(rc_s, 0, [[8, 128], [1, 8], [0, CH]])
            vop(lambda e, o=dst_ap.rearrange("p (c n) -> p c n", n=CH),
                i0=expt[:, :].rearrange("p (c n) -> p c n", n=CH), i1=rcb:
                    e.tensor_tensor(o, i0, i1, op=ALU.mult),
                r=["expt", "rc_s"], w=[dstname])

        softmax(0, pT[:, :], "pT")
        vop(lambda e: e.tensor_tensor(m_s[:, :], pT[:, :], bank[2][0:128, 0:512], op=ALU.mult),
            r=["pT", "bank2"], w=["m_s"])
        mm(bank[3][0:2, 0:512], ones2col[:, :], m_s[:, :], "bank3", ["ones2col", "m_s"])
        vop(lambda e: e.tensor_copy(s2s[:, :], bank[3][0:2, 0:512]), r=["bank3"], w=["s2s"])
        mm(bank[4][0:128, 0:512], halfsel[:, :], s2s[:, :], "bank4", ["halfsel", "s2s"])
        vop(lambda e: e.tensor_tensor(t1_s[:, :], pT[:, :], bank[4][0:128, 0:512], op=ALU.mult),
            r=["pT", "bank4"], w=["t1_s"])
        vop(lambda e: e.tensor_tensor(hh_s[:, :], m_s[:, :], t1_s[:, :], op=ALU.subtract),
            r=["m_s", "t1_s"], w=["hh_s"])
        lrin_b = bass.AP(bc_c[bsl], 0, [[32, 128], [1, 8], [0, CH]])
        lrout_b = bass.AP(bc_c[bsl], 8, [[32, 128], [1, 8], [0, CH]])
        vop(lambda e, i1=lrin_b: e.tensor_tensor(
                hh_s[:, :].rearrange("p (c n) -> p c n", n=CH),
                hh_s[:, :].rearrange("p (c n) -> p c n", n=CH), i1, op=ALU.mult),
            r=["hh_s", bcn], w=["hh_s"])
        vop(lambda e, i1=lrout_b: e.tensor_tensor(
                ph_s[:, :].rearrange("p (c n) -> p c n", n=CH),
                pT[:, :].rearrange("p (c n) -> p c n", n=CH), i1, op=ALU.mult),
            r=["pT", bcn], w=["ph_s"])
        for j in range(4):
            transpose_q(3, j, ph_s[:, 128 * j:128 * j + 128], ["ph_s"])
        vop(lambda e: e.tensor_copy(ptok[:, :, :],
                                    bank[3][0:128, 0:512].rearrange("p (a b) -> p a b", b=128)),
            r=[f"bank3q{j}" for j in range(4)], w=["ptok"])
        for j in range(4):
            transpose_q(4, j, hh_s[:, 128 * j:128 * j + 128], ["hh_s"])
        sop(lambda e: _copy(e, htok[:, :, :],
                            bank[4][0:128, 0:512].rearrange("p (a b) -> p a b", b=128)),
            r=[f"bank4q{j}" for j in range(4)], w=["htok"])
        softmax(1, pqT[:, :], "pqT")
        for k in range(8):
            par, j = k % 2, k // 2
            for hh in range(2):
                ksl = ktok[64 * par:64 * par + 64, j, 64 * hh:64 * hh + 64]
                vsl = vtok[64 * par:64 * par + 64, j, 64 * hh:64 * hh + 64]
                psl = ptok[64 * par:64 * par + 64, j, 64 * hh:64 * hh + 64]
                hsl = htok[64 * par:64 * par + 64, j, 64 * hh:64 * hh + 64]
                mm(bank[5][64 * hh:64 * hh + 64, 64 * k:64 * k + 64], ksl, hsl,
                   "bank5", ["ktok", "htok"], tp=(64 * par, 64 * hh))
                mm(bank[6][64 * hh:64 * hh + 64, 64 * k:64 * k + 64], psl, vsl,
                   "bank6", ["ptok", "vtok"], tp=(64 * par, 64 * hh))
                mm(bank[7][64 * hh:64 * hh + 64, 64 * k:64 * k + 64], vsl, psl,
                   "bank7", ["vtok", "ptok"], tp=(64 * par, 64 * hh))
        for k in range(8):
            for hh in range(2):
                mm(bank[1][64 * hh:64 * hh + 64, 64 * k:64 * k + 64],
                   wout[64 * hh:64 * hh + 64, k, :],
                   pqT[64 * hh:64 * hh + 64, 64 * k:64 * k + 64], "bank1",
                   ["wout", "pqT"], tp=(64 * hh, 64 * hh))
        vop(lambda e, o=ostg[:, :, slice(CH * sub, CH * (sub + 1))],
            i_=bank[1][0:128, 0:512].rearrange("p (a b) -> p a b", b=CH):
                e.tensor_copy(o, i_), r=["bank1"], w=["ostg"])
        wdin_b = bass.AP(bc_c[bsl], 16, [[32, 128], [1, 8], [0, 64]])
        wdout_b = bass.AP(bc_c[bsl], 24, [[32, 128], [1, 8], [0, 64]])
        for st_t, st_n, bkidx, wd_ap in [(winT, "winT", 5, wdin_b),
                                         (wout, "wout", 6, wdout_b),
                                         (woutT, "woutT", 7, wdout_b)]:
            gop(lambda e, o=st_t[:, :, :], i1=wd_ap:
                    e.tensor_tensor(o, o, i1, op=ALU.mult),
                r=[st_n, bcn], w=[st_n])
            vop(lambda e, o=st_t[:, :, :].rearrange("p a b -> p (a b)"), bk=bank[bkidx]:
                    e.tensor_tensor(o, bk[0:128, 0:512], o, op=ALU.add),
                r=[f"bank{bkidx}", st_n], w=[st_n])
        pump(8)
        if sub == CPG - 1:
            p.op("gpsimd", lambda e, o=oscr[:, :, G * g:G * (g + 1)], i=ostg[:, :, :]:
                    e.dma_start(out=o, in_=i),
                 reads=[bf("ostg")], writes=[bf("oscr")], dma=True, sem="g_oscr")

    pending = []

    def pump(n):
        done = 0
        while pending and done < n:
            try:
                next(pending[0])
                done += 1
            except StopIteration:
                pending.pop(0)

    for g in range(min(2, n_groups)):
        for _ in proj_group_gen(g):
            pass
    for c in range(n_chunks):
        g = c // CPG
        if c % CPG == 0:
            while pending:
                pump(100)
        scan_chunk(c, pump)
        if c % CPG == CPG - 1 and g + 2 < n_groups:
            pending.append(proj_group_gen(g + 2))
    while pending:
        pump(100)

    tapspec = {
        "kT0": (kT[0][:, :, :], "kT0"), "vT0": (vT[0][:, :, :], "vT0"),
        "qT0": (qT[0][:, :, :], "qT0"), "lrwd_c": (lrwd_c[:, :], "lrwd_c"),
        "flat2": (flat2[:, :], "flat2"), "winT": (winT[:, :, :], "winT"),
        "wout": (wout[:, :, :], "wout"), "woutT": (woutT[:, :, :], "woutT"),
        "ostg": (ostg[:, :, :], "ostg"), "pqT": (pqT[:, :], "pqT"),
        "pT": (pT[:, :], "pT"), "bc_c0": (bc_c[0][:, :], "bc_c0"),
        "xT": (xT[:, :, :], "xT"),
        "dbg1": (dbg1[:, :], "dbg1"), "dbg2": (dbg2[:, :], "dbg2"),
        "dbg3": (dbg3[:, :], "dbg3"),
    }
    tap_outs = {}
    if "oscr" in taps:
        td = nc.dram_tensor("tap_oscr", [128, 8, L], F32, kind="ExternalOutput").ap()
        tap_outs["oscr"] = (128, 8, L)
        p.op("gpsimd", lambda e, o=td, i=oscr[:, :, :]: e.dma_start(out=o, in_=i),
             reads=[bf("oscr")], dma=True, sem="g_out")
    for name in taps:
        if name not in tapspec:
            continue
        ap, bufname = tapspec[name]
        shp = list(ap.shape)
        td = nc.dram_tensor("tap_" + name, shp, F32, kind="ExternalOutput").ap()
        tap_outs[name] = tuple(shp)
        p.op("gpsimd", lambda e, o=td, i=ap: e.dma_start(out=o, in_=i),
             reads=[bf(bufname)], dma=True, sem="g_out")

    if do_post:
        last_by_sem = {}
        for o in p.ops:
            last_by_sem[o.sem] = o
        p.barrier = list(last_by_sem.values())
        esA.close()

    if do_post:
        import contextlib as _ctx
        esB1 = _ctx.ExitStack()
        sb = lambda name, shape: es.enter_context(nc.sbuf_tensor(name, list(shape), F32))
        sb1 = lambda name, shape: esB1.enter_context(nc.sbuf_tensor(name, list(shape), F32))
        x1T = sb("x1T", (128, 8, TOWN))
        acc = sb("acc", (128, NT, DIM))
        comb = sb("comb", (128, NT, 8))
        combT = sb("combT", (8, NT, 128))
        r_s = sb("r_s", (128, DIM))
        s1a = sb("s1a", (128, 1))
        s1b = sb("s1b", (128, 1))
        s2a = sb("s2a", (128, 1))
        s2b = sb("s2b", (128, 1))
        mva = sb("mva", (128, 1))
        rst = sb("rst", (128, 1))
        g8 = sb("g8", (128, 8))
        gsc = sb("gsc", (128, 8))
        gm1 = sb("gm1", (128, 1))
        ge1 = sb("ge1", (128, 8))
        gs1 = sb("gs1", (128, 8))
        gf1 = sb("gf1", (128, 8))
        gp2 = sb("gp2", (128, 8))
        gmask = sb("gmask", (128, 8))
        gw_s = sb("gw_s", (128, 8, 8))
        outt = sb("outt", (128, DIM))
        dbgB = sb("dbgB", (128, 2 * DIM))
        woT_s = sb1("woT_s", (128, 8, DIM))
        otA = [sb1("otA0", (128, 8, 128))] * 2
        otB = [sb1("otB0", (128, 8, 128))] * 2
        otS = sb1("otS", (128, 8, 128))
        xres = [sb1("xres0", (128, DIM))] * 2

        dma_in("woT_s", woT_s[:], woT_d.rearrange("(k p) f -> p k f", p=128))
        dma_in("gw_s", gw_s[:], gw_d.rearrange("(k p) e -> p k e", p=128))


        def post_tile(t):
            slot = t % 2
            dma_in("otA0", otA[slot][:], oscr[:, :, 128 * t:128 * t + 128],
                   sem="s_otA0")
            dma_in("otB0", otB[slot][:], oscr[:, :, TOWN + 128 * t:TOWN + 128 * t + 128],
                   sem="s_otB0")
            dma_in("xres0", xres[slot][:], xown_d[128 * t:128 * t + 128, :],
                   sem="s_xres0")
            vop(lambda e, i0=otA[slot][:, :, :].rearrange("p a b -> p (a b)"),
                sc=msel[:, 1:2]:
                    e.tensor_scalar(otS[:, :, :].rearrange("p a b -> p (a b)"), i0, sc, None,
                                    op0=ALU.mult),
                r=["otA0", "msel"], w=["otS"])
            vop(lambda e, i0=otB[slot][:, :, :].rearrange("p a b -> p (a b)"),
                sc=msel[:, 0:1]:
                    e.scalar_tensor_tensor(otS[:, :, :].rearrange("p a b -> p (a b)"),
                                           i0, sc, otS[:, :, :].rearrange("p a b -> p (a b)"),
                                           op0=ALU.mult, op1=ALU.add),
                r=["otB0", "msel", "otS"], w=["otS"])
            for fh in range(2):
                for oc in range(8):
                    mm(bank[fh][0:128, 0:512], otS[:, oc, :],
                       woT_s[:, oc, 512 * fh:512 * fh + 512], f"bank{fh}",
                       ["otS", "woT_s"], start=(oc == 0), stop=(oc == 7))

            for fh in range(2):
                vop(lambda e, o=r_s[:, 512 * fh:512 * fh + 512], i0=bank[fh][0:128, 0:512],
                    i1=xres[slot][:, 512 * fh:512 * fh + 512]:
                        e.tensor_tensor(o, i0, i1, op=ALU.add),
                    r=[f"bank{fh}", "xres0"], w=["r_s"])
            vop(lambda e: e.tensor_reduce(s1a[:, :], r_s[:, :], axis=AX.X, op=ALU.add),
                r=["r_s"], w=["s1a"], hz=True)
            vop(lambda e: e.tensor_scalar(mva[:, :], s1a[:, :], -1.0 / DIM, None, op0=ALU.mult),
                r=["s1a"], w=["mva"])
            for fh in range(2):
                nm = "s2a" if fh == 0 else "s2b"
                a_out = s2a if fh == 0 else s2b
                sop(lambda e, i=r_s[:, 512 * fh:512 * fh + 512], a=a_out:
                        e.activation(bank[6][0:128, 0:512], i, ACT.Square,
                                     bias=mva[:, :], accum_out=a[:, :]),
                    r=["r_s", "mva"], w=["bank6", nm], hz=True)
            vop(lambda e: e.tensor_tensor(s2a[:, :], s2a[:, :], s2b[:, :], op=ALU.add),
                r=["s2a", "s2b"], w=["s2a"])
            sop(lambda e: e.activation(rst[:, :], s2a[:, :], ACT.Sqrt,
                                       bias=float(EPS_LN), scale=1.0 / DIM),
                r=["s2a", "epsv"], w=["rst"])
            vop(lambda e: e.reciprocal(rst[:, :], rst[:, :]), r=["rst"], w=["rst"], hz=True)
            vop(lambda e, o=acc[:, t, :]:
                    e.tensor_scalar(o, r_s[:, :], mva[:, :], rst[:, :],
                                    op0=ALU.add, op1=ALU.mult),
                r=["r_s", "mva", "rst"], w=["acc"])
            for dk in range(8):
                bkn = f"bank{2 + dk % 2}"
                bk = bank[2 + dk % 2]
                col = 128 * ((dk // 2) % 2)
                transpose(bk[0:128, col:col + 128], acc[:, t, 128 * dk:128 * dk + 128],
                          bkn, ["acc"])
                vop(lambda e, o=x1T[:, dk, 128 * t:128 * t + 128], i_=bk[0:128, col:col + 128]:
                        e.tensor_copy(o, i_), r=[bkn], w=["x1T"])
            for dk in range(8):
                mm(bank[4][0:128, 0:8], x1T[:, dk, 128 * t:128 * t + 128], gw_s[:, dk, :],
                   "bank4", ["x1T", "gw_s"], start=(dk == 0), stop=False)
            mm(bank[4][0:128, 0:8], onesrow[:, :], gb_s[:, :], "bank4",
               ["onesrow", "gb_s"], start=False, stop=True)
            vop(lambda e: e.tensor_scalar(gsc[:, :], bank[4][0:128, 0:8], float(GEPS), None,
                                          op0=ALU.max), r=["bank4"], w=["gsc"])
            vop(lambda e: e.reduce_max(gm1[:, :], gsc[:, :], axis=AX.X), r=["gsc"], w=["gm1"], hz=True)
            vop(lambda e: e.tensor_scalar(gm1[:, :], gm1[:, :], -1.0, None, op0=ALU.mult),
                r=["gm1"], w=["gm1"])
            sop(lambda e: e.activation(g8[:, :], gsc[:, :], ACT.Exp, bias=gm1[:, :]),
                r=["gsc", "gm1"], w=["g8"])
            vop(lambda e: e.reduce_sum(gm1[:, :], g8[:, :], axis=AX.X), r=["g8"], w=["gm1"], hz=True)
            vop(lambda e: e.reciprocal(gm1[:, :], gm1[:, :]), r=["gm1"], w=["gm1"], hz=True)
            vop(lambda e: e.tensor_scalar(g8[:, :], g8[:, :], gm1[:, :], None, op0=ALU.mult),
                r=["g8", "gm1"], w=["g8"])
            vop(lambda e: e.reduce_max(gm1[:, :], g8[:, :], axis=AX.X), r=["g8"], w=["gm1"], hz=True)
            gm1b = bass.AP(gm1, 0, [[1, 128], [1, 1], [0, 8]])
            vop(lambda e, i1=gm1b: e.tensor_tensor(ge1[:, :].unsqueeze(1),
                g8[:, :].unsqueeze(1), i1, op=ALU.is_ge), r=["g8", "gm1"], w=["ge1"])
            vop(lambda e: e.tensor_tensor(gs1[:, :], ge1[:, :], pow2[:, :], op=ALU.mult),
                r=["ge1", "pow2"], w=["gs1"])
            vop(lambda e: e.reduce_max(gm1[:, :], gs1[:, :], axis=AX.X), r=["gs1"], w=["gm1"], hz=True)
            gm1b2 = bass.AP(gm1, 0, [[1, 128], [1, 1], [0, 8]])
            vop(lambda e, i1=gm1b2: e.tensor_tensor(gf1[:, :].unsqueeze(1),
                gs1[:, :].unsqueeze(1), i1, op=ALU.is_ge), r=["gs1", "gm1"], w=["gf1"])
            vop(lambda e: e.scalar_tensor_tensor(gp2[:, :], gf1[:, :], -2.0, g8[:, :],
                                                 op0=ALU.mult, op1=ALU.add),
                r=["gf1", "g8"], w=["gp2"])
            vop(lambda e: e.reduce_max(gm1[:, :], gp2[:, :], axis=AX.X), r=["gp2"], w=["gm1"], hz=True)
            gm1b3 = bass.AP(gm1, 0, [[1, 128], [1, 1], [0, 8]])
            vop(lambda e, i1=gm1b3: e.tensor_tensor(ge1[:, :].unsqueeze(1),
                gp2[:, :].unsqueeze(1), i1, op=ALU.is_ge), r=["gp2", "gm1"], w=["ge1"])
            vop(lambda e: e.tensor_tensor(gs1[:, :], ge1[:, :], pow2[:, :], op=ALU.mult),
                r=["ge1", "pow2"], w=["gs1"])
            vop(lambda e: e.reduce_max(gm1[:, :], gs1[:, :], axis=AX.X), r=["gs1"], w=["gm1"], hz=True)
            gm1b4 = bass.AP(gm1, 0, [[1, 128], [1, 1], [0, 8]])
            vop(lambda e, i1=gm1b4: e.tensor_tensor(ge1[:, :].unsqueeze(1),
                gs1[:, :].unsqueeze(1), i1, op=ALU.is_ge), r=["gs1", "gm1"], w=["ge1"])
            vop(lambda e: e.tensor_tensor(gmask[:, :], gf1[:, :], ge1[:, :], op=ALU.add),
                r=["gf1", "ge1"], w=["gmask"])
            vop(lambda e: e.tensor_scalar(gs1[:, :], g8[:, :], float(WTHR), None, op0=ALU.max),
                r=["g8"], w=["gs1"])
            vop(lambda e: e.tensor_tensor(gs1[:, :], gs1[:, :], gmask[:, :], op=ALU.mult),
                r=["gs1", "gmask"], w=["gs1"])
            vop(lambda e: e.reduce_sum(gm1[:, :], gs1[:, :], axis=AX.X), r=["gs1"], w=["gm1"], hz=True)
            vop(lambda e: e.reciprocal(gm1[:, :], gm1[:, :]), r=["gm1"], w=["gm1"], hz=True)
            vop(lambda e, o=comb[:, t, :]: e.tensor_scalar(o, gs1[:, :], gm1[:, :], 0.5,
                                                           op0=ALU.mult, op1=ALU.mult),
                r=["gs1", "gm1"], w=["comb"])
            if t == 0:
                vop(lambda e: e.tensor_copy(dbgB[:, 64:72], bank[4][0:128, 0:8]),
                    r=["bank4"], w=["dbgB"])
                vop(lambda e: e.tensor_copy(dbgB[:, 0:8], gsc[:, :]), r=["gsc"], w=["dbgB"])
                vop(lambda e: e.tensor_copy(dbgB[:, 8:16], g8[:, :]), r=["g8"], w=["dbgB"])
                vop(lambda e: e.tensor_copy(dbgB[:, 16:24], gf1[:, :]), r=["gf1"], w=["dbgB"])
                vop(lambda e: e.tensor_copy(dbgB[:, 24:32], ge1[:, :]), r=["ge1"], w=["dbgB"])
                vop(lambda e: e.tensor_copy(dbgB[:, 32:40], gmask[:, :]), r=["gmask"], w=["dbgB"])
                vop(lambda e: e.tensor_copy(dbgB[:, 40:48], gs1[:, :]), r=["gs1"], w=["dbgB"])
                vop(lambda e: e.tensor_copy(dbgB[:, 48:49], gm1[:, :]), r=["gm1"], w=["dbgB"])
            transpose(bank[5][0:8, 0:128], comb[:, t, :], "bank5", ["comb"])
            vop(lambda e, o=combT[:, t, :]: e.tensor_copy(o, bank[5][0:8, 0:128]),
                r=["bank5"], w=["combT"])
            for fh in range(2):
                mm(bank[6][0:128, 0:512], combT[:, t, :], eb_s[:, 512 * fh:512 * fh + 512],
                   "bank6", ["combT", "eb_s"])
                vop(lambda e, o=acc[:, t, 512 * fh:512 * fh + 512], i0=bank[6][0:128, 0:512]:
                        e.tensor_tensor(o, i0, o, op=ALU.add),
                    r=["bank6", "acc"], w=["acc"])

        for t in range(NT):
            post_tile(t)

        last_by_sem = {}
        for o in p.ops:
            last_by_sem[o.sem] = o
        p.barrier = list(last_by_sem.values())
        p.barrier_applied = set()
        esB1.close()
        ew_s = [sb(f"ew_s{i}", (128, 8, 512)) for i in range(2)]

        for e_i in range(NEXP):
            for fh in range(2):
                slot = (e_i * 2 + fh) % 2
                dma_in(f"ew_s{slot}", ew_s[slot][:],
                       ew_d[e_i, :, 512 * fh:512 * fh + 512].rearrange("(k p) f -> p k f", p=128),
                       sem=f"s_ew{slot}")
                for t in range(NT):
                    bkn = f"bank{t % 4}"
                    bk = bank[t % 4]
                    for dk in range(8):
                        mm(bk[0:128, 0:512], x1T[:, dk, 128 * t:128 * t + 128],
                           ew_s[slot][:, dk, :], bkn, ["x1T", f"ew_s{slot}"],
                           start=(dk == 0), stop=(dk == 7))
                    vop(lambda e, o=acc[:, t, 512 * fh:512 * fh + 512], i0=bk[0:128, 0:512],
                        sc=comb[:, t, e_i:e_i + 1]:
                            e.scalar_tensor_tensor(o, i0, sc, o, op0=ALU.mult, op1=ALU.add),
                        r=[bkn, "comb", "acc"], w=["acc"])

        for t in range(NT):
            for fh in range(2):
                nm = "s1a" if fh == 0 else "s1b"
                a_out = s1a if fh == 0 else s1b
                vop(lambda e, i=acc[:, t, 512 * fh:512 * fh + 512], a=a_out:
                        e.tensor_reduce(a[:, :], i, axis=AX.X, op=ALU.add),
                    r=["acc"], w=[nm], hz=True)
            vop(lambda e: e.tensor_tensor(s1a[:, :], s1a[:, :], s1b[:, :], op=ALU.add),
                r=["s1a", "s1b"], w=["s1a"])
            vop(lambda e: e.tensor_scalar(mva[:, :], s1a[:, :], -1.0 / DIM, None, op0=ALU.mult),
                r=["s1a"], w=["mva"])
            for fh in range(2):
                nm = "s2a" if fh == 0 else "s2b"
                a_out = s2a if fh == 0 else s2b
                sop(lambda e, i=acc[:, t, 512 * fh:512 * fh + 512], a=a_out:
                        e.activation(bank[6][0:128, 0:512], i, ACT.Square,
                                     bias=mva[:, :], accum_out=a[:, :]),
                    r=["acc", "mva"], w=["bank6", nm], hz=True)
            vop(lambda e: e.tensor_tensor(s2a[:, :], s2a[:, :], s2b[:, :], op=ALU.add),
                r=["s2a", "s2b"], w=["s2a"])
            sop(lambda e: e.activation(rst[:, :], s2a[:, :], ACT.Sqrt,
                                       bias=float(EPS_LN), scale=1.0 / DIM),
                r=["s2a", "epsv"], w=["rst"])
            vop(lambda e: e.reciprocal(rst[:, :], rst[:, :]), r=["rst"], w=["rst"], hz=True)
            vop(lambda e, o=outt[:, :], i=acc[:, t, :]:
                    e.tensor_scalar(o, i, mva[:, :], rst[:, :], op0=ALU.add, op1=ALU.mult),
                r=["acc", "mva", "rst"], w=["outt"])
            p.op("gpsimd", lambda e, o=out_d[128 * t:128 * t + 128, :], i=outt[:, :]:
                    e.dma_start(out=o, in_=i),
                 reads=[bf("outt")], writes=[bf("out_d")], dma=True, sem="g_out")

    if do_post:
        tapspecB = {"acc": (acc[:, :, :], "acc"), "comb": (comb[:, :, :], "comb"),
                    "x1T": (x1T[:, :, :], "x1T"), "dbgB": (dbgB[:, :], "dbgB")}
        for name in taps:
            if name not in tapspecB:
                continue
            ap, bufname = tapspecB[name]
            shp = list(ap.shape)
            td = nc.dram_tensor("tap_" + name, shp, F32, kind="ExternalOutput").ap()
            tap_outs[name] = tuple(shp)
            p.op("gpsimd", lambda e, o=td, i=ap: e.dma_start(out=o, in_=i),
                 reads=[bf(bufname)], dma=True, sem="g_out")

    final_counts = p.plan()
    sems = {s: es.enter_context(nc.semaphore("sem_" + s)) for s in final_counts}

    with nc.Block() as block:
        @block.sync
        def _(eng):
            p.emit("sync", eng, sems)
            for s2 in sorted(final_counts):
                if s2 != "sync" and final_counts[s2]:
                    eng.wait_ge(sems[s2], final_counts[s2])

        @block.tensor
        def _(eng):
            p.emit("tensor", eng, sems)

        @block.vector
        def _(eng):
            p.emit("vector", eng, sems)

        @block.scalar
        def _(eng):
            p.emit("scalar", eng, sems)

        @block.gpsimd
        def _(eng):
            p.emit("gpsimd", eng, sems)

    if not do_post:
        esA.close()
    es.close()
    return nc, tap_outs


def make_in_maps(inputs):
    prep = host_prep(inputs)
    x = np.ascontiguousarray(np.asarray(inputs["L0_attn_x"], dtype=np.float32))
    in_maps = []
    for c in range(8):
        b, half = c // 2, c % 2
        ms = np.zeros((128, 2), np.float32)
        ms[:, 0] = float(half)
        ms[:, 1] = 1.0 - float(half)
        m = {"x": np.ascontiguousarray(x[b]),
             "x_own": np.ascontiguousarray(x[b, TOWN * half:TOWN * (half + 1), :]),
             "mselw": ms}
        m.update(prep)
        in_maps.append(m)
    return in_maps


def kernel(**inputs):
    in_maps = make_in_maps(inputs)
    nc, _ = build_core()
    mybir.codegen_inst_isa_subclasses(nc)
    res = run_bass_kernel_spmd(nc, in_maps, core_ids=list(range(8)))
    out = np.empty((B, L, DIM), np.float32)
    for c in range(8):
        b, half = c // 2, c % 2
        out[b, TOWN * half:TOWN * (half + 1), :] = np.asarray(res.results[c]["out"])
    return out


# version 08
# speedup vs baseline: 1.1829x; 1.0227x over previous
import numpy as np
import concourse.bass as bass
import concourse.mybir as mybir
from concourse.bass_utils import run_bass_kernel_spmd

F32 = mybir.dt.float32
AX = mybir.AxisListType
ALU = mybir.AluOpType
ACT = mybir.ActivationFunctionType

B, L, DIM = 4, 4096, 1024
CH = 64
NCH = L // CH
G = 256
NG = L // G
CPG = G // CH
NEXP = 8
TOWN = L // 2
NT = TOWN // 128
EPS_LN = 1e-5
BASE_LR, BASE_WD = 1e-3, 0.9
GEPS, WTHR = 1e-4, 0.1
SCAN_STAGE = 6


class _Op:
    __slots__ = ("eng", "emit", "deps", "inc", "post", "sem", "hz")

    def __init__(self, eng, emit, deps, inc, sem, hz):
        self.eng, self.emit, self.deps, self.inc, self.sem = eng, emit, deps, inc, sem
        self.hz = hz
        self.post = None


class Buf:
    __slots__ = ("writer", "readers")

    def __init__(self):
        self.writer = None
        self.readers = []


class Prog:
    ENGS = ("sync", "tensor", "vector", "scalar", "gpsimd")

    def __init__(self):
        self.ops = []
        self.barrier = None
        self.barrier_applied = set()

    def op(self, eng, emit, reads=(), writes=(), deps=(), dma=False, sem=None, hz=False):
        dd = list(deps)
        if self.barrier is not None and eng not in self.barrier_applied:
            dd.extend(self.barrier)
            self.barrier_applied.add(eng)
        for b in reads:
            if b.writer is not None:
                dd.append(b.writer)
        for b in writes:
            dd.extend(b.readers)
            if b.writer is not None:
                dd.append(b.writer)
        assert (sem is not None) == dma, "DMA ops need an explicit slot sem"
        o = _Op(eng, emit, dd, 16 if dma else 1, sem if dma else eng, hz)
        self.ops.append(o)
        for b in reads:
            b.readers.append(o)
        for b in writes:
            b.writer = o
            b.readers = []
        return o

    def plan(self):
        cnt = {}
        for o in self.ops:
            cnt[o.sem] = cnt.get(o.sem, 0) + o.inc
            o.post = cnt[o.sem]
        return cnt

    def _walk(self, eng_name):
        watermark = {}
        for o in self.ops:
            if o.eng != eng_name:
                continue
            best = {}
            for d in o.deps:
                if (d.eng == eng_name == "tensor" and o.eng == "tensor"
                        and d.sem == "tensor" and not d.hz):
                    continue
                if best.get(d.sem, -1) < d.post:
                    best[d.sem] = d.post
            waits = []
            for s2 in sorted(best):
                v = best[s2]
                if watermark.get(s2, 0) >= v:
                    continue
                waits.append((s2, v))
                watermark[s2] = v
            yield o, waits

    def needed_posts(self, final_counts):
        need = set()
        for e in self.ENGS:
            for o, waits in self._walk(e):
                for s2, v in waits:
                    need.add((s2, v))
        for s2, v in final_counts.items():
            need.add((s2, v))
        return need

    def emit(self, eng_name, eng, sems, need):
        pending = 0
        for o, waits in self._walk(eng_name):
            for s2, v in waits:
                eng.wait_ge(sems[s2], v)
            inst = o.emit(eng)
            if o.inc == 16:
                inst.then_inc(sems[o.sem], 16)
            else:
                pending += o.inc
                if (o.sem, o.post) in need:
                    inst.then_inc(sems[o.sem], pending)
                    pending = 0


def _head_perm():
    idx = np.empty(DIM, np.int64)
    for oc in range(8):
        for c2 in range(2):
            h = oc + 8 * c2
            idx[oc * 128 + c2 * 64: oc * 128 + c2 * 64 + 64] = np.arange(h * 64, h * 64 + 64)
    return idx


def host_prep(inp):
    idx = _head_perm()
    f32c = lambda a: np.ascontiguousarray(np.asarray(a), dtype=np.float32)
    d = {}
    d["wqT"] = f32c(np.asarray(inp["L0_attn_Wq"]).T[:, idx])
    d["wkT"] = f32c(np.asarray(inp["L0_attn_Wk"]).T[:, idx])
    d["wvT"] = f32c(np.asarray(inp["L0_attn_Wv"]).T[:, idx])
    d["wlbT"] = f32c(np.concatenate([np.asarray(inp["L0_attn_Wlr"]).T,
                                     np.asarray(inp["L0_attn_Wbeta"]).T], axis=1))
    d["woT"] = f32c(np.asarray(inp["L0_attn_Wo"]).T[idx, :])
    wi = np.asarray(inp["L0_attn_W_in_init"])[0]
    wo = np.asarray(inp["L0_attn_W_out_init"])[0]

    def compact(arr_hab):
        a = np.asarray(arr_hab).reshape(2, 8, 64, 64)
        return f32c(a.transpose(0, 2, 1, 3).reshape(128, 8, 64))

    d["winT0"] = compact(wi.transpose(1, 2, 0))
    d["wout0"] = compact(wo.transpose(1, 0, 2))
    d["woutT0"] = compact(wo.transpose(1, 2, 0))
    d["gw"] = f32c(inp["L0_ffn_gate_W"])
    d["gb"] = f32c(np.asarray(inp["L0_ffn_gate_b"]).reshape(1, 8))
    d["ew"] = f32c(inp["L0_ffn_expert_W"])
    d["eb"] = f32c(inp["L0_ffn_expert_b"])
    d["identw"] = np.eye(128, dtype=np.float32)
    oc2 = np.zeros((128, 2), np.float32)
    oc2[:64, 0] = 1.0
    oc2[64:, 1] = 1.0
    d["ones2colw"] = oc2
    hs = np.zeros((2, 128), np.float32)
    hs[0, :64] = 1.0
    hs[1, 64:] = 1.0
    d["halfselw"] = hs
    d["onesroww"] = np.ones((1, 128), np.float32)
    d["pow2w"] = np.ascontiguousarray(
        np.broadcast_to(2.0 ** -np.arange(8, dtype=np.float32), (128, 8)))
    c64 = np.empty((64, 1), np.float32)
    c64[:32] = BASE_LR / 64.0
    c64[32:] = BASE_WD / 64.0
    d["c64w"] = c64
    d["epsw"] = np.full((128, 1), EPS_LN, np.float32)
    return d


def build_core(n_groups=NG, n_chunks=NCH, do_post=True, taps=()):
    nc = bass.Bass("TRN2", target_bir_lowering=False, detect_race_conditions=False)
    p = Prog()
    bufs = {}

    def bf(name):
        if name not in bufs:
            bufs[name] = Buf()
        return bufs[name]

    def dram_in(name, shape):
        return nc.dram_tensor(name, list(shape), F32, kind="ExternalInput").ap()

    xb = dram_in("x", (L, DIM))
    xown_d = dram_in("x_own", (TOWN, DIM))
    msel_d = dram_in("mselw", (128, 2))
    wqT = dram_in("wqT", (DIM, DIM))
    wkT = dram_in("wkT", (DIM, DIM))
    wvT = dram_in("wvT", (DIM, DIM))
    wlbT_d = dram_in("wlbT", (DIM, 64))
    woT_d = dram_in("woT", (DIM, DIM))
    winT0 = dram_in("winT0", (128, 8, 64))
    wout0 = dram_in("wout0", (128, 8, 64))
    woutT0 = dram_in("woutT0", (128, 8, 64))
    gw_d = dram_in("gw", (DIM, 8))
    gb_d = dram_in("gb", (1, 8))
    ew_d = dram_in("ew", (NEXP, DIM, DIM))
    eb_d = dram_in("eb", (NEXP, DIM))
    ident_d = dram_in("identw", (128, 128))
    ones2col_d = dram_in("ones2colw", (128, 2))
    halfsel_d = dram_in("halfselw", (2, 128))
    onesrow_d = dram_in("onesroww", (1, 128))
    pow2_d = dram_in("pow2w", (128, 8))
    c64_d = dram_in("c64w", (64, 1))
    eps_d = dram_in("epsw", (128, 1))

    out_d = nc.dram_tensor("out", [TOWN, DIM], F32, kind="ExternalOutput").ap()
    oscr = nc.dram_tensor("oscr", [128, 8, L], F32, kind="Internal").ap()

    import contextlib
    es = contextlib.ExitStack()
    esA = contextlib.ExitStack()
    sbP = lambda name, shape: es.enter_context(nc.sbuf_tensor(name, list(shape), F32))
    sb = lambda name, shape: esA.enter_context(nc.sbuf_tensor(name, list(shape), F32))
    psum = lambda name, shape: es.enter_context(nc.psum_tensor(name, list(shape), F32))

    ident = sbP("ident", (128, 128))
    onesrow = sbP("onesrow", (1, 128))
    pow2 = sbP("pow2", (128, 8))
    gb_s = sbP("gb_s", (1, 8))
    eb_s = sbP("eb_s", (8, DIM))
    msel = sbP("msel", (128, 2))
    epsv = sbP("epsv", (128, 1))
    nc.const_aps.register(F32, float(EPS_LN), epsv[:, 0:1])

    wq_s = sb("wq_s", (128, 8, DIM))
    wk_s = sb("wk_s", (128, 8, DIM))
    wlb_s = sb("wlb_s", (128, 8, 64))
    wv_s = [sb(f"wv_s{i}", (128, 8, 128)) for i in range(2)]
    ones2col = sb("ones2col", (128, 2))
    halfsel = sb("halfsel", (2, 128))
    c64 = sb("c64", (64, 1))

    winT = sb("winT", (128, 8, 64))
    wout = sb("wout", (128, 8, 64))
    woutT = sb("woutT", (128, 8, 64))

    xrow = [sb(f"xrow{i}", (128, 2, DIM)) for i in range(2)]
    xT = sb("xT", (128, 8, G))
    qT = [sb(f"qT{i}", (128, CPG, 4, 128)) for i in range(2)]
    kT = [sb(f"kT{i}", (128, CPG, 4, 128)) for i in range(2)]
    vT = [sb(f"vT{i}", (128, CPG, 4, 128)) for i in range(2)]
    lrwdT = sb("lrwdT", (64, G))
    lrwd_c = sb("lrwd_c", (64, NCH))
    lrtmp = sb("lrtmp", (4, 64))
    flat2 = sb("flat2", (2, NCH * 32))
    bc_c = [sb(f"bc_c{i}", (128, 32)) for i in range(2)]

    pqT = sb("pqT", (128, 512))
    pT = sb("pT", (128, 512))
    zsub = sb("zsub", (128, 512))
    expt = sb("expt", (128, 512))
    m_s = sb("m_s", (128, 512))
    t1_s = sb("t1_s", (128, 512))
    hh_s = sb("hh_s", (128, 512))
    ph_s = sb("ph_s", (128, 512))
    s2s = sb("s2s", (2, 512))
    mx_s = sb("mx_s", (128, 8))
    sm_s = sb("sm_s", (128, 8))
    rc_s = sb("rc_s", (128, 8))
    ktok = sb("ktok", (128, 4, 128))
    vtok = sb("vtok", (128, 4, 128))
    ptok = sb("ptok", (128, 4, 128))
    htok = sb("htok", (128, 4, 128))
    ostg = sb("ostg", (128, 8, G))
    dbg1 = sb("dbg1", (128, 512))
    dbg2 = sb("dbg2", (128, 512))
    dbg3 = sb("dbg3", (128, 512))

    bank = [psum(f"bank{i}", (128, 512)) for i in range(8)]

    def dma_in(bufname, out_ap, in_ap, eng="sync", extra_reads=(), sem="s_init"):
        return p.op(eng, lambda e, o=out_ap, i=in_ap: e.dma_start(out=o, in_=i),
                    reads=list(extra_reads), writes=[bf(bufname)], dma=True, sem=sem)

    dma_in("wq_s", wq_s[:], wqT.rearrange("(k p) f -> p k f", p=128))
    dma_in("wk_s", wk_s[:], wkT.rearrange("(k p) f -> p k f", p=128))
    dma_in("wlb_s", wlb_s[:], wlbT_d.rearrange("(k p) f -> p k f", p=128))
    for name, s_t, d_t in [("ident", ident, ident_d), ("ones2col", ones2col, ones2col_d),
                           ("halfsel", halfsel, halfsel_d), ("onesrow", onesrow, onesrow_d),
                           ("pow2", pow2, pow2_d), ("c64", c64, c64_d),
                           ("gb_s", gb_s, gb_d), ("eb_s", eb_s, eb_d), ("msel", msel, msel_d),
                           ("epsv", epsv, eps_d)]:
        dma_in(name, s_t[:], d_t[:])
    dma_in("winT", winT[:], winT0[:])
    dma_in("wout", wout[:], wout0[:])
    dma_in("woutT", woutT[:], woutT0[:])

    def mm(out_ap, lhsT_ap, rhs_ap, wbankname, rbufnames, start=True, stop=True, tp=None):
        wr = [bf(wbankname)]
        if wbankname in ("bank3", "bank4", "bank5", "bank6"):
            wr += [bf(wbankname + f"q{j}") for j in range(4)]
        return p.op("tensor",
                    lambda e, o=out_ap, a=lhsT_ap, b=rhs_ap, s=start, t=stop, q=tp:
                        e.matmul(o, a, b, start=s, stop=t, tile_position=q),
                    reads=[bf(n) for n in rbufnames], writes=wr)

    def transpose(out_ap, in_ap, wbankname, rbufnames):
        np_ = in_ap.shape[0]
        return p.op("tensor",
                    lambda e, o=out_ap, i=in_ap, n_=np_: e.transpose(o, i, ident[0:n_, 0:n_]),
                    reads=[bf(n) for n in rbufnames] + [bf("ident")],
                    writes=[bf(wbankname)])

    def transpose_q(bkidx, j, in_ap, rbufnames):
        np_ = in_ap.shape[0]
        out_ap = bank[bkidx][0:128, 128 * j:128 * j + 128]
        return p.op("tensor",
                    lambda e, o=out_ap, i=in_ap, n_=np_: e.transpose(o, i, ident[0:n_, 0:n_]),
                    reads=[bf(n) for n in rbufnames] + [bf("ident"), bf(f"bank{bkidx}")],
                    writes=[bf(f"bank{bkidx}q{j}")])

    def bank_w(bkidx):
        return [bf(f"bank{bkidx}")] + [bf(f"bank{bkidx}q{j}") for j in range(4)]

    def vop(emit, r=(), w=(), hz=False):
        return p.op("vector", emit, reads=[bf(n) for n in r],
                    writes=[bf(n) for n in w], hz=hz)

    def sop(emit, r=(), w=(), hz=False):
        return p.op("scalar", emit, reads=[bf(n) for n in r],
                    writes=[bf(n) for n in w], hz=hz)

    def _copy(e, o, i_):
        return e.tensor_copy(o, i_) if hasattr(e, "tensor_copy") else e.copy(o, i_)

    def gop(emit, r=(), w=()):
        return p.op("gpsimd", emit, reads=[bf(n) for n in r], writes=[bf(n) for n in w])

    def proj_group_gen(g):
        slot = g % 2
        xr = xrow[slot]
        dma_in(f"xrow{slot}", xr[:], xb[G * g:G * (g + 1), :].rearrange("(t p) f -> p t f", p=128), sem=f"s_xrow{slot}")
        for quad in range(4):
            tt, dk0 = quad // 2, 4 * (quad % 2)
            bkidx = 3 + quad % 2
            for jj in range(4):
                transpose_q(bkidx, jj, xr[:, tt, 128 * (dk0 + jj):128 * (dk0 + jj) + 128],
                            [f"xrow{slot}"])
            cop = sop if (quad % 2) else vop
            xt_dst = bass.AP(xT, (4 * (quad % 2)) * G + 128 * tt,
                             [[8 * G, 128], [G, 4], [1, 128]])
            cop(lambda e, o=xt_dst, i_=bank[bkidx][0:128, 0:512]
                .rearrange("p (a b) -> p a b", b=128): _copy(e, o, i_),
                r=[f"bank{bkidx}q{j}" for j in range(4)], w=["xT"])
            yield

        def projmm(lhs_ap_fn, wname, dst_ap, dstname, act, bkidx):
            bkn, bk = f"bank{bkidx}", bank[bkidx]
            for dk in range(8):
                mm(bk[0:128, 0:G], lhs_ap_fn(dk), xT[:, dk, :], bkn, [wname, "xT"],
                   start=(dk == 0), stop=(dk == 7))
            src3 = bk[0:128, 0:G].rearrange("p (s n) -> p s n", n=CH)
            if act == "silu":
                sop(lambda e, o=dst_ap, i_=src3: e.activation(o, i_, ACT.Silu),
                    r=[bkn], w=[dstname])
            else:
                vop(lambda e, o=dst_ap, i_=src3: e.tensor_copy(o, i_),
                    r=[bkn], w=[dstname])

        def qkv_dst(tens, oc):
            return tens[:, 0:CPG, oc // 2, 64 * (oc % 2):64 * (oc % 2) + 64]

        for oc in range(8):
            projmm(lambda dk, oc=oc: wk_s[:, dk, 128 * oc:128 * oc + 128], "wk_s",
                   qkv_dst(kT[slot], oc), f"kT{slot}", "silu", 3 + oc % 2)
            yield
        for oc in range(8):
            dma_in(f"wv_s{oc % 2}", wv_s[oc % 2][:],
                   wvT.rearrange("(k p) f -> p k f", p=128)[:, :, 128 * oc:128 * oc + 128],
                   sem=f"s_wv{oc % 2}")
            projmm(lambda dk, oc=oc: wv_s[oc % 2][:, dk, :], f"wv_s{oc % 2}",
                   qkv_dst(vT[slot], oc), f"vT{slot}", "copy", 3 + oc % 2)
            yield
        for oc in range(8):
            projmm(lambda dk, oc=oc: wq_s[:, dk, 128 * oc:128 * oc + 128], "wq_s",
                   qkv_dst(qT[slot], oc), f"qT{slot}", "silu", 3 + oc % 2)
            yield
        bkn, bk = "bank3", bank[3]
        for dk in range(8):
            mm(bk[0:64, 0:G], wlb_s[:, dk, :], xT[:, dk, :], bkn, ["wlb_s", "xT"],
               start=(dk == 0), stop=(dk == 7))
        sop(lambda e, o=lrwdT[:, :], i_=bk[0:64, 0:G]: e.activation(o, i_, ACT.Sigmoid),
            r=[bkn], w=["lrwdT"])
        vop(lambda e, o=lrwd_c[:, CPG * g:CPG * (g + 1)].unsqueeze(2),
            i_=lrwdT[:, :].rearrange("p (c n) -> p c n", n=CH): e.reduce_sum(o, i_, axis=AX.X),
            r=["lrwdT"], w=["lrwd_c"], hz=True)
        vop(lambda e, o=lrwd_c[:, CPG * g:CPG * (g + 1)], i_=lrwd_c[:, CPG * g:CPG * (g + 1)]:
                e.tensor_scalar(o, i_, c64[:, :], None, op0=ALU.mult),
            r=["lrwd_c", "c64"], w=["lrwd_c"])
        transpose(bank[4][0:CPG, 0:64], lrwd_c[:, CPG * g:CPG * (g + 1)], "bank4", ["lrwd_c"])
        vop(lambda e, o=lrtmp[:, :], i_=bank[4][0:CPG, 0:64]: e.tensor_copy(o, i_),
            r=["bank4"], w=["lrtmp"])
        for hh in range(2):
            in_ap = bass.AP(lrtmp, 8 * hh, [[64, CPG], [16, 4], [1, 8]])
            out_ap = bass.AP(flat2, 32 * NCH * hh + 32 * CPG * g, [[32 * NCH, 1], [1, 32 * CPG]])
            p.op("gpsimd", lambda e, o=out_ap, i=in_ap: e.dma_start(out=o, in_=i),
                 reads=[bf("lrtmp")], writes=[bf("flat2")], dma=True, sem="g_flat")

    def scan_chunk(c, pump):
        g, sub = c // CPG, c % CPG
        slot = g % 2
        kTs, vTs, qTs = kT[slot], vT[slot], qT[slot]
        kTn, vTn, qTn = f"kT{slot}", f"vT{slot}", f"qT{slot}"
        bsl = c % 2
        bcn = f"bc_c{bsl}"

        def qkv_sl(tens, hh, k):
            return tens[64 * hh:64 * hh + 64, sub, k // 2, 64 * (k % 2):64 * (k % 2) + 64]

        mm(bank[7][0:128, 0:32], halfsel[:, :], flat2[0:2, 32 * c:32 * c + 32],
           "bank7", ["halfsel", "flat2"])
        vop(lambda e, o=bc_c[bsl][:, :], i_=bank[7][0:128, 0:32]: e.tensor_copy(o, i_),
            r=["bank7"], w=[bcn])

        for j in range(4):
            transpose_q(5, j, kTs[:, sub, j, :], [kTn])
        vop(lambda e: e.tensor_copy(ktok[:, :, :],
                                    bank[5][0:128, 0:512].rearrange("p (a b) -> p a b", b=128)),
            r=[f"bank5q{j}" for j in range(4)], w=["ktok"])
        for j in range(4):
            transpose_q(6, j, vTs[:, sub, j, :], [vTn])
        sop(lambda e: _copy(e, vtok[:, :, :],
                            bank[6][0:128, 0:512].rearrange("p (a b) -> p a b", b=128)),
            r=[f"bank6q{j}" for j in range(4)], w=["vtok"])

        for k in range(8):
            for hh in range(2):
                mm(bank[0][64 * hh:64 * hh + 64, 64 * k:64 * k + 64],
                   winT[64 * hh:64 * hh + 64, k, :], qkv_sl(kTs, hh, k),
                   "bank0", ["winT", kTn], tp=(64 * hh, 64 * hh))
        for k in range(8):
            for hh in range(2):
                mm(bank[1][64 * hh:64 * hh + 64, 64 * k:64 * k + 64],
                   winT[64 * hh:64 * hh + 64, k, :], qkv_sl(qTs, hh, k),
                   "bank1", ["winT", qTn], tp=(64 * hh, 64 * hh))
        for k in range(8):
            for hh in range(2):
                mm(bank[2][64 * hh:64 * hh + 64, 64 * k:64 * k + 64],
                   woutT[64 * hh:64 * hh + 64, k, :], qkv_sl(vTs, hh, k),
                   "bank2", ["woutT", vTn], tp=(64 * hh, 64 * hh))

        def softmax(bkidx, dst_ap, dstname):
            bkn, bk = f"bank{bkidx}", bank[bkidx]
            vop(lambda e, o=mx_s[:, :].unsqueeze(2),
                i_=bk[0:128, 0:512].rearrange("p (c n) -> p c n", n=CH):
                    e.reduce_max(o, i_, axis=AX.X), r=[bkn], w=["mx_s"], hz=True)
            mxb = bass.AP(mx_s, 0, [[8, 128], [1, 8], [0, CH]])
            vop(lambda e, o=zsub[:, :].rearrange("p (c n) -> p c n", n=CH),
                i0=bk[0:128, 0:512].rearrange("p (c n) -> p c n", n=CH), i1=mxb:
                    e.tensor_tensor(o, i0, i1, op=ALU.subtract),
                r=[bkn, "mx_s"], w=["zsub"])
            sop(lambda e: e.activation(expt[:, :], zsub[:, :], ACT.Exp),
                r=["zsub"], w=["expt"])
            vop(lambda e, o=sm_s[:, :].unsqueeze(2),
                i_=expt[:, :].rearrange("p (c n) -> p c n", n=CH):
                    e.reduce_sum(o, i_, axis=AX.X), r=["expt"], w=["sm_s"], hz=True)
            vop(lambda e: e.reciprocal(rc_s[:, :], sm_s[:, :]), r=["sm_s"], w=["rc_s"], hz=True)
            rcb = bass.AP(rc_s, 0, [[8, 128], [1, 8], [0, CH]])
            vop(lambda e, o=dst_ap.rearrange("p (c n) -> p c n", n=CH),
                i0=expt[:, :].rearrange("p (c n) -> p c n", n=CH), i1=rcb:
                    e.tensor_tensor(o, i0, i1, op=ALU.mult),
                r=["expt", "rc_s"], w=[dstname])

        softmax(0, pT[:, :], "pT")
        vop(lambda e: e.tensor_tensor(m_s[:, :], pT[:, :], bank[2][0:128, 0:512], op=ALU.mult),
            r=["pT", "bank2"], w=["m_s"])
        mm(bank[3][0:2, 0:512], ones2col[:, :], m_s[:, :], "bank3", ["ones2col", "m_s"])
        vop(lambda e: e.tensor_copy(s2s[:, :], bank[3][0:2, 0:512]), r=["bank3"], w=["s2s"])
        mm(bank[4][0:128, 0:512], halfsel[:, :], s2s[:, :], "bank4", ["halfsel", "s2s"])
        vop(lambda e: e.tensor_tensor(t1_s[:, :], pT[:, :], bank[4][0:128, 0:512], op=ALU.mult),
            r=["pT", "bank4"], w=["t1_s"])
        vop(lambda e: e.tensor_tensor(hh_s[:, :], m_s[:, :], t1_s[:, :], op=ALU.subtract),
            r=["m_s", "t1_s"], w=["hh_s"])
        lrin_b = bass.AP(bc_c[bsl], 0, [[32, 128], [1, 8], [0, CH]])
        lrout_b = bass.AP(bc_c[bsl], 8, [[32, 128], [1, 8], [0, CH]])
        vop(lambda e, i1=lrin_b: e.tensor_tensor(
                hh_s[:, :].rearrange("p (c n) -> p c n", n=CH),
                hh_s[:, :].rearrange("p (c n) -> p c n", n=CH), i1, op=ALU.mult),
            r=["hh_s", bcn], w=["hh_s"])
        vop(lambda e, i1=lrout_b: e.tensor_tensor(
                ph_s[:, :].rearrange("p (c n) -> p c n", n=CH),
                pT[:, :].rearrange("p (c n) -> p c n", n=CH), i1, op=ALU.mult),
            r=["pT", bcn], w=["ph_s"])
        for j in range(4):
            transpose_q(3, j, ph_s[:, 128 * j:128 * j + 128], ["ph_s"])
        vop(lambda e: e.tensor_copy(ptok[:, :, :],
                                    bank[3][0:128, 0:512].rearrange("p (a b) -> p a b", b=128)),
            r=[f"bank3q{j}" for j in range(4)], w=["ptok"])
        for j in range(4):
            transpose_q(4, j, hh_s[:, 128 * j:128 * j + 128], ["hh_s"])
        sop(lambda e: _copy(e, htok[:, :, :],
                            bank[4][0:128, 0:512].rearrange("p (a b) -> p a b", b=128)),
            r=[f"bank4q{j}" for j in range(4)], w=["htok"])
        softmax(1, pqT[:, :], "pqT")
        for k in range(8):
            par, j = k % 2, k // 2
            for hh in range(2):
                ksl = ktok[64 * par:64 * par + 64, j, 64 * hh:64 * hh + 64]
                vsl = vtok[64 * par:64 * par + 64, j, 64 * hh:64 * hh + 64]
                psl = ptok[64 * par:64 * par + 64, j, 64 * hh:64 * hh + 64]
                hsl = htok[64 * par:64 * par + 64, j, 64 * hh:64 * hh + 64]
                mm(bank[5][64 * hh:64 * hh + 64, 64 * k:64 * k + 64], ksl, hsl,
                   "bank5", ["ktok", "htok"], tp=(64 * par, 64 * hh))
                mm(bank[6][64 * hh:64 * hh + 64, 64 * k:64 * k + 64], psl, vsl,
                   "bank6", ["ptok", "vtok"], tp=(64 * par, 64 * hh))
                mm(bank[7][64 * hh:64 * hh + 64, 64 * k:64 * k + 64], vsl, psl,
                   "bank7", ["vtok", "ptok"], tp=(64 * par, 64 * hh))
        for k in range(8):
            for hh in range(2):
                mm(bank[1][64 * hh:64 * hh + 64, 64 * k:64 * k + 64],
                   wout[64 * hh:64 * hh + 64, k, :],
                   pqT[64 * hh:64 * hh + 64, 64 * k:64 * k + 64], "bank1",
                   ["wout", "pqT"], tp=(64 * hh, 64 * hh))
        vop(lambda e, o=ostg[:, :, slice(CH * sub, CH * (sub + 1))],
            i_=bank[1][0:128, 0:512].rearrange("p (a b) -> p a b", b=CH):
                e.tensor_copy(o, i_), r=["bank1"], w=["ostg"])
        wdin_b = bass.AP(bc_c[bsl], 16, [[32, 128], [1, 8], [0, 64]])
        wdout_b = bass.AP(bc_c[bsl], 24, [[32, 128], [1, 8], [0, 64]])
        for st_t, st_n, bkidx, wd_ap in [(winT, "winT", 5, wdin_b),
                                         (wout, "wout", 6, wdout_b),
                                         (woutT, "woutT", 7, wdout_b)]:
            gop(lambda e, o=st_t[:, :, :], i1=wd_ap:
                    e.tensor_tensor(o, o, i1, op=ALU.mult),
                r=[st_n, bcn], w=[st_n])
            vop(lambda e, o=st_t[:, :, :].rearrange("p a b -> p (a b)"), bk=bank[bkidx]:
                    e.tensor_tensor(o, bk[0:128, 0:512], o, op=ALU.add),
                r=[f"bank{bkidx}", st_n], w=[st_n])
        pump(8)
        if sub == CPG - 1:
            p.op("gpsimd", lambda e, o=oscr[:, :, G * g:G * (g + 1)], i=ostg[:, :, :]:
                    e.dma_start(out=o, in_=i),
                 reads=[bf("ostg")], writes=[bf("oscr")], dma=True, sem="g_oscr")

    pending = []

    def pump(n):
        done = 0
        while pending and done < n:
            try:
                next(pending[0])
                done += 1
            except StopIteration:
                pending.pop(0)

    for g in range(min(2, n_groups)):
        for _ in proj_group_gen(g):
            pass
    for c in range(n_chunks):
        g = c // CPG
        if c % CPG == 0:
            while pending:
                pump(100)
        scan_chunk(c, pump)
        if c % CPG == CPG - 1 and g + 2 < n_groups:
            pending.append(proj_group_gen(g + 2))
    while pending:
        pump(100)

    tapspec = {
        "kT0": (kT[0][:, :, :], "kT0"), "vT0": (vT[0][:, :, :], "vT0"),
        "qT0": (qT[0][:, :, :], "qT0"), "lrwd_c": (lrwd_c[:, :], "lrwd_c"),
        "flat2": (flat2[:, :], "flat2"), "winT": (winT[:, :, :], "winT"),
        "wout": (wout[:, :, :], "wout"), "woutT": (woutT[:, :, :], "woutT"),
        "ostg": (ostg[:, :, :], "ostg"), "pqT": (pqT[:, :], "pqT"),
        "pT": (pT[:, :], "pT"), "bc_c0": (bc_c[0][:, :], "bc_c0"),
        "xT": (xT[:, :, :], "xT"),
        "dbg1": (dbg1[:, :], "dbg1"), "dbg2": (dbg2[:, :], "dbg2"),
        "dbg3": (dbg3[:, :], "dbg3"),
    }
    tap_outs = {}
    if "oscr" in taps:
        td = nc.dram_tensor("tap_oscr", [128, 8, L], F32, kind="ExternalOutput").ap()
        tap_outs["oscr"] = (128, 8, L)
        p.op("gpsimd", lambda e, o=td, i=oscr[:, :, :]: e.dma_start(out=o, in_=i),
             reads=[bf("oscr")], dma=True, sem="g_out")
    for name in taps:
        if name not in tapspec:
            continue
        ap, bufname = tapspec[name]
        shp = list(ap.shape)
        td = nc.dram_tensor("tap_" + name, shp, F32, kind="ExternalOutput").ap()
        tap_outs[name] = tuple(shp)
        p.op("gpsimd", lambda e, o=td, i=ap: e.dma_start(out=o, in_=i),
             reads=[bf(bufname)], dma=True, sem="g_out")

    if do_post:
        last_by_sem = {}
        for o in p.ops:
            last_by_sem[o.sem] = o
        p.barrier = list(last_by_sem.values())
        esA.close()

    if do_post:
        import contextlib as _ctx
        esB1 = _ctx.ExitStack()
        sb = lambda name, shape: es.enter_context(nc.sbuf_tensor(name, list(shape), F32))
        sb1 = lambda name, shape: esB1.enter_context(nc.sbuf_tensor(name, list(shape), F32))
        x1T = sb("x1T", (128, 8, TOWN))
        acc = sb("acc", (128, NT, DIM))
        comb = sb("comb", (128, NT, 8))
        combT = sb("combT", (8, NT, 128))
        r_s = sb("r_s", (128, DIM))
        s1a = sb("s1a", (128, 1))
        s1b = sb("s1b", (128, 1))
        s2a = sb("s2a", (128, 1))
        s2b = sb("s2b", (128, 1))
        mva = sb("mva", (128, 1))
        rst = sb("rst", (128, 1))
        g8 = sb("g8", (128, 8))
        gsc = sb("gsc", (128, 8))
        gm1 = sb("gm1", (128, 1))
        ge1 = sb("ge1", (128, 8))
        gs1 = sb("gs1", (128, 8))
        gf1 = sb("gf1", (128, 8))
        gp2 = sb("gp2", (128, 8))
        gmask = sb("gmask", (128, 8))
        gw_s = sb("gw_s", (128, 8, 8))
        outt = sb("outt", (128, DIM))
        woT_s = sb1("woT_s", (128, 8, DIM))
        otA = [sb1(f"otA{i}", (128, 8, 128)) for i in range(2)]
        otB = [sb1(f"otB{i}", (128, 8, 128)) for i in range(2)]
        otS = sb1("otS", (128, 8, 128))
        xres = [sb1("xres0", (128, DIM))] * 2

        dma_in("woT_s", woT_s[:], woT_d.rearrange("(k p) f -> p k f", p=128))
        dma_in("gw_s", gw_s[:], gw_d.rearrange("(k p) e -> p k e", p=128))


        def post_tile(t):
            slot = t % 2
            dma_in(f"otA{slot}", otA[slot][:], oscr[:, :, 128 * t:128 * t + 128],
                   sem=f"s_otA{slot}")
            dma_in(f"otB{slot}", otB[slot][:], oscr[:, :, TOWN + 128 * t:TOWN + 128 * t + 128],
                   sem=f"s_otB{slot}")
            dma_in("xres0", xres[slot][:], xown_d[128 * t:128 * t + 128, :],
                   sem="s_xres0")
            vop(lambda e, i0=otA[slot][:, :, :].rearrange("p a b -> p (a b)"),
                sc=msel[:, 1:2]:
                    e.tensor_scalar(otS[:, :, :].rearrange("p a b -> p (a b)"), i0, sc, None,
                                    op0=ALU.mult),
                r=[f"otA{slot}", "msel"], w=["otS"])
            vop(lambda e, i0=otB[slot][:, :, :].rearrange("p a b -> p (a b)"),
                sc=msel[:, 0:1]:
                    e.scalar_tensor_tensor(otS[:, :, :].rearrange("p a b -> p (a b)"),
                                           i0, sc, otS[:, :, :].rearrange("p a b -> p (a b)"),
                                           op0=ALU.mult, op1=ALU.add),
                r=[f"otB{slot}", "msel", "otS"], w=["otS"])
            for fh in range(2):
                for oc in range(8):
                    mm(bank[fh][0:128, 0:512], otS[:, oc, :],
                       woT_s[:, oc, 512 * fh:512 * fh + 512], f"bank{fh}",
                       ["otS", "woT_s"], start=(oc == 0), stop=(oc == 7))

            for fh in range(2):
                vop(lambda e, o=r_s[:, 512 * fh:512 * fh + 512], i0=bank[fh][0:128, 0:512],
                    i1=xres[slot][:, 512 * fh:512 * fh + 512]:
                        e.tensor_tensor(o, i0, i1, op=ALU.add),
                    r=[f"bank{fh}", "xres0"], w=["r_s"])
            vop(lambda e: e.tensor_reduce(s1a[:, :], r_s[:, :], axis=AX.X, op=ALU.add),
                r=["r_s"], w=["s1a"], hz=True)
            vop(lambda e: e.tensor_scalar(mva[:, :], s1a[:, :], -1.0 / DIM, None, op0=ALU.mult),
                r=["s1a"], w=["mva"])
            for fh in range(2):
                nm = "s2a" if fh == 0 else "s2b"
                a_out = s2a if fh == 0 else s2b
                sop(lambda e, i=r_s[:, 512 * fh:512 * fh + 512], a=a_out:
                        e.activation(bank[6][0:128, 0:512], i, ACT.Square,
                                     bias=mva[:, :], accum_out=a[:, :]),
                    r=["r_s", "mva"], w=["bank6", nm], hz=True)
            vop(lambda e: e.tensor_tensor(s2a[:, :], s2a[:, :], s2b[:, :], op=ALU.add),
                r=["s2a", "s2b"], w=["s2a"])
            sop(lambda e: e.activation(rst[:, :], s2a[:, :], ACT.Sqrt,
                                       bias=float(EPS_LN), scale=1.0 / DIM),
                r=["s2a", "epsv"], w=["rst"])
            vop(lambda e: e.reciprocal(rst[:, :], rst[:, :]), r=["rst"], w=["rst"], hz=True)
            vop(lambda e, o=acc[:, t, :]:
                    e.tensor_scalar(o, r_s[:, :], mva[:, :], rst[:, :],
                                    op0=ALU.add, op1=ALU.mult),
                r=["r_s", "mva", "rst"], w=["acc"])
            for dk in range(8):
                bkn = f"bank{2 + dk % 2}"
                bk = bank[2 + dk % 2]
                col = 128 * ((dk // 2) % 2)
                transpose(bk[0:128, col:col + 128], acc[:, t, 128 * dk:128 * dk + 128],
                          bkn, ["acc"])
                vop(lambda e, o=x1T[:, dk, 128 * t:128 * t + 128], i_=bk[0:128, col:col + 128]:
                        e.tensor_copy(o, i_), r=[bkn], w=["x1T"])
            for dk in range(8):
                mm(bank[4][0:128, 0:8], x1T[:, dk, 128 * t:128 * t + 128], gw_s[:, dk, :],
                   "bank4", ["x1T", "gw_s"], start=(dk == 0), stop=False)
            mm(bank[4][0:128, 0:8], onesrow[:, :], gb_s[:, :], "bank4",
               ["onesrow", "gb_s"], start=False, stop=True)
            vop(lambda e: e.tensor_scalar(gsc[:, :], bank[4][0:128, 0:8], float(GEPS), None,
                                          op0=ALU.max), r=["bank4"], w=["gsc"])
            vop(lambda e: e.reduce_max(gm1[:, :], gsc[:, :], axis=AX.X), r=["gsc"], w=["gm1"], hz=True)
            vop(lambda e: e.tensor_scalar(gm1[:, :], gm1[:, :], -1.0, None, op0=ALU.mult),
                r=["gm1"], w=["gm1"])
            sop(lambda e: e.activation(g8[:, :], gsc[:, :], ACT.Exp, bias=gm1[:, :]),
                r=["gsc", "gm1"], w=["g8"])
            vop(lambda e: e.reduce_sum(gm1[:, :], g8[:, :], axis=AX.X), r=["g8"], w=["gm1"], hz=True)
            vop(lambda e: e.reciprocal(gm1[:, :], gm1[:, :]), r=["gm1"], w=["gm1"], hz=True)
            vop(lambda e: e.tensor_scalar(g8[:, :], g8[:, :], gm1[:, :], None, op0=ALU.mult),
                r=["g8", "gm1"], w=["g8"])
            vop(lambda e: e.reduce_max(gm1[:, :], g8[:, :], axis=AX.X), r=["g8"], w=["gm1"], hz=True)
            gm1b = bass.AP(gm1, 0, [[1, 128], [1, 1], [0, 8]])
            vop(lambda e, i1=gm1b: e.tensor_tensor(ge1[:, :].unsqueeze(1),
                g8[:, :].unsqueeze(1), i1, op=ALU.is_ge), r=["g8", "gm1"], w=["ge1"])
            vop(lambda e: e.tensor_tensor(gs1[:, :], ge1[:, :], pow2[:, :], op=ALU.mult),
                r=["ge1", "pow2"], w=["gs1"])
            vop(lambda e: e.reduce_max(gm1[:, :], gs1[:, :], axis=AX.X), r=["gs1"], w=["gm1"], hz=True)
            gm1b2 = bass.AP(gm1, 0, [[1, 128], [1, 1], [0, 8]])
            vop(lambda e, i1=gm1b2: e.tensor_tensor(gf1[:, :].unsqueeze(1),
                gs1[:, :].unsqueeze(1), i1, op=ALU.is_ge), r=["gs1", "gm1"], w=["gf1"])
            vop(lambda e: e.scalar_tensor_tensor(gp2[:, :], gf1[:, :], -2.0, g8[:, :],
                                                 op0=ALU.mult, op1=ALU.add),
                r=["gf1", "g8"], w=["gp2"])
            vop(lambda e: e.reduce_max(gm1[:, :], gp2[:, :], axis=AX.X), r=["gp2"], w=["gm1"], hz=True)
            gm1b3 = bass.AP(gm1, 0, [[1, 128], [1, 1], [0, 8]])
            vop(lambda e, i1=gm1b3: e.tensor_tensor(ge1[:, :].unsqueeze(1),
                gp2[:, :].unsqueeze(1), i1, op=ALU.is_ge), r=["gp2", "gm1"], w=["ge1"])
            vop(lambda e: e.tensor_tensor(gs1[:, :], ge1[:, :], pow2[:, :], op=ALU.mult),
                r=["ge1", "pow2"], w=["gs1"])
            vop(lambda e: e.reduce_max(gm1[:, :], gs1[:, :], axis=AX.X), r=["gs1"], w=["gm1"], hz=True)
            gm1b4 = bass.AP(gm1, 0, [[1, 128], [1, 1], [0, 8]])
            vop(lambda e, i1=gm1b4: e.tensor_tensor(ge1[:, :].unsqueeze(1),
                gs1[:, :].unsqueeze(1), i1, op=ALU.is_ge), r=["gs1", "gm1"], w=["ge1"])
            vop(lambda e: e.tensor_tensor(gmask[:, :], gf1[:, :], ge1[:, :], op=ALU.add),
                r=["gf1", "ge1"], w=["gmask"])
            vop(lambda e: e.tensor_scalar(gs1[:, :], g8[:, :], float(WTHR), None, op0=ALU.max),
                r=["g8"], w=["gs1"])
            vop(lambda e: e.tensor_tensor(gs1[:, :], gs1[:, :], gmask[:, :], op=ALU.mult),
                r=["gs1", "gmask"], w=["gs1"])
            vop(lambda e: e.reduce_sum(gm1[:, :], gs1[:, :], axis=AX.X), r=["gs1"], w=["gm1"], hz=True)
            vop(lambda e: e.reciprocal(gm1[:, :], gm1[:, :]), r=["gm1"], w=["gm1"], hz=True)
            vop(lambda e, o=comb[:, t, :]: e.tensor_scalar(o, gs1[:, :], gm1[:, :], 0.5,
                                                           op0=ALU.mult, op1=ALU.mult),
                r=["gs1", "gm1"], w=["comb"])
            transpose(bank[5][0:8, 0:128], comb[:, t, :], "bank5", ["comb"])
            vop(lambda e, o=combT[:, t, :]: e.tensor_copy(o, bank[5][0:8, 0:128]),
                r=["bank5"], w=["combT"])
            for fh in range(2):
                mm(bank[6][0:128, 0:512], combT[:, t, :], eb_s[:, 512 * fh:512 * fh + 512],
                   "bank6", ["combT", "eb_s"])
                vop(lambda e, o=acc[:, t, 512 * fh:512 * fh + 512], i0=bank[6][0:128, 0:512]:
                        e.tensor_tensor(o, i0, o, op=ALU.add),
                    r=["bank6", "acc"], w=["acc"])

        for t in range(NT):
            post_tile(t)

        last_by_sem = {}
        for o in p.ops:
            last_by_sem[o.sem] = o
        p.barrier = list(last_by_sem.values())
        p.barrier_applied = set()
        esB1.close()
        ew_s = [sb(f"ew_s{i}", (128, 8, 512)) for i in range(2)]

        for e_i in range(NEXP):
            for fh in range(2):
                slot = (e_i * 2 + fh) % 2
                dma_in(f"ew_s{slot}", ew_s[slot][:],
                       ew_d[e_i, :, 512 * fh:512 * fh + 512].rearrange("(k p) f -> p k f", p=128),
                       sem=f"s_ew{slot}")
                for t in range(NT):
                    bkn = f"bank{t % 4}"
                    bk = bank[t % 4]
                    for dk in range(8):
                        mm(bk[0:128, 0:512], x1T[:, dk, 128 * t:128 * t + 128],
                           ew_s[slot][:, dk, :], bkn, ["x1T", f"ew_s{slot}"],
                           start=(dk == 0), stop=(dk == 7))
                    vop(lambda e, o=acc[:, t, 512 * fh:512 * fh + 512], i0=bk[0:128, 0:512],
                        sc=comb[:, t, e_i:e_i + 1]:
                            e.scalar_tensor_tensor(o, i0, sc, o, op0=ALU.mult, op1=ALU.add),
                        r=[bkn, "comb", "acc"], w=["acc"])

        for t in range(NT):
            for fh in range(2):
                nm = "s1a" if fh == 0 else "s1b"
                a_out = s1a if fh == 0 else s1b
                vop(lambda e, i=acc[:, t, 512 * fh:512 * fh + 512], a=a_out:
                        e.tensor_reduce(a[:, :], i, axis=AX.X, op=ALU.add),
                    r=["acc"], w=[nm], hz=True)
            vop(lambda e: e.tensor_tensor(s1a[:, :], s1a[:, :], s1b[:, :], op=ALU.add),
                r=["s1a", "s1b"], w=["s1a"])
            vop(lambda e: e.tensor_scalar(mva[:, :], s1a[:, :], -1.0 / DIM, None, op0=ALU.mult),
                r=["s1a"], w=["mva"])
            for fh in range(2):
                nm = "s2a" if fh == 0 else "s2b"
                a_out = s2a if fh == 0 else s2b
                sop(lambda e, i=acc[:, t, 512 * fh:512 * fh + 512], a=a_out:
                        e.activation(bank[6][0:128, 0:512], i, ACT.Square,
                                     bias=mva[:, :], accum_out=a[:, :]),
                    r=["acc", "mva"], w=["bank6", nm], hz=True)
            vop(lambda e: e.tensor_tensor(s2a[:, :], s2a[:, :], s2b[:, :], op=ALU.add),
                r=["s2a", "s2b"], w=["s2a"])
            sop(lambda e: e.activation(rst[:, :], s2a[:, :], ACT.Sqrt,
                                       bias=float(EPS_LN), scale=1.0 / DIM),
                r=["s2a", "epsv"], w=["rst"])
            vop(lambda e: e.reciprocal(rst[:, :], rst[:, :]), r=["rst"], w=["rst"], hz=True)
            vop(lambda e, o=outt[:, :], i=acc[:, t, :]:
                    e.tensor_scalar(o, i, mva[:, :], rst[:, :], op0=ALU.add, op1=ALU.mult),
                r=["acc", "mva", "rst"], w=["outt"])
            p.op("gpsimd", lambda e, o=out_d[128 * t:128 * t + 128, :], i=outt[:, :]:
                    e.dma_start(out=o, in_=i),
                 reads=[bf("outt")], writes=[bf("out_d")], dma=True, sem="g_out")

    if do_post:
        tapspecB = {"acc": (acc[:, :, :], "acc"), "comb": (comb[:, :, :], "comb"),
                    "x1T": (x1T[:, :, :], "x1T")}
        for name in taps:
            if name not in tapspecB:
                continue
            ap, bufname = tapspecB[name]
            shp = list(ap.shape)
            td = nc.dram_tensor("tap_" + name, shp, F32, kind="ExternalOutput").ap()
            tap_outs[name] = tuple(shp)
            p.op("gpsimd", lambda e, o=td, i=ap: e.dma_start(out=o, in_=i),
                 reads=[bf(bufname)], dma=True, sem="g_out")

    final_counts = p.plan()
    need = p.needed_posts(final_counts)
    sems = {s: es.enter_context(nc.semaphore("sem_" + s)) for s in final_counts}

    with nc.Block() as block:
        @block.sync
        def _(eng):
            p.emit("sync", eng, sems, need)
            for s2 in sorted(final_counts):
                if s2 != "sync" and final_counts[s2]:
                    eng.wait_ge(sems[s2], final_counts[s2])

        @block.tensor
        def _(eng):
            p.emit("tensor", eng, sems, need)

        @block.vector
        def _(eng):
            p.emit("vector", eng, sems, need)

        @block.scalar
        def _(eng):
            p.emit("scalar", eng, sems, need)

        @block.gpsimd
        def _(eng):
            p.emit("gpsimd", eng, sems, need)

    if not do_post:
        esA.close()
    es.close()
    return nc, tap_outs


def make_in_maps(inputs):
    prep = host_prep(inputs)
    x = np.ascontiguousarray(np.asarray(inputs["L0_attn_x"], dtype=np.float32))
    in_maps = []
    for c in range(8):
        b, half = c // 2, c % 2
        ms = np.zeros((128, 2), np.float32)
        ms[:, 0] = float(half)
        ms[:, 1] = 1.0 - float(half)
        m = {"x": np.ascontiguousarray(x[b]),
             "x_own": np.ascontiguousarray(x[b, TOWN * half:TOWN * (half + 1), :]),
             "mselw": ms}
        m.update(prep)
        in_maps.append(m)
    return in_maps


def kernel(**inputs):
    in_maps = make_in_maps(inputs)
    nc, _ = build_core()
    mybir.codegen_inst_isa_subclasses(nc)
    res = run_bass_kernel_spmd(nc, in_maps, core_ids=list(range(8)))
    out = np.empty((B, L, DIM), np.float32)
    for c in range(8):
        b, half = c // 2, c % 2
        out[b, TOWN * half:TOWN * (half + 1), :] = np.asarray(res.results[c]["out"])
    return out


# version 11
# speedup vs baseline: 1.1974x; 1.0123x over previous
import numpy as np
import concourse.bass as bass
import concourse.mybir as mybir
from concourse.bass_utils import run_bass_kernel_spmd

F32 = mybir.dt.float32
AX = mybir.AxisListType
ALU = mybir.AluOpType
ACT = mybir.ActivationFunctionType

B, L, DIM = 4, 4096, 1024
CH = 64
NCH = L // CH
G = 256
NG = L // G
CPG = G // CH
NEXP = 8
TOWN = L // 2
NT = TOWN // 128
EPS_LN = 1e-5
BASE_LR, BASE_WD = 1e-3, 0.9
GEPS, WTHR = 1e-4, 0.1
SCAN_STAGE = 6


class _Op:
    __slots__ = ("eng", "emit", "deps", "inc", "post", "sem", "hz")

    def __init__(self, eng, emit, deps, inc, sem, hz):
        self.eng, self.emit, self.deps, self.inc, self.sem = eng, emit, deps, inc, sem
        self.hz = hz
        self.post = None


class Buf:
    __slots__ = ("writer", "readers")

    def __init__(self):
        self.writer = None
        self.readers = []


class Prog:
    ENGS = ("sync", "tensor", "vector", "scalar", "gpsimd")

    def __init__(self):
        self.ops = []
        self.barrier = None
        self.barrier_applied = set()

    def op(self, eng, emit, reads=(), writes=(), deps=(), dma=False, sem=None, hz=False):
        dd = list(deps)
        if self.barrier is not None and eng not in self.barrier_applied:
            dd.extend(self.barrier)
            self.barrier_applied.add(eng)
        for b in reads:
            if b.writer is not None:
                dd.append(b.writer)
        for b in writes:
            dd.extend(b.readers)
            if b.writer is not None:
                dd.append(b.writer)
        assert (sem is not None) == dma, "DMA ops need an explicit slot sem"
        o = _Op(eng, emit, dd, 16 if dma else 1, sem if dma else eng, hz)
        self.ops.append(o)
        for b in reads:
            b.readers.append(o)
        for b in writes:
            b.writer = o
            b.readers = []
        return o

    def plan(self):
        cnt = {}
        for o in self.ops:
            cnt[o.sem] = cnt.get(o.sem, 0) + o.inc
            o.post = cnt[o.sem]
        return cnt

    def _walk(self, eng_name):
        watermark = {}
        for o in self.ops:
            if o.eng != eng_name:
                continue
            best = {}
            for d in o.deps:
                if (d.eng == eng_name == "tensor" and o.eng == "tensor"
                        and d.sem == "tensor" and not d.hz):
                    continue
                if best.get(d.sem, -1) < d.post:
                    best[d.sem] = d.post
            waits = []
            for s2 in sorted(best):
                v = best[s2]
                if watermark.get(s2, 0) >= v:
                    continue
                waits.append((s2, v))
                watermark[s2] = v
            yield o, waits

    def needed_posts(self, final_counts):
        need = set()
        for e in self.ENGS:
            for o, waits in self._walk(e):
                for s2, v in waits:
                    need.add((s2, v))
        for s2, v in final_counts.items():
            need.add((s2, v))
        return need

    def emit(self, eng_name, eng, sems, need):
        pending = 0
        for o, waits in self._walk(eng_name):
            for s2, v in waits:
                eng.wait_ge(sems[s2], v)
            inst = o.emit(eng)
            if o.inc == 16:
                inst.then_inc(sems[o.sem], 16)
            else:
                pending += o.inc
                if (o.sem, o.post) in need:
                    inst.then_inc(sems[o.sem], pending)
                    pending = 0


def _head_perm():
    idx = np.empty(DIM, np.int64)
    for oc in range(8):
        for c2 in range(2):
            h = oc + 8 * c2
            idx[oc * 128 + c2 * 64: oc * 128 + c2 * 64 + 64] = np.arange(h * 64, h * 64 + 64)
    return idx


def host_prep(inp):
    idx = _head_perm()
    f32c = lambda a: np.ascontiguousarray(np.asarray(a), dtype=np.float32)
    d = {}
    d["wqT"] = f32c(np.asarray(inp["L0_attn_Wq"]).T[:, idx])
    d["wkT"] = f32c(np.asarray(inp["L0_attn_Wk"]).T[:, idx])
    d["wvT"] = f32c(np.asarray(inp["L0_attn_Wv"]).T[:, idx])
    d["wlbT"] = f32c(np.concatenate([np.asarray(inp["L0_attn_Wlr"]).T,
                                     np.asarray(inp["L0_attn_Wbeta"]).T], axis=1))
    d["woT"] = f32c(np.asarray(inp["L0_attn_Wo"]).T[idx, :])
    wi = np.asarray(inp["L0_attn_W_in_init"])[0]
    wo = np.asarray(inp["L0_attn_W_out_init"])[0]

    def compact(arr_hab):
        a = np.asarray(arr_hab).reshape(2, 8, 64, 64)
        return f32c(a.transpose(0, 2, 1, 3).reshape(128, 8, 64))

    d["winT0"] = compact(wi.transpose(1, 2, 0))
    d["wout0"] = compact(wo.transpose(1, 0, 2))
    d["woutT0"] = compact(wo.transpose(1, 2, 0))
    d["gw"] = f32c(inp["L0_ffn_gate_W"])
    d["gb"] = f32c(np.asarray(inp["L0_ffn_gate_b"]).reshape(1, 8))
    d["ew"] = f32c(inp["L0_ffn_expert_W"])
    d["eb"] = f32c(inp["L0_ffn_expert_b"])
    d["identw"] = np.eye(128, dtype=np.float32)
    oc2 = np.zeros((128, 2), np.float32)
    oc2[:64, 0] = 1.0
    oc2[64:, 1] = 1.0
    d["ones2colw"] = oc2
    hs = np.zeros((2, 128), np.float32)
    hs[0, :64] = 1.0
    hs[1, 64:] = 1.0
    d["halfselw"] = hs
    d["onesroww"] = np.ones((1, 128), np.float32)
    d["pow2w"] = np.ascontiguousarray(
        np.broadcast_to(2.0 ** -np.arange(8, dtype=np.float32), (128, 8)))
    c64 = np.empty((64, 1), np.float32)
    c64[:32] = BASE_LR / 64.0
    c64[32:] = BASE_WD / 64.0
    d["c64w"] = c64
    d["epsw"] = np.full((128, 1), EPS_LN, np.float32)
    return d


def build_core(n_groups=NG, n_chunks=NCH, do_post=True, taps=()):
    nc = bass.Bass("TRN2", target_bir_lowering=False, detect_race_conditions=False)
    p = Prog()
    bufs = {}

    def bf(name):
        if name not in bufs:
            bufs[name] = Buf()
        return bufs[name]

    def dram_in(name, shape):
        return nc.dram_tensor(name, list(shape), F32, kind="ExternalInput").ap()

    xb = dram_in("x", (L, DIM))
    xown_d = dram_in("x_own", (TOWN, DIM))
    msel_d = dram_in("mselw", (128, 2))
    wqT = dram_in("wqT", (DIM, DIM))
    wkT = dram_in("wkT", (DIM, DIM))
    wvT = dram_in("wvT", (DIM, DIM))
    wlbT_d = dram_in("wlbT", (DIM, 64))
    woT_d = dram_in("woT", (DIM, DIM))
    winT0 = dram_in("winT0", (128, 8, 64))
    wout0 = dram_in("wout0", (128, 8, 64))
    woutT0 = dram_in("woutT0", (128, 8, 64))
    gw_d = dram_in("gw", (DIM, 8))
    gb_d = dram_in("gb", (1, 8))
    ew_d = dram_in("ew", (NEXP, DIM, DIM))
    eb_d = dram_in("eb", (NEXP, DIM))
    ident_d = dram_in("identw", (128, 128))
    ones2col_d = dram_in("ones2colw", (128, 2))
    halfsel_d = dram_in("halfselw", (2, 128))
    onesrow_d = dram_in("onesroww", (1, 128))
    pow2_d = dram_in("pow2w", (128, 8))
    c64_d = dram_in("c64w", (64, 1))
    eps_d = dram_in("epsw", (128, 1))

    out_d = nc.dram_tensor("out", [TOWN, DIM], F32, kind="ExternalOutput").ap()
    oscr = nc.dram_tensor("oscr", [128, 8, L], F32, kind="Internal").ap()

    import contextlib
    es = contextlib.ExitStack()
    esA = contextlib.ExitStack()
    sbP = lambda name, shape: es.enter_context(nc.sbuf_tensor(name, list(shape), F32))
    sb = lambda name, shape: esA.enter_context(nc.sbuf_tensor(name, list(shape), F32))
    psum = lambda name, shape: es.enter_context(nc.psum_tensor(name, list(shape), F32))

    ident = sbP("ident", (128, 128))
    onesrow = sbP("onesrow", (1, 128))
    pow2 = sbP("pow2", (128, 8))
    gb_s = sbP("gb_s", (1, 8))
    eb_s = sbP("eb_s", (8, DIM))
    msel = sbP("msel", (128, 2))
    epsv = sbP("epsv", (128, 1))
    nc.const_aps.register(F32, float(EPS_LN), epsv[:, 0:1])

    wq_s = sb("wq_s", (128, 8, DIM))
    wk_s = sb("wk_s", (128, 8, DIM))
    wlb_s = sb("wlb_s", (128, 8, 64))
    wv_s = [sb(f"wv_s{i}", (128, 8, 128)) for i in range(2)]
    ones2col = sb("ones2col", (128, 2))
    halfsel = sb("halfsel", (2, 128))
    c64 = sb("c64", (64, 1))

    winT = sb("winT", (128, 8, 64))
    wout = sb("wout", (128, 8, 64))
    woutT = sb("woutT", (128, 8, 64))

    xrow = [sb(f"xrow{i}", (128, 2, DIM)) for i in range(2)]
    xT = sb("xT", (128, 8, G))
    qT = [sb(f"qT{i}", (128, CPG, 4, 128)) for i in range(2)]
    kT = [sb(f"kT{i}", (128, CPG, 4, 128)) for i in range(2)]
    vT = [sb(f"vT{i}", (128, CPG, 4, 128)) for i in range(2)]
    lrwdT = sb("lrwdT", (64, G))
    lrwd_c = sb("lrwd_c", (64, NCH))
    lrtmp = sb("lrtmp", (4, 64))
    flat2 = sb("flat2", (2, NCH * 32))
    bc_c = [sb(f"bc_c{i}", (128, 32)) for i in range(2)]

    pqT = sb("pqT", (128, 512))
    pT = sb("pT", (128, 512))
    zsub = sb("zsub", (128, 512))
    expt = sb("expt", (128, 512))
    m_s = sb("m_s", (128, 512))
    t1_s = sb("t1_s", (128, 512))
    hh_s = sb("hh_s", (128, 512))
    ph_s = sb("ph_s", (128, 512))
    s2s = sb("s2s", (2, 512))
    mx_s = sb("mx_s", (128, 8))
    sm_s = sb("sm_s", (128, 8))
    rc_s = sb("rc_s", (128, 8))
    ktok = sb("ktok", (128, 4, 128))
    vtok = sb("vtok", (128, 4, 128))
    ptok = sb("ptok", (128, 4, 128))
    htok = sb("htok", (128, 4, 128))
    ostg = sb("ostg", (128, 8, G))
    dbg1 = sb("dbg1", (128, 512))
    dbg2 = sb("dbg2", (128, 512))
    dbg3 = sb("dbg3", (128, 512))

    bank = [psum(f"bank{i}", (128, 512)) for i in range(8)]

    def dma_in(bufname, out_ap, in_ap, eng="sync", extra_reads=(), sem="s_init"):
        return p.op(eng, lambda e, o=out_ap, i=in_ap: e.dma_start(out=o, in_=i),
                    reads=list(extra_reads), writes=[bf(bufname)], dma=True, sem=sem)

    dma_in("wq_s", wq_s[:], wqT.rearrange("(k p) f -> p k f", p=128))
    dma_in("wk_s", wk_s[:], wkT.rearrange("(k p) f -> p k f", p=128))
    dma_in("wlb_s", wlb_s[:], wlbT_d.rearrange("(k p) f -> p k f", p=128))
    for name, s_t, d_t in [("ident", ident, ident_d), ("ones2col", ones2col, ones2col_d),
                           ("halfsel", halfsel, halfsel_d), ("onesrow", onesrow, onesrow_d),
                           ("pow2", pow2, pow2_d), ("c64", c64, c64_d),
                           ("gb_s", gb_s, gb_d), ("eb_s", eb_s, eb_d), ("msel", msel, msel_d),
                           ("epsv", epsv, eps_d)]:
        dma_in(name, s_t[:], d_t[:])
    dma_in("winT", winT[:], winT0[:])
    dma_in("wout", wout[:], wout0[:])
    dma_in("woutT", woutT[:], woutT0[:])

    def mm(out_ap, lhsT_ap, rhs_ap, wbankname, rbufnames, start=True, stop=True, tp=None):
        wr = [bf(wbankname)]
        if wbankname in ("bank3", "bank4", "bank5", "bank6"):
            wr += [bf(wbankname + f"q{j}") for j in range(4)]
        return p.op("tensor",
                    lambda e, o=out_ap, a=lhsT_ap, b=rhs_ap, s=start, t=stop, q=tp:
                        e.matmul(o, a, b, start=s, stop=t, tile_position=q),
                    reads=[bf(n) for n in rbufnames], writes=wr)

    def transpose(out_ap, in_ap, wbankname, rbufnames):
        np_ = in_ap.shape[0]
        return p.op("tensor",
                    lambda e, o=out_ap, i=in_ap, n_=np_: e.transpose(o, i, ident[0:n_, 0:n_]),
                    reads=[bf(n) for n in rbufnames] + [bf("ident")],
                    writes=[bf(wbankname)])

    def transpose_q(bkidx, j, in_ap, rbufnames):
        np_ = in_ap.shape[0]
        out_ap = bank[bkidx][0:128, 128 * j:128 * j + 128]
        return p.op("tensor",
                    lambda e, o=out_ap, i=in_ap, n_=np_: e.transpose(o, i, ident[0:n_, 0:n_]),
                    reads=[bf(n) for n in rbufnames] + [bf("ident"), bf(f"bank{bkidx}")],
                    writes=[bf(f"bank{bkidx}q{j}")])

    def bank_w(bkidx):
        return [bf(f"bank{bkidx}")] + [bf(f"bank{bkidx}q{j}") for j in range(4)]

    def vop(emit, r=(), w=(), hz=False):
        return p.op("vector", emit, reads=[bf(n) for n in r],
                    writes=[bf(n) for n in w], hz=hz)

    def sop(emit, r=(), w=(), hz=False):
        return p.op("scalar", emit, reads=[bf(n) for n in r],
                    writes=[bf(n) for n in w], hz=hz)

    def _copy(e, o, i_):
        return e.tensor_copy(o, i_) if hasattr(e, "tensor_copy") else e.copy(o, i_)

    def gop(emit, r=(), w=()):
        return p.op("gpsimd", emit, reads=[bf(n) for n in r], writes=[bf(n) for n in w])

    def proj_group_gen(g):
        slot = g % 2
        xr = xrow[slot]
        dma_in(f"xrow{slot}", xr[:], xb[G * g:G * (g + 1), :].rearrange("(t p) f -> p t f", p=128), sem=f"s_xrow{slot}")
        for quad in range(4):
            tt, dk0 = quad // 2, 4 * (quad % 2)
            bkidx = 3 + quad % 2
            for jj in range(4):
                transpose_q(bkidx, jj, xr[:, tt, 128 * (dk0 + jj):128 * (dk0 + jj) + 128],
                            [f"xrow{slot}"])
            cop = sop if (quad % 2) else vop
            xt_dst = bass.AP(xT, (4 * (quad % 2)) * G + 128 * tt,
                             [[8 * G, 128], [G, 4], [1, 128]])
            cop(lambda e, o=xt_dst, i_=bank[bkidx][0:128, 0:512]
                .rearrange("p (a b) -> p a b", b=128): _copy(e, o, i_),
                r=[f"bank{bkidx}q{j}" for j in range(4)], w=["xT"])
            yield

        def projmm(lhs_ap_fn, wname, dst_ap, dstname, act, bkidx):
            bkn, bk = f"bank{bkidx}", bank[bkidx]
            for dk in range(8):
                mm(bk[0:128, 0:G], lhs_ap_fn(dk), xT[:, dk, :], bkn, [wname, "xT"],
                   start=(dk == 0), stop=(dk == 7))
            src3 = bk[0:128, 0:G].rearrange("p (s n) -> p s n", n=CH)
            if act == "silu":
                sop(lambda e, o=dst_ap, i_=src3: e.activation(o, i_, ACT.Silu),
                    r=[bkn], w=[dstname])
            else:
                vop(lambda e, o=dst_ap, i_=src3: e.tensor_copy(o, i_),
                    r=[bkn], w=[dstname])

        def qkv_dst(tens, oc):
            return tens[:, 0:CPG, oc // 2, 64 * (oc % 2):64 * (oc % 2) + 64]

        for oc in range(8):
            projmm(lambda dk, oc=oc: wk_s[:, dk, 128 * oc:128 * oc + 128], "wk_s",
                   qkv_dst(kT[slot], oc), f"kT{slot}", "silu", 3 + oc % 2)
            yield
        for oc in range(8):
            dma_in(f"wv_s{oc % 2}", wv_s[oc % 2][:],
                   wvT.rearrange("(k p) f -> p k f", p=128)[:, :, 128 * oc:128 * oc + 128],
                   sem=f"s_wv{oc % 2}")
            projmm(lambda dk, oc=oc: wv_s[oc % 2][:, dk, :], f"wv_s{oc % 2}",
                   qkv_dst(vT[slot], oc), f"vT{slot}", "copy", 3 + oc % 2)
            yield
        for oc in range(8):
            projmm(lambda dk, oc=oc: wq_s[:, dk, 128 * oc:128 * oc + 128], "wq_s",
                   qkv_dst(qT[slot], oc), f"qT{slot}", "silu", 3 + oc % 2)
            yield
        bkn, bk = "bank3", bank[3]
        for dk in range(8):
            mm(bk[0:64, 0:G], wlb_s[:, dk, :], xT[:, dk, :], bkn, ["wlb_s", "xT"],
               start=(dk == 0), stop=(dk == 7))
        sop(lambda e, o=lrwdT[:, :], i_=bk[0:64, 0:G]: e.activation(o, i_, ACT.Sigmoid),
            r=[bkn], w=["lrwdT"])
        vop(lambda e, o=lrwd_c[:, CPG * g:CPG * (g + 1)].unsqueeze(2),
            i_=lrwdT[:, :].rearrange("p (c n) -> p c n", n=CH): e.reduce_sum(o, i_, axis=AX.X),
            r=["lrwdT"], w=["lrwd_c"], hz=True)
        vop(lambda e, o=lrwd_c[:, CPG * g:CPG * (g + 1)], i_=lrwd_c[:, CPG * g:CPG * (g + 1)]:
                e.tensor_scalar(o, i_, c64[:, :], None, op0=ALU.mult),
            r=["lrwd_c", "c64"], w=["lrwd_c"])
        transpose(bank[4][0:CPG, 0:64], lrwd_c[:, CPG * g:CPG * (g + 1)], "bank4", ["lrwd_c"])
        vop(lambda e, o=lrtmp[:, :], i_=bank[4][0:CPG, 0:64]: e.tensor_copy(o, i_),
            r=["bank4"], w=["lrtmp"])
        for hh in range(2):
            in_ap = bass.AP(lrtmp, 8 * hh, [[64, CPG], [16, 4], [1, 8]])
            out_ap = bass.AP(flat2, 32 * NCH * hh + 32 * CPG * g, [[32 * NCH, 1], [1, 32 * CPG]])
            p.op("gpsimd", lambda e, o=out_ap, i=in_ap: e.dma_start(out=o, in_=i),
                 reads=[bf("lrtmp")], writes=[bf("flat2")], dma=True, sem="g_flat")

    def scan_chunk(c, pump):
        g, sub = c // CPG, c % CPG
        slot = g % 2
        kTs, vTs, qTs = kT[slot], vT[slot], qT[slot]
        kTn, vTn, qTn = f"kT{slot}", f"vT{slot}", f"qT{slot}"
        bsl = c % 2
        bcn = f"bc_c{bsl}"

        def qkv_sl(tens, hh, k):
            return tens[64 * hh:64 * hh + 64, sub, k // 2, 64 * (k % 2):64 * (k % 2) + 64]

        mm(bank[7][0:128, 0:32], halfsel[:, :], flat2[0:2, 32 * c:32 * c + 32],
           "bank7", ["halfsel", "flat2"])
        vop(lambda e, o=bc_c[bsl][:, :], i_=bank[7][0:128, 0:32]: e.tensor_copy(o, i_),
            r=["bank7"], w=[bcn])

        for j in range(4):
            transpose_q(5, j, kTs[:, sub, j, :], [kTn])
        vop(lambda e: e.tensor_copy(ktok[:, :, :],
                                    bank[5][0:128, 0:512].rearrange("p (a b) -> p a b", b=128)),
            r=[f"bank5q{j}" for j in range(4)], w=["ktok"])
        for j in range(4):
            transpose_q(6, j, vTs[:, sub, j, :], [vTn])
        sop(lambda e: _copy(e, vtok[:, :, :],
                            bank[6][0:128, 0:512].rearrange("p (a b) -> p a b", b=128)),
            r=[f"bank6q{j}" for j in range(4)], w=["vtok"])

        for k in range(8):
            for hh in range(2):
                mm(bank[0][64 * hh:64 * hh + 64, 64 * k:64 * k + 64],
                   winT[64 * hh:64 * hh + 64, k, :], qkv_sl(kTs, hh, k),
                   "bank0", ["winT", kTn], tp=(64 * hh, 64 * hh))
        for k in range(8):
            for hh in range(2):
                mm(bank[1][64 * hh:64 * hh + 64, 64 * k:64 * k + 64],
                   winT[64 * hh:64 * hh + 64, k, :], qkv_sl(qTs, hh, k),
                   "bank1", ["winT", qTn], tp=(64 * hh, 64 * hh))
        for k in range(8):
            for hh in range(2):
                mm(bank[2][64 * hh:64 * hh + 64, 64 * k:64 * k + 64],
                   woutT[64 * hh:64 * hh + 64, k, :], qkv_sl(vTs, hh, k),
                   "bank2", ["woutT", vTn], tp=(64 * hh, 64 * hh))

        def softmax(bkidx, dst_ap, dstname):
            bkn, bk = f"bank{bkidx}", bank[bkidx]
            vop(lambda e, o=mx_s[:, :].unsqueeze(2),
                i_=bk[0:128, 0:512].rearrange("p (c n) -> p c n", n=CH):
                    e.reduce_max(o, i_, axis=AX.X), r=[bkn], w=["mx_s"], hz=True)
            mxb = bass.AP(mx_s, 0, [[8, 128], [1, 8], [0, CH]])
            vop(lambda e, o=zsub[:, :].rearrange("p (c n) -> p c n", n=CH),
                i0=bk[0:128, 0:512].rearrange("p (c n) -> p c n", n=CH), i1=mxb:
                    e.tensor_tensor(o, i0, i1, op=ALU.subtract),
                r=[bkn, "mx_s"], w=["zsub"])
            sop(lambda e: e.activation(expt[:, :], zsub[:, :], ACT.Exp),
                r=["zsub"], w=["expt"])
            vop(lambda e, o=sm_s[:, :].unsqueeze(2),
                i_=expt[:, :].rearrange("p (c n) -> p c n", n=CH):
                    e.reduce_sum(o, i_, axis=AX.X), r=["expt"], w=["sm_s"], hz=True)
            vop(lambda e: e.reciprocal(rc_s[:, :], sm_s[:, :]), r=["sm_s"], w=["rc_s"], hz=True)
            rcb = bass.AP(rc_s, 0, [[8, 128], [1, 8], [0, CH]])
            vop(lambda e, o=dst_ap.rearrange("p (c n) -> p c n", n=CH),
                i0=expt[:, :].rearrange("p (c n) -> p c n", n=CH), i1=rcb:
                    e.tensor_tensor(o, i0, i1, op=ALU.mult),
                r=["expt", "rc_s"], w=[dstname])

        softmax(0, pT[:, :], "pT")
        vop(lambda e: e.tensor_tensor(m_s[:, :], pT[:, :], bank[2][0:128, 0:512], op=ALU.mult),
            r=["pT", "bank2"], w=["m_s"])
        mm(bank[3][0:2, 0:512], ones2col[:, :], m_s[:, :], "bank3", ["ones2col", "m_s"])
        vop(lambda e: e.tensor_copy(s2s[:, :], bank[3][0:2, 0:512]), r=["bank3"], w=["s2s"])
        mm(bank[4][0:128, 0:512], halfsel[:, :], s2s[:, :], "bank4", ["halfsel", "s2s"])
        vop(lambda e: e.tensor_tensor(t1_s[:, :], pT[:, :], bank[4][0:128, 0:512], op=ALU.mult),
            r=["pT", "bank4"], w=["t1_s"])
        vop(lambda e: e.tensor_tensor(hh_s[:, :], m_s[:, :], t1_s[:, :], op=ALU.subtract),
            r=["m_s", "t1_s"], w=["hh_s"])
        lrin_b = bass.AP(bc_c[bsl], 0, [[32, 128], [1, 8], [0, CH]])
        lrout_b = bass.AP(bc_c[bsl], 8, [[32, 128], [1, 8], [0, CH]])
        vop(lambda e, i1=lrin_b: e.tensor_tensor(
                hh_s[:, :].rearrange("p (c n) -> p c n", n=CH),
                hh_s[:, :].rearrange("p (c n) -> p c n", n=CH), i1, op=ALU.mult),
            r=["hh_s", bcn], w=["hh_s"])
        vop(lambda e, i1=lrout_b: e.tensor_tensor(
                ph_s[:, :].rearrange("p (c n) -> p c n", n=CH),
                pT[:, :].rearrange("p (c n) -> p c n", n=CH), i1, op=ALU.mult),
            r=["pT", bcn], w=["ph_s"])
        for j in range(4):
            transpose_q(3, j, ph_s[:, 128 * j:128 * j + 128], ["ph_s"])
        vop(lambda e: e.tensor_copy(ptok[:, :, :],
                                    bank[3][0:128, 0:512].rearrange("p (a b) -> p a b", b=128)),
            r=[f"bank3q{j}" for j in range(4)], w=["ptok"])
        for j in range(4):
            transpose_q(4, j, hh_s[:, 128 * j:128 * j + 128], ["hh_s"])
        sop(lambda e: _copy(e, htok[:, :, :],
                            bank[4][0:128, 0:512].rearrange("p (a b) -> p a b", b=128)),
            r=[f"bank4q{j}" for j in range(4)], w=["htok"])
        softmax(1, pqT[:, :], "pqT")
        for k in range(8):
            par, j = k % 2, k // 2
            for hh in range(2):
                ksl = ktok[64 * par:64 * par + 64, j, 64 * hh:64 * hh + 64]
                vsl = vtok[64 * par:64 * par + 64, j, 64 * hh:64 * hh + 64]
                psl = ptok[64 * par:64 * par + 64, j, 64 * hh:64 * hh + 64]
                hsl = htok[64 * par:64 * par + 64, j, 64 * hh:64 * hh + 64]
                mm(bank[5][64 * hh:64 * hh + 64, 64 * k:64 * k + 64], ksl, hsl,
                   "bank5", ["ktok", "htok"], tp=(64 * par, 64 * hh))
                mm(bank[6][64 * hh:64 * hh + 64, 64 * k:64 * k + 64], psl, vsl,
                   "bank6", ["ptok", "vtok"], tp=(64 * par, 64 * hh))
                mm(bank[7][64 * hh:64 * hh + 64, 64 * k:64 * k + 64], vsl, psl,
                   "bank7", ["vtok", "ptok"], tp=(64 * par, 64 * hh))
        for k in range(8):
            for hh in range(2):
                mm(bank[1][64 * hh:64 * hh + 64, 64 * k:64 * k + 64],
                   wout[64 * hh:64 * hh + 64, k, :],
                   pqT[64 * hh:64 * hh + 64, 64 * k:64 * k + 64], "bank1",
                   ["wout", "pqT"], tp=(64 * hh, 64 * hh))
        vop(lambda e, o=ostg[:, :, slice(CH * sub, CH * (sub + 1))],
            i_=bank[1][0:128, 0:512].rearrange("p (a b) -> p a b", b=CH):
                e.tensor_copy(o, i_), r=["bank1"], w=["ostg"])
        wdin_b = bass.AP(bc_c[bsl], 16, [[32, 128], [1, 8], [0, 64]])
        wdout_b = bass.AP(bc_c[bsl], 24, [[32, 128], [1, 8], [0, 64]])
        for st_t, st_n, bkidx, wd_ap in [(winT, "winT", 5, wdin_b),
                                         (wout, "wout", 6, wdout_b),
                                         (woutT, "woutT", 7, wdout_b)]:
            gop(lambda e, o=st_t[:, :, :], i1=wd_ap:
                    e.tensor_tensor(o, o, i1, op=ALU.mult),
                r=[st_n, bcn], w=[st_n])
            vop(lambda e, o=st_t[:, :, :].rearrange("p a b -> p (a b)"), bk=bank[bkidx]:
                    e.tensor_tensor(o, bk[0:128, 0:512], o, op=ALU.add),
                r=[f"bank{bkidx}", st_n], w=[st_n])
        pump(8)
        if sub == CPG - 1:
            p.op("gpsimd", lambda e, o=oscr[:, :, G * g:G * (g + 1)], i=ostg[:, :, :]:
                    e.dma_start(out=o, in_=i),
                 reads=[bf("ostg")], writes=[bf("oscr")], dma=True, sem="g_oscr")

    pending = []

    def pump(n):
        done = 0
        while pending and done < n:
            try:
                next(pending[0])
                done += 1
            except StopIteration:
                pending.pop(0)

    for g in range(min(2, n_groups)):
        for _ in proj_group_gen(g):
            pass
    for c in range(n_chunks):
        g = c // CPG
        if c % CPG == 0:
            while pending:
                pump(100)
        scan_chunk(c, pump)
        if c % CPG == CPG - 1 and g + 2 < n_groups:
            pending.append(proj_group_gen(g + 2))
    while pending:
        pump(100)

    tapspec = {
        "kT0": (kT[0][:, :, :], "kT0"), "vT0": (vT[0][:, :, :], "vT0"),
        "qT0": (qT[0][:, :, :], "qT0"), "lrwd_c": (lrwd_c[:, :], "lrwd_c"),
        "flat2": (flat2[:, :], "flat2"), "winT": (winT[:, :, :], "winT"),
        "wout": (wout[:, :, :], "wout"), "woutT": (woutT[:, :, :], "woutT"),
        "ostg": (ostg[:, :, :], "ostg"), "pqT": (pqT[:, :], "pqT"),
        "pT": (pT[:, :], "pT"), "bc_c0": (bc_c[0][:, :], "bc_c0"),
        "xT": (xT[:, :, :], "xT"),
        "dbg1": (dbg1[:, :], "dbg1"), "dbg2": (dbg2[:, :], "dbg2"),
        "dbg3": (dbg3[:, :], "dbg3"),
    }
    tap_outs = {}
    if "oscr" in taps:
        td = nc.dram_tensor("tap_oscr", [128, 8, L], F32, kind="ExternalOutput").ap()
        tap_outs["oscr"] = (128, 8, L)
        p.op("gpsimd", lambda e, o=td, i=oscr[:, :, :]: e.dma_start(out=o, in_=i),
             reads=[bf("oscr")], dma=True, sem="g_out")
    for name in taps:
        if name not in tapspec:
            continue
        ap, bufname = tapspec[name]
        shp = list(ap.shape)
        td = nc.dram_tensor("tap_" + name, shp, F32, kind="ExternalOutput").ap()
        tap_outs[name] = tuple(shp)
        p.op("gpsimd", lambda e, o=td, i=ap: e.dma_start(out=o, in_=i),
             reads=[bf(bufname)], dma=True, sem="g_out")

    if do_post:
        last_by_sem = {}
        for o in p.ops:
            last_by_sem[o.sem] = o
        p.barrier = list(last_by_sem.values())
        esA.close()

    if do_post:
        import contextlib as _ctx
        esB1 = _ctx.ExitStack()
        sb = lambda name, shape: es.enter_context(nc.sbuf_tensor(name, list(shape), F32))
        sb1 = lambda name, shape: esB1.enter_context(nc.sbuf_tensor(name, list(shape), F32))
        x1T = sb("x1T", (128, 8, TOWN))
        acc = sb("acc", (128, NT, DIM))
        comb = sb("comb", (128, NT, 8))
        combT = sb("combT", (8, NT, 128))
        r_sP = [sb(f"r_s{i}", (128, DIM)) for i in range(2)]
        s1aP = [sb(f"s1a{i}", (128, 1)) for i in range(2)]
        s1bP = [sb(f"s1b{i}", (128, 1)) for i in range(2)]
        s2aP = [sb(f"s2a{i}", (128, 1)) for i in range(2)]
        s2bP = [sb(f"s2b{i}", (128, 1)) for i in range(2)]
        mvaP = [sb(f"mva{i}", (128, 1)) for i in range(2)]
        rstP = [sb(f"rst{i}", (128, 1)) for i in range(2)]
        g8P = [sb(f"g8{i}", (128, 8)) for i in range(2)]
        gscP = [sb(f"gsc{i}", (128, 8)) for i in range(2)]
        gm1P = [sb(f"gm1{i}", (128, 1)) for i in range(2)]
        ge1P = [sb(f"ge1{i}", (128, 8)) for i in range(2)]
        gs1P = [sb(f"gs1{i}", (128, 8)) for i in range(2)]
        gf1P = [sb(f"gf1{i}", (128, 8)) for i in range(2)]
        gp2P = [sb(f"gp2{i}", (128, 8)) for i in range(2)]
        gmaskP = [sb(f"gmask{i}", (128, 8)) for i in range(2)]
        outtP = [sb(f"outt{i}", (128, DIM)) for i in range(2)]
        gw_s = sb("gw_s", (128, 8, 8))
        woT_s = sb1("woT_s", (128, 8, DIM))
        otA = [sb1(f"otA{i}", (128, 8, 128)) for i in range(2)]
        otB = [sb1("otB0", (128, 8, 128))] * 2
        xres = [sb1("xres0", (128, DIM))] * 2

        dma_in("woT_s", woT_s[:], woT_d.rearrange("(k p) f -> p k f", p=128))
        dma_in("gw_s", gw_s[:], gw_d.rearrange("(k p) e -> p k e", p=128))


        def post_tile(t):
            slot = t % 2
            r_s, s1a, s1b = r_sP[slot], s1aP[slot], s1bP[slot]
            s2a, s2b, mva, rst = s2aP[slot], s2bP[slot], mvaP[slot], rstP[slot]
            g8, gsc, gm1, ge1 = g8P[slot], gscP[slot], gm1P[slot], ge1P[slot]
            gs1, gf1, gp2, gmask = gs1P[slot], gf1P[slot], gp2P[slot], gmaskP[slot]
            S = str(slot)
            dma_in(f"otA{slot}", otA[slot][:], oscr[:, :, 128 * t:128 * t + 128],
                   sem=f"s_otA{slot}")
            dma_in("otB0", otB[slot][:], oscr[:, :, TOWN + 128 * t:TOWN + 128 * t + 128],
                   sem="s_otB0")
            dma_in("xres0", xres[slot][:], xown_d[128 * t:128 * t + 128, :],
                   sem="s_xres0")
            ots_flat = otA[slot][:, :, :].rearrange("p a b -> p (a b)")
            vop(lambda e, o=ots_flat, sc=msel[:, 1:2]:
                    e.tensor_scalar(o, o, sc, None, op0=ALU.mult),
                r=[f"otA{slot}", "msel"], w=[f"otA{slot}"])
            vop(lambda e, o=ots_flat, i0=otB[slot][:, :, :].rearrange("p a b -> p (a b)"),
                sc=msel[:, 0:1]:
                    e.scalar_tensor_tensor(o, i0, sc, o, op0=ALU.mult, op1=ALU.add),
                r=["otB0", "msel", f"otA{slot}"], w=[f"otA{slot}"])
            for fh in range(2):
                for oc in range(8):
                    mm(bank[fh][0:128, 0:512], otA[slot][:, oc, :],
                       woT_s[:, oc, 512 * fh:512 * fh + 512], f"bank{fh}",
                       [f"otA{slot}", "woT_s"], start=(oc == 0), stop=(oc == 7))

            for fh in range(2):
                vop(lambda e, o=r_s[:, 512 * fh:512 * fh + 512], i0=bank[fh][0:128, 0:512],
                    i1=xres[slot][:, 512 * fh:512 * fh + 512]:
                        e.tensor_tensor(o, i0, i1, op=ALU.add),
                    r=[f"bank{fh}", "xres0"], w=["r_s" + S])
            vop(lambda e: e.tensor_reduce(s1a[:, :], r_s[:, :], axis=AX.X, op=ALU.add),
                r=["r_s" + S], w=["s1a" + S], hz=True)
            vop(lambda e: e.tensor_scalar(mva[:, :], s1a[:, :], -1.0 / DIM, None, op0=ALU.mult),
                r=["s1a" + S], w=["mva" + S])
            for fh in range(2):
                nm = ("s2a" if fh == 0 else "s2b") + S
                a_out = s2a if fh == 0 else s2b
                sop(lambda e, i=r_s[:, 512 * fh:512 * fh + 512], a=a_out:
                        e.activation(bank[6][0:128, 0:512], i, ACT.Square,
                                     bias=mva[:, :], accum_out=a[:, :]),
                    r=["r_s" + S, "mva" + S], w=["bank6", nm], hz=True)
            vop(lambda e: e.tensor_tensor(s2a[:, :], s2a[:, :], s2b[:, :], op=ALU.add),
                r=["s2a" + S, "s2b" + S], w=["s2a" + S])
            sop(lambda e: e.activation(rst[:, :], s2a[:, :], ACT.Sqrt,
                                       bias=float(EPS_LN), scale=1.0 / DIM),
                r=["s2a" + S, "epsv"], w=["rst" + S])
            vop(lambda e: e.reciprocal(rst[:, :], rst[:, :]), r=["rst" + S], w=["rst" + S], hz=True)
            vop(lambda e, o=acc[:, t, :]:
                    e.tensor_scalar(o, r_s[:, :], mva[:, :], rst[:, :],
                                    op0=ALU.add, op1=ALU.mult),
                r=["r_s" + S, "mva" + S, "rst" + S], w=["acc"])
            for dk in range(8):
                bkn = f"bank{2 + dk % 2}"
                bk = bank[2 + dk % 2]
                col = 128 * ((dk // 2) % 2)
                transpose(bk[0:128, col:col + 128], acc[:, t, 128 * dk:128 * dk + 128],
                          bkn, ["acc"])
                vop(lambda e, o=x1T[:, dk, 128 * t:128 * t + 128], i_=bk[0:128, col:col + 128]:
                        e.tensor_copy(o, i_), r=[bkn], w=["x1T"])
            for dk in range(8):
                mm(bank[4][0:128, 0:8], x1T[:, dk, 128 * t:128 * t + 128], gw_s[:, dk, :],
                   "bank4", ["x1T", "gw_s"], start=(dk == 0), stop=False)
            mm(bank[4][0:128, 0:8], onesrow[:, :], gb_s[:, :], "bank4",
               ["onesrow", "gb_s"], start=False, stop=True)
            vop(lambda e: e.tensor_scalar(gsc[:, :], bank[4][0:128, 0:8], float(GEPS), None,
                                          op0=ALU.max), r=["bank4"], w=["gsc" + S])
            vop(lambda e: e.reduce_max(gm1[:, :], gsc[:, :], axis=AX.X), r=["gsc" + S], w=["gm1" + S], hz=True)
            vop(lambda e: e.tensor_scalar(gm1[:, :], gm1[:, :], -1.0, None, op0=ALU.mult),
                r=["gm1" + S], w=["gm1" + S])
            sop(lambda e: e.activation(g8[:, :], gsc[:, :], ACT.Exp, bias=gm1[:, :]),
                r=["gsc" + S, "gm1" + S], w=["g8" + S])
            vop(lambda e: e.reduce_sum(gm1[:, :], g8[:, :], axis=AX.X), r=["g8" + S], w=["gm1" + S], hz=True)
            vop(lambda e: e.reciprocal(gm1[:, :], gm1[:, :]), r=["gm1" + S], w=["gm1" + S], hz=True)
            vop(lambda e: e.tensor_scalar(g8[:, :], g8[:, :], gm1[:, :], None, op0=ALU.mult),
                r=["g8" + S, "gm1" + S], w=["g8" + S])
            vop(lambda e: e.reduce_max(gm1[:, :], g8[:, :], axis=AX.X), r=["g8" + S], w=["gm1" + S], hz=True)
            gm1b = bass.AP(gm1, 0, [[1, 128], [1, 1], [0, 8]])
            vop(lambda e, i1=gm1b: e.tensor_tensor(ge1[:, :].unsqueeze(1),
                g8[:, :].unsqueeze(1), i1, op=ALU.is_ge), r=["g8" + S, "gm1" + S], w=["ge1" + S])
            vop(lambda e: e.tensor_tensor(gs1[:, :], ge1[:, :], pow2[:, :], op=ALU.mult),
                r=["ge1" + S, "pow2"], w=["gs1" + S])
            vop(lambda e: e.reduce_max(gm1[:, :], gs1[:, :], axis=AX.X), r=["gs1" + S], w=["gm1" + S], hz=True)
            gm1b2 = bass.AP(gm1, 0, [[1, 128], [1, 1], [0, 8]])
            vop(lambda e, i1=gm1b2: e.tensor_tensor(gf1[:, :].unsqueeze(1),
                gs1[:, :].unsqueeze(1), i1, op=ALU.is_ge), r=["gs1" + S, "gm1" + S], w=["gf1" + S])
            vop(lambda e: e.scalar_tensor_tensor(gp2[:, :], gf1[:, :], -2.0, g8[:, :],
                                                 op0=ALU.mult, op1=ALU.add),
                r=["gf1" + S, "g8" + S], w=["gp2" + S])
            vop(lambda e: e.reduce_max(gm1[:, :], gp2[:, :], axis=AX.X), r=["gp2" + S], w=["gm1" + S], hz=True)
            gm1b3 = bass.AP(gm1, 0, [[1, 128], [1, 1], [0, 8]])
            vop(lambda e, i1=gm1b3: e.tensor_tensor(ge1[:, :].unsqueeze(1),
                gp2[:, :].unsqueeze(1), i1, op=ALU.is_ge), r=["gp2" + S, "gm1" + S], w=["ge1" + S])
            vop(lambda e: e.tensor_tensor(gs1[:, :], ge1[:, :], pow2[:, :], op=ALU.mult),
                r=["ge1" + S, "pow2"], w=["gs1" + S])
            vop(lambda e: e.reduce_max(gm1[:, :], gs1[:, :], axis=AX.X), r=["gs1" + S], w=["gm1" + S], hz=True)
            gm1b4 = bass.AP(gm1, 0, [[1, 128], [1, 1], [0, 8]])
            vop(lambda e, i1=gm1b4: e.tensor_tensor(ge1[:, :].unsqueeze(1),
                gs1[:, :].unsqueeze(1), i1, op=ALU.is_ge), r=["gs1" + S, "gm1" + S], w=["ge1" + S])
            vop(lambda e: e.tensor_tensor(gmask[:, :], gf1[:, :], ge1[:, :], op=ALU.add),
                r=["gf1" + S, "ge1" + S], w=["gmask" + S])
            vop(lambda e: e.tensor_scalar(gs1[:, :], g8[:, :], float(WTHR), None, op0=ALU.max),
                r=["g8" + S], w=["gs1" + S])
            vop(lambda e: e.tensor_tensor(gs1[:, :], gs1[:, :], gmask[:, :], op=ALU.mult),
                r=["gs1" + S, "gmask" + S], w=["gs1" + S])
            vop(lambda e: e.reduce_sum(gm1[:, :], gs1[:, :], axis=AX.X), r=["gs1" + S], w=["gm1" + S], hz=True)
            vop(lambda e: e.reciprocal(gm1[:, :], gm1[:, :]), r=["gm1" + S], w=["gm1" + S], hz=True)
            vop(lambda e, o=comb[:, t, :]: e.tensor_scalar(o, gs1[:, :], gm1[:, :], 0.5,
                                                           op0=ALU.mult, op1=ALU.mult),
                r=["gs1" + S, "gm1" + S], w=["comb"])
            transpose(bank[5][0:8, 0:128], comb[:, t, :], "bank5", ["comb"])
            vop(lambda e, o=combT[:, t, :]: e.tensor_copy(o, bank[5][0:8, 0:128]),
                r=["bank5"], w=["combT"])
            for fh in range(2):
                mm(bank[6][0:128, 0:512], combT[:, t, :], eb_s[:, 512 * fh:512 * fh + 512],
                   "bank6", ["combT", "eb_s"])
                vop(lambda e, o=acc[:, t, 512 * fh:512 * fh + 512], i0=bank[6][0:128, 0:512]:
                        e.tensor_tensor(o, i0, o, op=ALU.add),
                    r=["bank6", "acc"], w=["acc"])

        for t in range(NT):
            post_tile(t)

        last_by_sem = {}
        for o in p.ops:
            last_by_sem[o.sem] = o
        p.barrier = list(last_by_sem.values())
        p.barrier_applied = set()
        esB1.close()
        ew_s = [sb(f"ew_s{i}", (128, 8, 512)) for i in range(2)]

        for e_i in range(NEXP):
            for fh in range(2):
                slot = (e_i * 2 + fh) % 2
                dma_in(f"ew_s{slot}", ew_s[slot][:],
                       ew_d[e_i, :, 512 * fh:512 * fh + 512].rearrange("(k p) f -> p k f", p=128),
                       sem=f"s_ew{slot}")
                for t in range(NT):
                    bkn = f"bank{t % 4}"
                    bk = bank[t % 4]
                    for dk in range(8):
                        mm(bk[0:128, 0:512], x1T[:, dk, 128 * t:128 * t + 128],
                           ew_s[slot][:, dk, :], bkn, ["x1T", f"ew_s{slot}"],
                           start=(dk == 0), stop=(dk == 7))
                    vop(lambda e, o=acc[:, t, 512 * fh:512 * fh + 512], i0=bk[0:128, 0:512],
                        sc=comb[:, t, e_i:e_i + 1]:
                            e.scalar_tensor_tensor(o, i0, sc, o, op0=ALU.mult, op1=ALU.add),
                        r=[bkn, "comb", "acc"], w=["acc"])

        for t in range(NT):
            S = str(t % 2)
            r_s, s1a, s1b = r_sP[t % 2], s1aP[t % 2], s1bP[t % 2]
            s2a, s2b, mva, rst = s2aP[t % 2], s2bP[t % 2], mvaP[t % 2], rstP[t % 2]
            outt = outtP[t % 2]
            for fh in range(2):
                nm = ("s1a" if fh == 0 else "s1b") + S
                a_out = s1a if fh == 0 else s1b
                vop(lambda e, i=acc[:, t, 512 * fh:512 * fh + 512], a=a_out:
                        e.tensor_reduce(a[:, :], i, axis=AX.X, op=ALU.add),
                    r=["acc"], w=[nm], hz=True)
            vop(lambda e, a=s1a, b=s1b: e.tensor_tensor(a[:, :], a[:, :], b[:, :], op=ALU.add),
                r=["s1a" + S, "s1b" + S], w=["s1a" + S])
            vop(lambda e, m=mva, a=s1a: e.tensor_scalar(m[:, :], a[:, :], -1.0 / DIM, None,
                                                        op0=ALU.mult),
                r=["s1a" + S], w=["mva" + S])
            for fh in range(2):
                nm = ("s2a" if fh == 0 else "s2b") + S
                a_out = s2a if fh == 0 else s2b
                sop(lambda e, i=acc[:, t, 512 * fh:512 * fh + 512], a=a_out, m=mva:
                        e.activation(bank[6][0:128, 0:512], i, ACT.Square,
                                     bias=m[:, :], accum_out=a[:, :]),
                    r=["acc", "mva" + S], w=["bank6", nm], hz=True)
            vop(lambda e, a=s2a, b=s2b: e.tensor_tensor(a[:, :], a[:, :], b[:, :], op=ALU.add),
                r=["s2a" + S, "s2b" + S], w=["s2a" + S])
            sop(lambda e, rr=rst, a=s2a: e.activation(rr[:, :], a[:, :], ACT.Sqrt,
                                                      bias=float(EPS_LN), scale=1.0 / DIM),
                r=["s2a" + S, "epsv"], w=["rst" + S])
            vop(lambda e, rr=rst: e.reciprocal(rr[:, :], rr[:, :]),
                r=["rst" + S], w=["rst" + S], hz=True)
            vop(lambda e, o=outt[:, :], i=acc[:, t, :], m=mva, rr=rst:
                    e.tensor_scalar(o, i, m[:, :], rr[:, :], op0=ALU.add, op1=ALU.mult),
                r=["acc", "mva" + S, "rst" + S], w=["outt" + S])
            p.op("gpsimd", lambda e, o=out_d[128 * t:128 * t + 128, :], i=outt[:, :]:
                    e.dma_start(out=o, in_=i),
                 reads=[bf("outt" + S)], writes=[bf("out_d")], dma=True, sem="g_out")

    if do_post:
        tapspecB = {"acc": (acc[:, :, :], "acc"), "comb": (comb[:, :, :], "comb"),
                    "x1T": (x1T[:, :, :], "x1T")}
        for name in taps:
            if name not in tapspecB:
                continue
            ap, bufname = tapspecB[name]
            shp = list(ap.shape)
            td = nc.dram_tensor("tap_" + name, shp, F32, kind="ExternalOutput").ap()
            tap_outs[name] = tuple(shp)
            p.op("gpsimd", lambda e, o=td, i=ap: e.dma_start(out=o, in_=i),
                 reads=[bf(bufname)], dma=True, sem="g_out")

    final_counts = p.plan()
    need = p.needed_posts(final_counts)
    sems = {s: es.enter_context(nc.semaphore("sem_" + s)) for s in final_counts}

    with nc.Block() as block:
        @block.sync
        def _(eng):
            p.emit("sync", eng, sems, need)
            for s2 in sorted(final_counts):
                if s2 != "sync" and final_counts[s2]:
                    eng.wait_ge(sems[s2], final_counts[s2])

        @block.tensor
        def _(eng):
            p.emit("tensor", eng, sems, need)

        @block.vector
        def _(eng):
            p.emit("vector", eng, sems, need)

        @block.scalar
        def _(eng):
            p.emit("scalar", eng, sems, need)

        @block.gpsimd
        def _(eng):
            p.emit("gpsimd", eng, sems, need)

    if not do_post:
        esA.close()
    es.close()
    return nc, tap_outs


def make_in_maps(inputs):
    prep = host_prep(inputs)
    x = np.ascontiguousarray(np.asarray(inputs["L0_attn_x"], dtype=np.float32))
    in_maps = []
    for c in range(8):
        b, half = c // 2, c % 2
        ms = np.zeros((128, 2), np.float32)
        ms[:, 0] = float(half)
        ms[:, 1] = 1.0 - float(half)
        m = {"x": np.ascontiguousarray(x[b]),
             "x_own": np.ascontiguousarray(x[b, TOWN * half:TOWN * (half + 1), :]),
             "mselw": ms}
        m.update(prep)
        in_maps.append(m)
    return in_maps


def kernel(**inputs):
    in_maps = make_in_maps(inputs)
    nc, _ = build_core()
    mybir.codegen_inst_isa_subclasses(nc)
    res = run_bass_kernel_spmd(nc, in_maps, core_ids=list(range(8)))
    out = np.empty((B, L, DIM), np.float32)
    for c in range(8):
        b, half = c // 2, c % 2
        out[b, TOWN * half:TOWN * (half + 1), :] = np.asarray(res.results[c]["out"])
    return out


# version 12 (final)
# speedup vs baseline: 1.5354x; 1.2822x over previous
import numpy as np
import concourse.bass as bass
import concourse.mybir as mybir
from concourse.bass_utils import run_bass_kernel_spmd

F32 = mybir.dt.float32
AX = mybir.AxisListType
ALU = mybir.AluOpType
ACT = mybir.ActivationFunctionType

B, L, DIM = 4, 4096, 1024
CH = 64
NCH = L // CH
G = 256
NG = L // G
CPG = G // CH
NEXP = 8
TOWN = L // 2
NT = TOWN // 128
EPS_LN = 1e-5
BASE_LR, BASE_WD = 1e-3, 0.9
GEPS, WTHR = 1e-4, 0.1
SCAN_STAGE = 6


class _Op:
    __slots__ = ("eng", "emit", "deps", "inc", "post", "sem", "hz")

    def __init__(self, eng, emit, deps, inc, sem, hz):
        self.eng, self.emit, self.deps, self.inc, self.sem = eng, emit, deps, inc, sem
        self.hz = hz
        self.post = None


class Buf:
    __slots__ = ("writer", "readers")

    def __init__(self):
        self.writer = None
        self.readers = []


class Prog:
    ENGS = ("sync", "tensor", "vector", "scalar", "gpsimd")

    def __init__(self):
        self.ops = []
        self.barrier = None
        self.barrier_applied = set()

    def op(self, eng, emit, reads=(), writes=(), deps=(), dma=False, sem=None, hz=False):
        dd = list(deps)
        if self.barrier is not None and eng not in self.barrier_applied:
            dd.extend(self.barrier)
            self.barrier_applied.add(eng)
        for b in reads:
            if b.writer is not None:
                dd.append(b.writer)
        for b in writes:
            dd.extend(b.readers)
            if b.writer is not None:
                dd.append(b.writer)
        assert (sem is not None) == dma, "DMA ops need an explicit slot sem"
        o = _Op(eng, emit, dd, 16 if dma else 1, sem if dma else eng, hz)
        self.ops.append(o)
        for b in reads:
            b.readers.append(o)
        for b in writes:
            b.writer = o
            b.readers = []
        return o

    def plan(self):
        cnt = {}
        for o in self.ops:
            cnt[o.sem] = cnt.get(o.sem, 0) + o.inc
            o.post = cnt[o.sem]
        return cnt

    def _walk(self, eng_name):
        watermark = {}
        for o in self.ops:
            if o.eng != eng_name:
                continue
            best = {}
            for d in o.deps:
                if (d.eng == eng_name == "tensor" and o.eng == "tensor"
                        and d.sem == "tensor" and not d.hz):
                    continue
                if best.get(d.sem, -1) < d.post:
                    best[d.sem] = d.post
            waits = []
            for s2 in sorted(best):
                v = best[s2]
                if watermark.get(s2, 0) >= v:
                    continue
                waits.append((s2, v))
                watermark[s2] = v
            yield o, waits

    def needed_posts(self, final_counts):
        need = set()
        for e in self.ENGS:
            for o, waits in self._walk(e):
                for s2, v in waits:
                    need.add((s2, v))
        for s2, v in final_counts.items():
            need.add((s2, v))
        return need

    def emit(self, eng_name, eng, sems, need):
        pending = 0
        for o, waits in self._walk(eng_name):
            for s2, v in waits:
                eng.wait_ge(sems[s2], v)
            inst = o.emit(eng)
            if o.inc == 16:
                inst.then_inc(sems[o.sem], 16)
            else:
                pending += o.inc
                if (o.sem, o.post) in need:
                    inst.then_inc(sems[o.sem], pending)
                    pending = 0


def host_prep_shared(inp):
    f32c = lambda a: np.ascontiguousarray(np.asarray(a), dtype=np.float32)
    d = {}
    idx = np.empty(DIM, np.int64)
    for r in range(2):
        for k in range(4):
            for c2 in range(2):
                h = 8 * r + 4 * c2 + k
                base = (r * 4 + k) * 128 + c2 * 64
                idx[base:base + 64] = np.arange(h * 64, h * 64 + 64)
    d["woT"] = f32c(np.asarray(inp["L0_attn_Wo"]).T[idx, :])
    d["gw"] = f32c(inp["L0_ffn_gate_W"])
    d["gb"] = f32c(np.asarray(inp["L0_ffn_gate_b"]).reshape(1, 8))
    d["ew"] = f32c(inp["L0_ffn_expert_W"])
    d["eb"] = f32c(inp["L0_ffn_expert_b"])
    d["identw"] = np.eye(128, dtype=np.float32)
    oc2 = np.zeros((128, 2), np.float32)
    oc2[:64, 0] = 1.0
    oc2[64:, 1] = 1.0
    d["ones2colw"] = oc2
    hs = np.zeros((2, 128), np.float32)
    hs[0, :64] = 1.0
    hs[1, 64:] = 1.0
    d["halfselw"] = hs
    d["onesroww"] = np.ones((1, 128), np.float32)
    d["pow2w"] = np.ascontiguousarray(
        np.broadcast_to(2.0 ** -np.arange(8, dtype=np.float32), (128, 8)))
    c64 = np.empty((32, 1), np.float32)
    c64[:16] = BASE_LR / 64.0
    c64[16:] = BASE_WD / 64.0
    d["c64w"] = c64
    d["epsw"] = np.full((128, 1), EPS_LN, np.float32)
    return d


def host_prep_half(inp, half):
    f32c = lambda a: np.ascontiguousarray(np.asarray(a), dtype=np.float32)
    idx = np.empty(512, np.int64)
    for oc in range(4):
        for c2 in range(2):
            h = 8 * half + oc + 4 * c2
            idx[oc * 128 + c2 * 64: oc * 128 + c2 * 64 + 64] = np.arange(h * 64, h * 64 + 64)
    d = {}
    d["wqT"] = f32c(np.asarray(inp["L0_attn_Wq"]).T[:, idx])
    d["wkT"] = f32c(np.asarray(inp["L0_attn_Wk"]).T[:, idx])
    d["wvT"] = f32c(np.asarray(inp["L0_attn_Wv"]).T[:, idx])
    lrl = np.concatenate([np.asarray(inp["L0_attn_Wlr"]).T,
                          np.asarray(inp["L0_attn_Wbeta"]).T], axis=1)
    cols = []
    for q in range(4):
        for l in range(8):
            cols.append(q * 16 + 8 * half + l)
    d["wlbT"] = f32c(lrl[:, cols])
    wi = np.asarray(inp["L0_attn_W_in_init"])[0]
    wo = np.asarray(inp["L0_attn_W_out_init"])[0]

    def compact(arr_hab):
        a = np.asarray(arr_hab).reshape(2, 4, 64, 64)
        return f32c(a.transpose(0, 2, 1, 3).reshape(128, 4, 64))

    hs = slice(8 * half, 8 * half + 8)
    d["winT0"] = compact(wi.transpose(1, 2, 0)[hs])
    d["wout0"] = compact(wo.transpose(1, 0, 2)[hs])
    d["woutT0"] = compact(wo.transpose(1, 2, 0)[hs])
    return d


def build_core(n_groups=NG, n_chunks=NCH, do_post=True, taps=()):
    nc = bass.Bass("TRN2", target_bir_lowering=False, detect_race_conditions=False)
    p = Prog()
    bufs = {}

    def bf(name):
        if name not in bufs:
            bufs[name] = Buf()
        return bufs[name]

    def dram_in(name, shape):
        return nc.dram_tensor(name, list(shape), F32, kind="ExternalInput").ap()

    xb = dram_in("x", (L, DIM))
    xown_d = dram_in("x_own", (TOWN, DIM))
    msel_d = dram_in("mselw", (128, 2))
    wqT = dram_in("wqT", (DIM, 512))
    wkT = dram_in("wkT", (DIM, 512))
    wvT = dram_in("wvT", (DIM, 512))
    wlbT_d = dram_in("wlbT", (DIM, 32))
    woT_d = dram_in("woT", (DIM, DIM))
    winT0 = dram_in("winT0", (128, 4, 64))
    wout0 = dram_in("wout0", (128, 4, 64))
    woutT0 = dram_in("woutT0", (128, 4, 64))
    gw_d = dram_in("gw", (DIM, 8))
    gb_d = dram_in("gb", (1, 8))
    ew_d = dram_in("ew", (NEXP, DIM, DIM))
    eb_d = dram_in("eb", (NEXP, DIM))
    ident_d = dram_in("identw", (128, 128))
    ones2col_d = dram_in("ones2colw", (128, 2))
    halfsel_d = dram_in("halfselw", (2, 128))
    onesrow_d = dram_in("onesroww", (1, 128))
    pow2_d = dram_in("pow2w", (128, 8))
    c64_d = dram_in("c64w", (32, 1))
    eps_d = dram_in("epsw", (128, 1))

    out_d = nc.dram_tensor("out", [TOWN, DIM], F32, kind="ExternalOutput").ap()
    oscr = nc.dram_tensor("oscr", [128, 4, L], F32, kind="Internal").ap()
    oex = nc.dram_tensor("oex", [2, 128, 4, L], F32, kind="Internal").ap()

    import contextlib
    es = contextlib.ExitStack()
    esA = contextlib.ExitStack()
    sbP = lambda name, shape: es.enter_context(nc.sbuf_tensor(name, list(shape), F32))
    sb = lambda name, shape: esA.enter_context(nc.sbuf_tensor(name, list(shape), F32))
    psum = lambda name, shape: es.enter_context(nc.psum_tensor(name, list(shape), F32))

    ident = sbP("ident", (128, 128))
    onesrow = sbP("onesrow", (1, 128))
    pow2 = sbP("pow2", (128, 8))
    gb_s = sbP("gb_s", (1, 8))
    eb_s = sbP("eb_s", (8, DIM))
    msel = sbP("msel", (128, 2))
    epsv = sbP("epsv", (128, 1))
    nc.const_aps.register(F32, float(EPS_LN), epsv[:, 0:1])

    wq_s = sb("wq_s", (128, 8, 512))
    wk_s = sb("wk_s", (128, 8, 512))
    wlb_s = sb("wlb_s", (128, 8, 32))
    wv_s = [sb(f"wv_s{i}", (128, 8, 128)) for i in range(2)]
    ones2col = sb("ones2col", (128, 2))
    halfsel = sb("halfsel", (2, 128))
    c64 = sb("c64", (32, 1))

    winT = sb("winT", (128, 4, 64))
    wout = sb("wout", (128, 4, 64))
    woutT = sb("woutT", (128, 4, 64))

    xrow = [sb(f"xrow{i}", (128, 2, DIM)) for i in range(2)]
    xT = sb("xT", (128, 8, G))
    qT = [sb(f"qT{i}", (128, CPG, 2, 128)) for i in range(2)]
    kT = [sb(f"kT{i}", (128, CPG, 2, 128)) for i in range(2)]
    vT = [sb(f"vT{i}", (128, CPG, 2, 128)) for i in range(2)]
    lrwdT = sb("lrwdT", (32, G))
    lrwd_c = sb("lrwd_c", (32, NCH))
    lrtmp = sb("lrtmp", (4, 32))
    flat2 = sb("flat2", (2, NCH * 16))
    bc_c = [sb(f"bc_c{i}", (128, 16)) for i in range(2)]

    pqT = sb("pqT", (128, 256))
    pT = sb("pT", (128, 256))
    zsub = sb("zsub", (128, 256))
    expt = sb("expt", (128, 256))
    m_s = sb("m_s", (128, 256))
    t1_s = sb("t1_s", (128, 256))
    hh_s = sb("hh_s", (128, 256))
    ph_s = sb("ph_s", (128, 256))
    s2s = sb("s2s", (2, 256))
    mx_s = sb("mx_s", (128, 4))
    sm_s = sb("sm_s", (128, 4))
    rc_s = sb("rc_s", (128, 4))
    ktok = sb("ktok", (128, 2, 128))
    vtok = sb("vtok", (128, 2, 128))
    ptok = sb("ptok", (128, 2, 128))
    htok = sb("htok", (128, 2, 128))
    ostg = sb("ostg", (128, 4, G))
    dbg1 = sb("dbg1", (128, 512))
    dbg2 = sb("dbg2", (128, 512))
    dbg3 = sb("dbg3", (128, 512))

    bank = [psum(f"bank{i}", (128, 512)) for i in range(8)]

    def dma_in(bufname, out_ap, in_ap, eng="sync", extra_reads=(), sem="s_init"):
        return p.op(eng, lambda e, o=out_ap, i=in_ap: e.dma_start(out=o, in_=i),
                    reads=list(extra_reads), writes=[bf(bufname)], dma=True, sem=sem)

    dma_in("wq_s", wq_s[:], wqT.rearrange("(k p) f -> p k f", p=128))
    dma_in("wk_s", wk_s[:], wkT.rearrange("(k p) f -> p k f", p=128))
    dma_in("wlb_s", wlb_s[:], wlbT_d.rearrange("(k p) f -> p k f", p=128))
    for name, s_t, d_t in [("ident", ident, ident_d), ("ones2col", ones2col, ones2col_d),
                           ("halfsel", halfsel, halfsel_d), ("onesrow", onesrow, onesrow_d),
                           ("pow2", pow2, pow2_d), ("c64", c64, c64_d),
                           ("gb_s", gb_s, gb_d), ("eb_s", eb_s, eb_d), ("msel", msel, msel_d),
                           ("epsv", epsv, eps_d)]:
        dma_in(name, s_t[:], d_t[:])
    dma_in("winT", winT[:], winT0[:])
    dma_in("wout", wout[:], wout0[:])
    dma_in("woutT", woutT[:], woutT0[:])

    def mm(out_ap, lhsT_ap, rhs_ap, wbankname, rbufnames, start=True, stop=True, tp=None):
        wr = [bf(wbankname)]
        if wbankname in ("bank3", "bank4", "bank5", "bank6"):
            wr += [bf(wbankname + f"q{j}") for j in range(4)]
        return p.op("tensor",
                    lambda e, o=out_ap, a=lhsT_ap, b=rhs_ap, s=start, t=stop, q=tp:
                        e.matmul(o, a, b, start=s, stop=t, tile_position=q),
                    reads=[bf(n) for n in rbufnames], writes=wr)

    def transpose(out_ap, in_ap, wbankname, rbufnames):
        np_ = in_ap.shape[0]
        return p.op("tensor",
                    lambda e, o=out_ap, i=in_ap, n_=np_: e.transpose(o, i, ident[0:n_, 0:n_]),
                    reads=[bf(n) for n in rbufnames] + [bf("ident")],
                    writes=[bf(wbankname)])

    def transpose_q(bkidx, j, in_ap, rbufnames):
        np_ = in_ap.shape[0]
        out_ap = bank[bkidx][0:128, 128 * j:128 * j + 128]
        return p.op("tensor",
                    lambda e, o=out_ap, i=in_ap, n_=np_: e.transpose(o, i, ident[0:n_, 0:n_]),
                    reads=[bf(n) for n in rbufnames] + [bf("ident"), bf(f"bank{bkidx}")],
                    writes=[bf(f"bank{bkidx}q{j}")])

    def bank_w(bkidx):
        return [bf(f"bank{bkidx}")] + [bf(f"bank{bkidx}q{j}") for j in range(4)]

    def vop(emit, r=(), w=(), hz=False):
        return p.op("vector", emit, reads=[bf(n) for n in r],
                    writes=[bf(n) for n in w], hz=hz)

    def sop(emit, r=(), w=(), hz=False):
        return p.op("scalar", emit, reads=[bf(n) for n in r],
                    writes=[bf(n) for n in w], hz=hz)

    def _copy(e, o, i_):
        return e.tensor_copy(o, i_) if hasattr(e, "tensor_copy") else e.copy(o, i_)

    def gop(emit, r=(), w=()):
        return p.op("gpsimd", emit, reads=[bf(n) for n in r], writes=[bf(n) for n in w])

    def proj_group_gen(g):
        slot = g % 2
        xr = xrow[slot]
        dma_in(f"xrow{slot}", xr[:], xb[G * g:G * (g + 1), :].rearrange("(t p) f -> p t f", p=128), sem=f"s_xrow{slot}")
        for quad in range(4):
            tt, dk0 = quad // 2, 4 * (quad % 2)
            bkidx = 3 + quad % 2
            for jj in range(4):
                transpose_q(bkidx, jj, xr[:, tt, 128 * (dk0 + jj):128 * (dk0 + jj) + 128],
                            [f"xrow{slot}"])
            cop = sop if (quad % 2) else vop
            xt_dst = bass.AP(xT, (4 * (quad % 2)) * G + 128 * tt,
                             [[8 * G, 128], [G, 4], [1, 128]])
            cop(lambda e, o=xt_dst, i_=bank[bkidx][0:128, 0:512]
                .rearrange("p (a b) -> p a b", b=128): _copy(e, o, i_),
                r=[f"bank{bkidx}q{j}" for j in range(4)], w=["xT"])
            yield

        def projmm(lhs_ap_fn, wname, dst_ap, dstname, act, bkidx):
            bkn, bk = f"bank{bkidx}", bank[bkidx]
            for dk in range(8):
                mm(bk[0:128, 0:G], lhs_ap_fn(dk), xT[:, dk, :], bkn, [wname, "xT"],
                   start=(dk == 0), stop=(dk == 7))
            src3 = bk[0:128, 0:G].rearrange("p (s n) -> p s n", n=CH)
            if act == "silu":
                sop(lambda e, o=dst_ap, i_=src3: e.activation(o, i_, ACT.Silu),
                    r=[bkn], w=[dstname])
            else:
                vop(lambda e, o=dst_ap, i_=src3: e.tensor_copy(o, i_),
                    r=[bkn], w=[dstname])

        def qkv_dst(tens, oc):
            return tens[:, 0:CPG, oc // 2, 64 * (oc % 2):64 * (oc % 2) + 64]

        for oc in range(4):
            projmm(lambda dk, oc=oc: wk_s[:, dk, 128 * oc:128 * oc + 128], "wk_s",
                   qkv_dst(kT[slot], oc), f"kT{slot}", "silu", 3 + oc % 2)
            yield
        for oc in range(4):
            dma_in(f"wv_s{oc % 2}", wv_s[oc % 2][:],
                   wvT.rearrange("(k p) f -> p k f", p=128)[:, :, 128 * oc:128 * oc + 128],
                   sem=f"s_wv{oc % 2}")
            projmm(lambda dk, oc=oc: wv_s[oc % 2][:, dk, :], f"wv_s{oc % 2}",
                   qkv_dst(vT[slot], oc), f"vT{slot}", "copy", 3 + oc % 2)
            yield
        for oc in range(4):
            projmm(lambda dk, oc=oc: wq_s[:, dk, 128 * oc:128 * oc + 128], "wq_s",
                   qkv_dst(qT[slot], oc), f"qT{slot}", "silu", 3 + oc % 2)
            yield
        bkn, bk = "bank3", bank[3]
        for dk in range(8):
            mm(bk[0:32, 0:G], wlb_s[:, dk, :], xT[:, dk, :], bkn, ["wlb_s", "xT"],
               start=(dk == 0), stop=(dk == 7))
        sop(lambda e, o=lrwdT[:, :], i_=bk[0:32, 0:G]: e.activation(o, i_, ACT.Sigmoid),
            r=[bkn], w=["lrwdT"])
        vop(lambda e, o=lrwd_c[:, CPG * g:CPG * (g + 1)].unsqueeze(2),
            i_=lrwdT[:, :].rearrange("p (c n) -> p c n", n=CH): e.reduce_sum(o, i_, axis=AX.X),
            r=["lrwdT"], w=["lrwd_c"], hz=True)
        vop(lambda e, o=lrwd_c[:, CPG * g:CPG * (g + 1)], i_=lrwd_c[:, CPG * g:CPG * (g + 1)]:
                e.tensor_scalar(o, i_, c64[:, :], None, op0=ALU.mult),
            r=["lrwd_c", "c64"], w=["lrwd_c"])
        transpose(bank[4][0:CPG, 0:32], lrwd_c[:, CPG * g:CPG * (g + 1)], "bank4", ["lrwd_c"])
        vop(lambda e, o=lrtmp[:, :], i_=bank[4][0:CPG, 0:32]: e.tensor_copy(o, i_),
            r=["bank4"], w=["lrtmp"])
        for hh in range(2):
            in_ap = bass.AP(lrtmp, 4 * hh, [[32, CPG], [8, 4], [1, 4]])
            out_ap = bass.AP(flat2, 16 * NCH * hh + 16 * CPG * g, [[16 * NCH, 1], [1, 16 * CPG]])
            p.op("gpsimd", lambda e, o=out_ap, i=in_ap: e.dma_start(out=o, in_=i),
                 reads=[bf("lrtmp")], writes=[bf("flat2")], dma=True, sem="g_flat")

    def scan_chunk(c, pump):
        g, sub = c // CPG, c % CPG
        slot = g % 2
        kTs, vTs, qTs = kT[slot], vT[slot], qT[slot]
        kTn, vTn, qTn = f"kT{slot}", f"vT{slot}", f"qT{slot}"
        bsl = c % 2
        bcn = f"bc_c{bsl}"

        def qkv_sl(tens, hh, k):
            return tens[64 * hh:64 * hh + 64, sub, k // 2, 64 * (k % 2):64 * (k % 2) + 64]

        mm(bank[7][0:128, 0:16], halfsel[:, :], flat2[0:2, 16 * c:16 * c + 16],
           "bank7", ["halfsel", "flat2"])
        vop(lambda e, o=bc_c[bsl][:, :], i_=bank[7][0:128, 0:16]: e.tensor_copy(o, i_),
            r=["bank7"], w=[bcn])

        for j in range(2):
            transpose_q(5, j, kTs[:, sub, j, :], [kTn])
        vop(lambda e: e.tensor_copy(ktok[:, :, :],
                                    bank[5][0:128, 0:256].rearrange("p (a b) -> p a b", b=128)),
            r=[f"bank5q{j}" for j in range(2)], w=["ktok"])
        for j in range(2):
            transpose_q(6, j, vTs[:, sub, j, :], [vTn])
        sop(lambda e: _copy(e, vtok[:, :, :],
                            bank[6][0:128, 0:256].rearrange("p (a b) -> p a b", b=128)),
            r=[f"bank6q{j}" for j in range(2)], w=["vtok"])

        for k in range(4):
            for hh in range(2):
                mm(bank[0][64 * hh:64 * hh + 64, 64 * k:64 * k + 64],
                   winT[64 * hh:64 * hh + 64, k, :], qkv_sl(kTs, hh, k),
                   "bank0", ["winT", kTn], tp=(64 * hh, 64 * hh))
        for k in range(4):
            for hh in range(2):
                mm(bank[1][64 * hh:64 * hh + 64, 64 * k:64 * k + 64],
                   winT[64 * hh:64 * hh + 64, k, :], qkv_sl(qTs, hh, k),
                   "bank1", ["winT", qTn], tp=(64 * hh, 64 * hh))
        for k in range(4):
            for hh in range(2):
                mm(bank[2][64 * hh:64 * hh + 64, 64 * k:64 * k + 64],
                   woutT[64 * hh:64 * hh + 64, k, :], qkv_sl(vTs, hh, k),
                   "bank2", ["woutT", vTn], tp=(64 * hh, 64 * hh))

        def softmax(bkidx, dst_ap, dstname):
            bkn, bk = f"bank{bkidx}", bank[bkidx]
            vop(lambda e, o=mx_s[:, :].unsqueeze(2),
                i_=bk[0:128, 0:256].rearrange("p (c n) -> p c n", n=CH):
                    e.reduce_max(o, i_, axis=AX.X), r=[bkn], w=["mx_s"], hz=True)
            mxb = bass.AP(mx_s, 0, [[4, 128], [1, 4], [0, CH]])
            vop(lambda e, o=zsub[:, :].rearrange("p (c n) -> p c n", n=CH),
                i0=bk[0:128, 0:256].rearrange("p (c n) -> p c n", n=CH), i1=mxb:
                    e.tensor_tensor(o, i0, i1, op=ALU.subtract),
                r=[bkn, "mx_s"], w=["zsub"])
            sop(lambda e: e.activation(expt[:, :], zsub[:, :], ACT.Exp),
                r=["zsub"], w=["expt"])
            vop(lambda e, o=sm_s[:, :].unsqueeze(2),
                i_=expt[:, :].rearrange("p (c n) -> p c n", n=CH):
                    e.reduce_sum(o, i_, axis=AX.X), r=["expt"], w=["sm_s"], hz=True)
            vop(lambda e: e.reciprocal(rc_s[:, :], sm_s[:, :]), r=["sm_s"], w=["rc_s"], hz=True)
            rcb = bass.AP(rc_s, 0, [[4, 128], [1, 4], [0, CH]])
            vop(lambda e, o=dst_ap.rearrange("p (c n) -> p c n", n=CH),
                i0=expt[:, :].rearrange("p (c n) -> p c n", n=CH), i1=rcb:
                    e.tensor_tensor(o, i0, i1, op=ALU.mult),
                r=["expt", "rc_s"], w=[dstname])

        softmax(0, pT[:, :], "pT")
        vop(lambda e: e.tensor_tensor(m_s[:, :], pT[:, :], bank[2][0:128, 0:256], op=ALU.mult),
            r=["pT", "bank2"], w=["m_s"])
        mm(bank[3][0:2, 0:256], ones2col[:, :], m_s[:, :], "bank3", ["ones2col", "m_s"])
        vop(lambda e: e.tensor_copy(s2s[:, :], bank[3][0:2, 0:256]), r=["bank3"], w=["s2s"])
        mm(bank[4][0:128, 0:256], halfsel[:, :], s2s[:, :], "bank4", ["halfsel", "s2s"])
        vop(lambda e: e.tensor_tensor(t1_s[:, :], pT[:, :], bank[4][0:128, 0:256], op=ALU.mult),
            r=["pT", "bank4"], w=["t1_s"])
        vop(lambda e: e.tensor_tensor(hh_s[:, :], m_s[:, :], t1_s[:, :], op=ALU.subtract),
            r=["m_s", "t1_s"], w=["hh_s"])
        lrin_b = bass.AP(bc_c[bsl], 0, [[16, 128], [1, 4], [0, CH]])
        lrout_b = bass.AP(bc_c[bsl], 4, [[16, 128], [1, 4], [0, CH]])
        vop(lambda e, i1=lrin_b: e.tensor_tensor(
                hh_s[:, :].rearrange("p (c n) -> p c n", n=CH),
                hh_s[:, :].rearrange("p (c n) -> p c n", n=CH), i1, op=ALU.mult),
            r=["hh_s", bcn], w=["hh_s"])
        vop(lambda e, i1=lrout_b: e.tensor_tensor(
                ph_s[:, :].rearrange("p (c n) -> p c n", n=CH),
                pT[:, :].rearrange("p (c n) -> p c n", n=CH), i1, op=ALU.mult),
            r=["pT", bcn], w=["ph_s"])
        for j in range(2):
            transpose_q(3, j, ph_s[:, 128 * j:128 * j + 128], ["ph_s"])
        vop(lambda e: e.tensor_copy(ptok[:, :, :],
                                    bank[3][0:128, 0:256].rearrange("p (a b) -> p a b", b=128)),
            r=[f"bank3q{j}" for j in range(2)], w=["ptok"])
        for j in range(2):
            transpose_q(4, j, hh_s[:, 128 * j:128 * j + 128], ["hh_s"])
        sop(lambda e: _copy(e, htok[:, :, :],
                            bank[4][0:128, 0:256].rearrange("p (a b) -> p a b", b=128)),
            r=[f"bank4q{j}" for j in range(2)], w=["htok"])
        softmax(1, pqT[:, :], "pqT")
        for k in range(4):
            par, j = k % 2, k // 2
            for hh in range(2):
                ksl = ktok[64 * par:64 * par + 64, j, 64 * hh:64 * hh + 64]
                vsl = vtok[64 * par:64 * par + 64, j, 64 * hh:64 * hh + 64]
                psl = ptok[64 * par:64 * par + 64, j, 64 * hh:64 * hh + 64]
                hsl = htok[64 * par:64 * par + 64, j, 64 * hh:64 * hh + 64]
                mm(bank[5][64 * hh:64 * hh + 64, 64 * k:64 * k + 64], ksl, hsl,
                   "bank5", ["ktok", "htok"], tp=(64 * par, 64 * hh))
                mm(bank[6][64 * hh:64 * hh + 64, 64 * k:64 * k + 64], psl, vsl,
                   "bank6", ["ptok", "vtok"], tp=(64 * par, 64 * hh))
                mm(bank[7][64 * hh:64 * hh + 64, 64 * k:64 * k + 64], vsl, psl,
                   "bank7", ["vtok", "ptok"], tp=(64 * par, 64 * hh))
        for k in range(4):
            for hh in range(2):
                mm(bank[1][64 * hh:64 * hh + 64, 64 * k:64 * k + 64],
                   wout[64 * hh:64 * hh + 64, k, :],
                   pqT[64 * hh:64 * hh + 64, 64 * k:64 * k + 64], "bank1",
                   ["wout", "pqT"], tp=(64 * hh, 64 * hh))
        vop(lambda e, o=ostg[:, :, slice(CH * sub, CH * (sub + 1))],
            i_=bank[1][0:128, 0:256].rearrange("p (a b) -> p a b", b=CH):
                e.tensor_copy(o, i_), r=["bank1"], w=["ostg"])
        wdin_b = bass.AP(bc_c[bsl], 8, [[16, 128], [1, 4], [0, 64]])
        wdout_b = bass.AP(bc_c[bsl], 12, [[16, 128], [1, 4], [0, 64]])
        for st_t, st_n, bkidx, wd_ap in [(winT, "winT", 5, wdin_b),
                                         (wout, "wout", 6, wdout_b),
                                         (woutT, "woutT", 7, wdout_b)]:
            gop(lambda e, o=st_t[:, :, :], i1=wd_ap:
                    e.tensor_tensor(o, o, i1, op=ALU.mult),
                r=[st_n, bcn], w=[st_n])
            vop(lambda e, o=st_t[:, :, :].rearrange("p a b -> p (a b)"), bk=bank[bkidx]:
                    e.tensor_tensor(o, bk[0:128, 0:256], o, op=ALU.add),
                r=[f"bank{bkidx}", st_n], w=[st_n])
        pump(8)
        if sub == CPG - 1:
            p.op("gpsimd", lambda e, o=oscr[:, :, G * g:G * (g + 1)], i=ostg[:, :, :]:
                    e.dma_start(out=o, in_=i),
                 reads=[bf("ostg")], writes=[bf("oscr")], dma=True, sem="g_oscr")

    pending = []

    def pump(n):
        done = 0
        while pending and done < n:
            try:
                next(pending[0])
                done += 1
            except StopIteration:
                pending.pop(0)

    for g in range(min(2, n_groups)):
        for _ in proj_group_gen(g):
            pass
    for c in range(n_chunks):
        g = c // CPG
        if c % CPG == 0:
            while pending:
                pump(100)
        scan_chunk(c, pump)
        if c % CPG == CPG - 1 and g + 2 < n_groups:
            pending.append(proj_group_gen(g + 2))
    while pending:
        pump(100)

    tapspec = {
        "kT0": (kT[0][:, :, :], "kT0"), "vT0": (vT[0][:, :, :], "vT0"),
        "qT0": (qT[0][:, :, :], "qT0"), "lrwd_c": (lrwd_c[:, :], "lrwd_c"),
        "flat2": (flat2[:, :], "flat2"), "winT": (winT[:, :, :], "winT"),
        "wout": (wout[:, :, :], "wout"), "woutT": (woutT[:, :, :], "woutT"),
        "ostg": (ostg[:, :, :], "ostg"), "pqT": (pqT[:, :], "pqT"),
        "pT": (pT[:, :], "pT"), "bc_c0": (bc_c[0][:, :], "bc_c0"),
        "xT": (xT[:, :, :], "xT"),
        "dbg1": (dbg1[:, :], "dbg1"), "dbg2": (dbg2[:, :], "dbg2"),
        "dbg3": (dbg3[:, :], "dbg3"),
    }
    tap_outs = {}
    if "oscr" in taps:
        td = nc.dram_tensor("tap_oscr", [128, 8, L], F32, kind="ExternalOutput").ap()
        tap_outs["oscr"] = (128, 8, L)
        p.op("gpsimd", lambda e, o=td, i=oscr[:, :, :]: e.dma_start(out=o, in_=i),
             reads=[bf("oscr")], dma=True, sem="g_out")
    for name in taps:
        if name not in tapspec:
            continue
        ap, bufname = tapspec[name]
        shp = list(ap.shape)
        td = nc.dram_tensor("tap_" + name, shp, F32, kind="ExternalOutput").ap()
        tap_outs[name] = tuple(shp)
        p.op("gpsimd", lambda e, o=td, i=ap: e.dma_start(out=o, in_=i),
             reads=[bf(bufname)], dma=True, sem="g_out")

    if do_post:
        p.op("gpsimd",
             lambda e: e.collective_compute(
                 "AllGather", ALU.bypass,
                 replica_groups=[[0, 1], [2, 3], [4, 5], [6, 7]],
                 ins=[oscr[:, :, :]], outs=[oex[:, :, :, :]]),
             reads=[bf("oscr")], writes=[bf("oex")])

    if do_post:
        last_by_sem = {}
        for o in p.ops:
            last_by_sem[o.sem] = o
        p.barrier = list(last_by_sem.values())
        esA.close()

    if do_post:
        import contextlib as _ctx
        esB1 = _ctx.ExitStack()
        sb = lambda name, shape: es.enter_context(nc.sbuf_tensor(name, list(shape), F32))
        sb1 = lambda name, shape: esB1.enter_context(nc.sbuf_tensor(name, list(shape), F32))
        x1T = sb("x1T", (128, 8, TOWN))
        acc = sb("acc", (128, NT, DIM))
        comb = sb("comb", (128, NT, 8))
        combT = sb("combT", (8, NT, 128))
        r_sP = [sb(f"r_s{i}", (128, DIM)) for i in range(2)]
        s1aP = [sb(f"s1a{i}", (128, 1)) for i in range(2)]
        s1bP = [sb(f"s1b{i}", (128, 1)) for i in range(2)]
        s2aP = [sb(f"s2a{i}", (128, 1)) for i in range(2)]
        s2bP = [sb(f"s2b{i}", (128, 1)) for i in range(2)]
        mvaP = [sb(f"mva{i}", (128, 1)) for i in range(2)]
        rstP = [sb(f"rst{i}", (128, 1)) for i in range(2)]
        g8P = [sb(f"g8{i}", (128, 8)) for i in range(2)]
        gscP = [sb(f"gsc{i}", (128, 8)) for i in range(2)]
        gm1P = [sb(f"gm1{i}", (128, 1)) for i in range(2)]
        ge1P = [sb(f"ge1{i}", (128, 8)) for i in range(2)]
        gs1P = [sb(f"gs1{i}", (128, 8)) for i in range(2)]
        gf1P = [sb(f"gf1{i}", (128, 8)) for i in range(2)]
        gp2P = [sb(f"gp2{i}", (128, 8)) for i in range(2)]
        gmaskP = [sb(f"gmask{i}", (128, 8)) for i in range(2)]
        outtP = [sb(f"outt{i}", (128, DIM)) for i in range(2)]
        gw_s = sb("gw_s", (128, 8, 8))
        woT_s = sb1("woT_s", (128, 8, DIM))
        otA = [sb1(f"otA{i}", (128, 8, 128)) for i in range(2)]
        otB = [sb1("otB0", (128, 8, 128))] * 2
        xres = [sb1("xres0", (128, DIM))] * 2

        dma_in("woT_s", woT_s[:], woT_d.rearrange("(k p) f -> p k f", p=128))
        dma_in("gw_s", gw_s[:], gw_d.rearrange("(k p) e -> p k e", p=128))


        def post_tile(t):
            slot = t % 2
            r_s, s1a, s1b = r_sP[slot], s1aP[slot], s1bP[slot]
            s2a, s2b, mva, rst = s2aP[slot], s2bP[slot], mvaP[slot], rstP[slot]
            g8, gsc, gm1, ge1 = g8P[slot], gscP[slot], gm1P[slot], ge1P[slot]
            gs1, gf1, gp2, gmask = gs1P[slot], gf1P[slot], gp2P[slot], gmaskP[slot]
            S = str(slot)
            for r in range(2):
                dma_in(f"otA{slot}", otA[slot][:, 4 * r:4 * r + 4, :],
                       oex[r, :, :, 128 * t:128 * t + 128],
                       extra_reads=[bf("oex")], sem=f"s_otA{slot}")
                dma_in("otB0", otB[slot][:, 4 * r:4 * r + 4, :],
                       oex[r, :, :, TOWN + 128 * t:TOWN + 128 * t + 128],
                       extra_reads=[bf("oex")], sem="s_otB0")
            dma_in("xres0", xres[slot][:], xown_d[128 * t:128 * t + 128, :],
                   sem="s_xres0")
            ots_flat = otA[slot][:, :, :].rearrange("p a b -> p (a b)")
            vop(lambda e, o=ots_flat, sc=msel[:, 1:2]:
                    e.tensor_scalar(o, o, sc, None, op0=ALU.mult),
                r=[f"otA{slot}", "msel"], w=[f"otA{slot}"])
            vop(lambda e, o=ots_flat, i0=otB[slot][:, :, :].rearrange("p a b -> p (a b)"),
                sc=msel[:, 0:1]:
                    e.scalar_tensor_tensor(o, i0, sc, o, op0=ALU.mult, op1=ALU.add),
                r=["otB0", "msel", f"otA{slot}"], w=[f"otA{slot}"])
            for fh in range(2):
                for oc in range(8):
                    mm(bank[fh][0:128, 0:512], otA[slot][:, oc, :],
                       woT_s[:, oc, 512 * fh:512 * fh + 512], f"bank{fh}",
                       [f"otA{slot}", "woT_s"], start=(oc == 0), stop=(oc == 7))

            for fh in range(2):
                vop(lambda e, o=r_s[:, 512 * fh:512 * fh + 512], i0=bank[fh][0:128, 0:512],
                    i1=xres[slot][:, 512 * fh:512 * fh + 512]:
                        e.tensor_tensor(o, i0, i1, op=ALU.add),
                    r=[f"bank{fh}", "xres0"], w=["r_s" + S])
            vop(lambda e: e.tensor_reduce(s1a[:, :], r_s[:, :], axis=AX.X, op=ALU.add),
                r=["r_s" + S], w=["s1a" + S], hz=True)
            vop(lambda e: e.tensor_scalar(mva[:, :], s1a[:, :], -1.0 / DIM, None, op0=ALU.mult),
                r=["s1a" + S], w=["mva" + S])
            for fh in range(2):
                nm = ("s2a" if fh == 0 else "s2b") + S
                a_out = s2a if fh == 0 else s2b
                sop(lambda e, i=r_s[:, 512 * fh:512 * fh + 512], a=a_out:
                        e.activation(bank[6][0:128, 0:512], i, ACT.Square,
                                     bias=mva[:, :], accum_out=a[:, :]),
                    r=["r_s" + S, "mva" + S], w=["bank6", nm], hz=True)
            vop(lambda e: e.tensor_tensor(s2a[:, :], s2a[:, :], s2b[:, :], op=ALU.add),
                r=["s2a" + S, "s2b" + S], w=["s2a" + S])
            sop(lambda e: e.activation(rst[:, :], s2a[:, :], ACT.Sqrt,
                                       bias=float(EPS_LN), scale=1.0 / DIM),
                r=["s2a" + S, "epsv"], w=["rst" + S])
            vop(lambda e: e.reciprocal(rst[:, :], rst[:, :]), r=["rst" + S], w=["rst" + S], hz=True)
            vop(lambda e, o=acc[:, t, :]:
                    e.tensor_scalar(o, r_s[:, :], mva[:, :], rst[:, :],
                                    op0=ALU.add, op1=ALU.mult),
                r=["r_s" + S, "mva" + S, "rst" + S], w=["acc"])
            for dk in range(8):
                bkn = f"bank{2 + dk % 2}"
                bk = bank[2 + dk % 2]
                col = 128 * ((dk // 2) % 2)
                transpose(bk[0:128, col:col + 128], acc[:, t, 128 * dk:128 * dk + 128],
                          bkn, ["acc"])
                vop(lambda e, o=x1T[:, dk, 128 * t:128 * t + 128], i_=bk[0:128, col:col + 128]:
                        e.tensor_copy(o, i_), r=[bkn], w=["x1T"])
            for dk in range(8):
                mm(bank[4][0:128, 0:8], x1T[:, dk, 128 * t:128 * t + 128], gw_s[:, dk, :],
                   "bank4", ["x1T", "gw_s"], start=(dk == 0), stop=False)
            mm(bank[4][0:128, 0:8], onesrow[:, :], gb_s[:, :], "bank4",
               ["onesrow", "gb_s"], start=False, stop=True)
            vop(lambda e: e.tensor_scalar(gsc[:, :], bank[4][0:128, 0:8], float(GEPS), None,
                                          op0=ALU.max), r=["bank4"], w=["gsc" + S])
            vop(lambda e: e.reduce_max(gm1[:, :], gsc[:, :], axis=AX.X), r=["gsc" + S], w=["gm1" + S], hz=True)
            vop(lambda e: e.tensor_scalar(gm1[:, :], gm1[:, :], -1.0, None, op0=ALU.mult),
                r=["gm1" + S], w=["gm1" + S])
            sop(lambda e: e.activation(g8[:, :], gsc[:, :], ACT.Exp, bias=gm1[:, :]),
                r=["gsc" + S, "gm1" + S], w=["g8" + S])
            vop(lambda e: e.reduce_sum(gm1[:, :], g8[:, :], axis=AX.X), r=["g8" + S], w=["gm1" + S], hz=True)
            vop(lambda e: e.reciprocal(gm1[:, :], gm1[:, :]), r=["gm1" + S], w=["gm1" + S], hz=True)
            vop(lambda e: e.tensor_scalar(g8[:, :], g8[:, :], gm1[:, :], None, op0=ALU.mult),
                r=["g8" + S, "gm1" + S], w=["g8" + S])
            vop(lambda e: e.reduce_max(gm1[:, :], g8[:, :], axis=AX.X), r=["g8" + S], w=["gm1" + S], hz=True)
            gm1b = bass.AP(gm1, 0, [[1, 128], [1, 1], [0, 8]])
            vop(lambda e, i1=gm1b: e.tensor_tensor(ge1[:, :].unsqueeze(1),
                g8[:, :].unsqueeze(1), i1, op=ALU.is_ge), r=["g8" + S, "gm1" + S], w=["ge1" + S])
            vop(lambda e: e.tensor_tensor(gs1[:, :], ge1[:, :], pow2[:, :], op=ALU.mult),
                r=["ge1" + S, "pow2"], w=["gs1" + S])
            vop(lambda e: e.reduce_max(gm1[:, :], gs1[:, :], axis=AX.X), r=["gs1" + S], w=["gm1" + S], hz=True)
            gm1b2 = bass.AP(gm1, 0, [[1, 128], [1, 1], [0, 8]])
            vop(lambda e, i1=gm1b2: e.tensor_tensor(gf1[:, :].unsqueeze(1),
                gs1[:, :].unsqueeze(1), i1, op=ALU.is_ge), r=["gs1" + S, "gm1" + S], w=["gf1" + S])
            vop(lambda e: e.scalar_tensor_tensor(gp2[:, :], gf1[:, :], -2.0, g8[:, :],
                                                 op0=ALU.mult, op1=ALU.add),
                r=["gf1" + S, "g8" + S], w=["gp2" + S])
            vop(lambda e: e.reduce_max(gm1[:, :], gp2[:, :], axis=AX.X), r=["gp2" + S], w=["gm1" + S], hz=True)
            gm1b3 = bass.AP(gm1, 0, [[1, 128], [1, 1], [0, 8]])
            vop(lambda e, i1=gm1b3: e.tensor_tensor(ge1[:, :].unsqueeze(1),
                gp2[:, :].unsqueeze(1), i1, op=ALU.is_ge), r=["gp2" + S, "gm1" + S], w=["ge1" + S])
            vop(lambda e: e.tensor_tensor(gs1[:, :], ge1[:, :], pow2[:, :], op=ALU.mult),
                r=["ge1" + S, "pow2"], w=["gs1" + S])
            vop(lambda e: e.reduce_max(gm1[:, :], gs1[:, :], axis=AX.X), r=["gs1" + S], w=["gm1" + S], hz=True)
            gm1b4 = bass.AP(gm1, 0, [[1, 128], [1, 1], [0, 8]])
            vop(lambda e, i1=gm1b4: e.tensor_tensor(ge1[:, :].unsqueeze(1),
                gs1[:, :].unsqueeze(1), i1, op=ALU.is_ge), r=["gs1" + S, "gm1" + S], w=["ge1" + S])
            vop(lambda e: e.tensor_tensor(gmask[:, :], gf1[:, :], ge1[:, :], op=ALU.add),
                r=["gf1" + S, "ge1" + S], w=["gmask" + S])
            vop(lambda e: e.tensor_scalar(gs1[:, :], g8[:, :], float(WTHR), None, op0=ALU.max),
                r=["g8" + S], w=["gs1" + S])
            vop(lambda e: e.tensor_tensor(gs1[:, :], gs1[:, :], gmask[:, :], op=ALU.mult),
                r=["gs1" + S, "gmask" + S], w=["gs1" + S])
            vop(lambda e: e.reduce_sum(gm1[:, :], gs1[:, :], axis=AX.X), r=["gs1" + S], w=["gm1" + S], hz=True)
            vop(lambda e: e.reciprocal(gm1[:, :], gm1[:, :]), r=["gm1" + S], w=["gm1" + S], hz=True)
            vop(lambda e, o=comb[:, t, :]: e.tensor_scalar(o, gs1[:, :], gm1[:, :], 0.5,
                                                           op0=ALU.mult, op1=ALU.mult),
                r=["gs1" + S, "gm1" + S], w=["comb"])
            transpose(bank[5][0:8, 0:128], comb[:, t, :], "bank5", ["comb"])
            vop(lambda e, o=combT[:, t, :]: e.tensor_copy(o, bank[5][0:8, 0:128]),
                r=["bank5"], w=["combT"])
            for fh in range(2):
                mm(bank[6][0:128, 0:512], combT[:, t, :], eb_s[:, 512 * fh:512 * fh + 512],
                   "bank6", ["combT", "eb_s"])
                vop(lambda e, o=acc[:, t, 512 * fh:512 * fh + 512], i0=bank[6][0:128, 0:512]:
                        e.tensor_tensor(o, i0, o, op=ALU.add),
                    r=["bank6", "acc"], w=["acc"])

        for t in range(NT):
            post_tile(t)

        last_by_sem = {}
        for o in p.ops:
            last_by_sem[o.sem] = o
        p.barrier = list(last_by_sem.values())
        p.barrier_applied = set()
        esB1.close()
        ew_s = [sb(f"ew_s{i}", (128, 8, 512)) for i in range(2)]

        for e_i in range(NEXP):
            for fh in range(2):
                slot = (e_i * 2 + fh) % 2
                dma_in(f"ew_s{slot}", ew_s[slot][:],
                       ew_d[e_i, :, 512 * fh:512 * fh + 512].rearrange("(k p) f -> p k f", p=128),
                       sem=f"s_ew{slot}")
                for t in range(NT):
                    bkn = f"bank{t % 4}"
                    bk = bank[t % 4]
                    for dk in range(8):
                        mm(bk[0:128, 0:512], x1T[:, dk, 128 * t:128 * t + 128],
                           ew_s[slot][:, dk, :], bkn, ["x1T", f"ew_s{slot}"],
                           start=(dk == 0), stop=(dk == 7))
                    vop(lambda e, o=acc[:, t, 512 * fh:512 * fh + 512], i0=bk[0:128, 0:512],
                        sc=comb[:, t, e_i:e_i + 1]:
                            e.scalar_tensor_tensor(o, i0, sc, o, op0=ALU.mult, op1=ALU.add),
                        r=[bkn, "comb", "acc"], w=["acc"])

        for t in range(NT):
            S = str(t % 2)
            r_s, s1a, s1b = r_sP[t % 2], s1aP[t % 2], s1bP[t % 2]
            s2a, s2b, mva, rst = s2aP[t % 2], s2bP[t % 2], mvaP[t % 2], rstP[t % 2]
            outt = outtP[t % 2]
            for fh in range(2):
                nm = ("s1a" if fh == 0 else "s1b") + S
                a_out = s1a if fh == 0 else s1b
                vop(lambda e, i=acc[:, t, 512 * fh:512 * fh + 512], a=a_out:
                        e.tensor_reduce(a[:, :], i, axis=AX.X, op=ALU.add),
                    r=["acc"], w=[nm], hz=True)
            vop(lambda e, a=s1a, b=s1b: e.tensor_tensor(a[:, :], a[:, :], b[:, :], op=ALU.add),
                r=["s1a" + S, "s1b" + S], w=["s1a" + S])
            vop(lambda e, m=mva, a=s1a: e.tensor_scalar(m[:, :], a[:, :], -1.0 / DIM, None,
                                                        op0=ALU.mult),
                r=["s1a" + S], w=["mva" + S])
            for fh in range(2):
                nm = ("s2a" if fh == 0 else "s2b") + S
                a_out = s2a if fh == 0 else s2b
                sop(lambda e, i=acc[:, t, 512 * fh:512 * fh + 512], a=a_out, m=mva:
                        e.activation(bank[6][0:128, 0:512], i, ACT.Square,
                                     bias=m[:, :], accum_out=a[:, :]),
                    r=["acc", "mva" + S], w=["bank6", nm], hz=True)
            vop(lambda e, a=s2a, b=s2b: e.tensor_tensor(a[:, :], a[:, :], b[:, :], op=ALU.add),
                r=["s2a" + S, "s2b" + S], w=["s2a" + S])
            sop(lambda e, rr=rst, a=s2a: e.activation(rr[:, :], a[:, :], ACT.Sqrt,
                                                      bias=float(EPS_LN), scale=1.0 / DIM),
                r=["s2a" + S, "epsv"], w=["rst" + S])
            vop(lambda e, rr=rst: e.reciprocal(rr[:, :], rr[:, :]),
                r=["rst" + S], w=["rst" + S], hz=True)
            vop(lambda e, o=outt[:, :], i=acc[:, t, :], m=mva, rr=rst:
                    e.tensor_scalar(o, i, m[:, :], rr[:, :], op0=ALU.add, op1=ALU.mult),
                r=["acc", "mva" + S, "rst" + S], w=["outt" + S])
            p.op("gpsimd", lambda e, o=out_d[128 * t:128 * t + 128, :], i=outt[:, :]:
                    e.dma_start(out=o, in_=i),
                 reads=[bf("outt" + S)], writes=[bf("out_d")], dma=True, sem="g_out")

    if do_post:
        tapspecB = {"acc": (acc[:, :, :], "acc"), "comb": (comb[:, :, :], "comb"),
                    "x1T": (x1T[:, :, :], "x1T")}
        for name in taps:
            if name not in tapspecB:
                continue
            ap, bufname = tapspecB[name]
            shp = list(ap.shape)
            td = nc.dram_tensor("tap_" + name, shp, F32, kind="ExternalOutput").ap()
            tap_outs[name] = tuple(shp)
            p.op("gpsimd", lambda e, o=td, i=ap: e.dma_start(out=o, in_=i),
                 reads=[bf(bufname)], dma=True, sem="g_out")

    final_counts = p.plan()
    need = p.needed_posts(final_counts)
    sems = {s: es.enter_context(nc.semaphore("sem_" + s)) for s in final_counts}

    with nc.Block() as block:
        @block.sync
        def _(eng):
            p.emit("sync", eng, sems, need)
            for s2 in sorted(final_counts):
                if s2 != "sync" and final_counts[s2]:
                    eng.wait_ge(sems[s2], final_counts[s2])

        @block.tensor
        def _(eng):
            p.emit("tensor", eng, sems, need)

        @block.vector
        def _(eng):
            p.emit("vector", eng, sems, need)

        @block.scalar
        def _(eng):
            p.emit("scalar", eng, sems, need)

        @block.gpsimd
        def _(eng):
            p.emit("gpsimd", eng, sems, need)

    if not do_post:
        esA.close()
    es.close()
    return nc, tap_outs


def make_in_maps(inputs):
    shared = host_prep_shared(inputs)
    halves = [host_prep_half(inputs, 0), host_prep_half(inputs, 1)]
    x = np.ascontiguousarray(np.asarray(inputs["L0_attn_x"], dtype=np.float32))
    in_maps = []
    for c in range(8):
        b, half = c // 2, c % 2
        ms = np.zeros((128, 2), np.float32)
        ms[:, 0] = float(half)
        ms[:, 1] = 1.0 - float(half)
        m = {"x": np.ascontiguousarray(x[b]),
             "x_own": np.ascontiguousarray(x[b, TOWN * half:TOWN * (half + 1), :]),
             "mselw": ms}
        m.update(shared)
        m.update(halves[half])
        in_maps.append(m)
    return in_maps


def kernel(**inputs):
    in_maps = make_in_maps(inputs)
    nc, _ = build_core()
    mybir.codegen_inst_isa_subclasses(nc)
    res = run_bass_kernel_spmd(nc, in_maps, core_ids=list(range(8)))
    out = np.empty((B, L, DIM), np.float32)
    for c in range(8):
        b, half = c // 2, c % 2
        out[b, TOWN * half:TOWN * (half + 1), :] = np.asarray(res.results[c]["out"])
    return out
